# version 04
# baseline (speedup 1.0000x reference)
import numpy as np
import ml_dtypes
import concourse.bass as bass
import concourse.mybir as mybir
from concourse.bass_utils import run_bass_kernel_spmd
from contextlib import ExitStack

F32 = mybir.dt.float32
BF16 = mybir.dt.bfloat16
I32 = mybir.dt.int32
AF = mybir.ActivationFunctionType
ALU = mybir.AluOpType
AX = mybir.AxisListType

MAGIC = 12582912.0
T = 2048
D = 2048
NDT = 16
HID1 = 4096
HID2 = 5632
NOC1 = HID1 // 128
NOC2 = HID2 // 128


def _sylvester(n):
    h = np.array([[1.0]])
    while h.shape[0] < n:
        h = np.block([[h, h], [h, -h]])
    return h


class B:

    def __init__(self, upto="all"):
        self.upto = upto
        self.uid = 0
        nc = bass.Bass("TRN2", target_bir_lowering=False)
        self.nc = nc
        self.ctx = ExitStack()

        dram = nc.dram_tensor
        self.xT = dram("xT", [D, T], F32, kind="ExternalInput")
        self.Wq = dram("Wq", [D, D], F32, kind="ExternalInput")
        self.Wk = dram("Wk", [D, D], F32, kind="ExternalInput")
        self.Wv = dram("Wv", [D, D], F32, kind="ExternalInput")
        self.Wo = dram("Wo", [D, D], F32, kind="ExternalInput")
        self.bo = dram("bo", [128, 16], F32, kind="ExternalInput")
        self.wup = dram("wup", [NOC1, D, 128], F32, kind="ExternalInput")
        self.wdn = dram("wdn", [16, HID1, 128], F32, kind="ExternalInput")
        self.wqkv = dram("wqkv", [D, 3 * D], F32, kind="ExternalInput")
        self.wout = dram("wout", [D, D], F32, kind="ExternalInput")
        self.wg = dram("wg", [NOC2, D, 128], I32, kind="ExternalInput")
        self.wu = dram("wu", [NOC2, D, 128], I32, kind="ExternalInput")
        self.wd = dram("wd", [16, HID2, 128], I32, kind="ExternalInput")
        self.sg = dram("sg", [128, NOC2], F32, kind="ExternalInput")
        self.su = dram("su", [128, NOC2], F32, kind="ExternalInput")
        self.sd = dram("sd", [128, 16], F32, kind="ExternalInput")
        self.ones_h = dram("ones_h", [128, 128], BF16, kind="ExternalInput")
        self.gsel_h = dram("gsel_h", [128, 128], BF16, kind="ExternalInput")
        self.ident_h = dram("ident_h", [128, 128], BF16, kind="ExternalInput")
        self.h128_h = dram("h128_h", [128, 128], BF16, kind="ExternalInput")
        self.rotC_h = dram("rotC_h", [32, T], F32, kind="ExternalInput")
        self.rotS_h = dram("rotS_h", [32, T], F32, kind="ExternalInput")
        self.rotCl_h = dram("rotCl_h", [32, T], F32, kind="ExternalInput")
        self.rotSl_h = dram("rotSl_h", [32, T], F32, kind="ExternalInput")
        self.misc_h = dram("misc_h", [128, 4], F32, kind="ExternalInput")
        self.mask_h = dram("mask_h", [128, 4 * 512], BF16, kind="ExternalInput")

        self.outT = dram("outT", [D, T], F32, kind="ExternalOutput")

        self.wup_q = dram("wup_q", [NOC1, D, 128], BF16)
        self.wdn_q = dram("wdn_q", [16, HID1, 128], BF16)
        self.k_scr = dram("k_scr", [16, 128, T], BF16)
        self.v_scr = dram("v_scr", [16, 128, D], BF16)
        self.x_scr = dram("x_scr", [16, 128, T], BF16)

        sb = lambda *a: self.ctx.enter_context(nc.sbuf_tensor(*a))
        self.xs = sb("xs", [128, NDT * T], BF16)
        self.kq = None
        self.ones = sb("ones", [128, 128], BF16)
        self.gsel = sb("gsel", [128, 128], BF16)
        self.ident = sb("ident", [128, 128], BF16)
        self.h128 = sb("h128", [128, 128], BF16)
        self.rotC = None
        self.rotS = None
        self.mask4 = None
        self.bo_sb = sb("bo_sb", [128, 16], F32)
        self.sg_sb = sb("sg_sb", [128, NOC2], F32)
        self.su_sb = sb("su_sb", [128, NOC2], F32)
        self.sd_sb = sb("sd_sb", [128, 16], F32)
        self.kvbd = sb("kvbd", [128, 16 * 128], BF16)
        self.ks_sb = sb("ks_sb", [128, 16], F32)
        self.gup = sb("gup", [128, 1], F32)
        self.mg = sb("mg", [128, 1], F32)
        self.misc = sb("misc", [128, 4], F32)
        self.qscl = sb("qscl", [128, 1], F32)
        self.mB = sb("mB", [128, 512], F32)
        self.rB = sb("rB", [128, 512], F32)

        self.sem = {}
        self.cnt = {}
        for name in ["qs", "qg", "qc", "da", "db", "dc", "dd", "sa", "sb",
                     "pe", "ve", "se", "pe2", "ve2", "se2", "vs"]:
            self.sem[name] = self.ctx.enter_context(nc.semaphore(name))
            self.cnt[name] = 0

    def name(self, prefix):
        self.uid += 1
        return f"{prefix}_{self.uid}"

    def dma(self, eng, dst, src, q):
        eng.dma_start(dst, src).then_inc(self.sem[q], 16)
        self.cnt[q] += 16
        return self.cnt[q]

    def tgt(self, name):
        return self.cnt[name]

    def vsync(self, e):
        self.cnt["vs"] += 1
        e.engine_nop().then_inc(self.sem["vs"], 1)
        e.wait_ge(self.sem["vs"], self.cnt["vs"])

    def xt(self, dt, lo, n):
        return self.xs[:, dt * T + lo: dt * T + lo + n]

    def kqt(self, it, lo, n):
        return self.kq[:, it * T + lo: it * T + lo + n]

    def prologue(self):
        nc, s = self.nc, self
        with nc.Block("prologue") as blk:
            @blk.sync
            def _(e):
                for dst, src in [
                    (s.ones, s.ones_h), (s.gsel, s.gsel_h), (s.ident, s.ident_h),
                    (s.h128, s.h128_h), (s.bo_sb, s.bo), (s.sg_sb, s.sg),
                    (s.su_sb, s.su), (s.sd_sb, s.sd), (s.misc, s.misc_h),
                ]:
                    s.dma(e, dst[:, :], src[:, :], "qs")

            @blk.gpsimd
            def _(e):
                xv = s.xT.rearrange("(dt p) t -> p dt t", p=128)
                xd = s.xs[:, :].rearrange("p (dt t) -> p dt t", dt=NDT)
                s.dma(e, xd, xv, "qg")
                e.memset(s.kvbd[:, :], 0)
                e.memset(s.mg[:, :], 512.0)

            @blk.vector
            def _(e):
                e.wait_ge(s.sem["qs"], s.tgt("qs"))
                e.wait_ge(s.sem["qg"], s.tgt("qg"))
                e.tensor_tensor(s.qscl[:, :], s.misc[:, 0:1], s.misc[:, 1:2],
                                op=ALU.add)

    def layernorm_chunk(self, c0, final=False, out_stage=None):
        nc, s = self.nc, self
        with (
            nc.sbuf_tensor(s.name("ln_sq"), [128, 1024], BF16) as sq,
            nc.psum_tensor(s.name("ln_ps"), [128, 1024], F32) as ps,
        ):
            psum_s = ps[:, 0:512]
            psum_q = ps[:, 512:1024]
            with nc.Block(s.name("ln1")) as blk:
                ve0 = s.cnt["ve"]
                pe0 = s.cnt["pe"]

                @blk.vector
                def _(e):
                    for dt in range(NDT):
                        if dt > 1:
                            e.wait_ge(s.sem["pe"], pe0 + dt - 1)
                        xt = s.xt(dt, c0, 512)
                        e.tensor_mul(sq[:, (dt % 2) * 512:(dt % 2 + 1) * 512],
                                     xt, xt).then_inc(s.sem["ve"], 1)
                        s.cnt["ve"] += 1

                @blk.tensor
                def _(e):
                    for dt in range(NDT):
                        xt = s.xt(dt, c0, 512)
                        e.wait_ge(s.sem["ve"], ve0 + dt + 1)
                        e.matmul(psum_s, s.ones[:, :], xt,
                                 start=(dt == 0), stop=(dt == NDT - 1),
                                 skip_group_check=True)
                        e.matmul(psum_q, s.ones[:, :],
                                 sq[:, (dt % 2) * 512:(dt % 2 + 1) * 512],
                                 start=(dt == 0), stop=(dt == NDT - 1),
                                 skip_group_check=True).then_inc(s.sem["pe"], 1)
                        s.cnt["pe"] += 1

            with nc.Block(s.name("ln2")) as blk:
                @blk.vector
                def _(e):
                    e.tensor_scalar(s.mB[:, :], psum_s, 1.0 / D, None, op0=ALU.mult)
                    e.tensor_scalar(s.rB[:, :], psum_q, 1.0 / D, None, op0=ALU.mult)
                    s.vsync(e)
                    e.tensor_tensor(sq[:, 0:512], s.mB[:, :], s.mB[:, :],
                                    op=ALU.mult)
                    s.vsync(e)
                    e.tensor_tensor(s.rB[:, :], s.rB[:, :], sq[:, 0:512],
                                    op=ALU.subtract)
                    s.vsync(e)
                    e.tensor_scalar(s.rB[:, :], s.rB[:, :], 1e-5, None,
                                    op0=ALU.add).then_inc(s.sem["ve"], 1)
                    s.cnt["ve"] += 1

                @blk.scalar
                def _(e):
                    e.wait_ge(s.sem["ve"], s.tgt("ve"))
                    e.activation(s.rB[:, :], s.rB[:, :], AF.Sqrt).then_inc(
                        s.sem["se"], 1)
                    s.cnt["se"] += 1

            with nc.Block(s.name("ln3")) as blk:
                @blk.vector
                def _(e):
                    e.wait_ge(s.sem["se"], s.tgt("se"))
                    e.reciprocal(s.rB[:, :], s.rB[:, :])
                    s.vsync(e)
                    for dt in range(NDT):
                        xt = s.xt(dt, c0, 512)
                        sqp = sq[:, (dt % 2) * 512:(dt % 2 + 1) * 512]
                        e.tensor_sub(sqp, xt, s.mB[:, :])
                        s.vsync(e)
                        if final:
                            e.tensor_mul(out_stage[:, dt * 512:(dt + 1) * 512],
                                         sqp, s.rB[:, :])
                        else:
                            e.tensor_mul(xt, sqp, s.rB[:, :])

    def l0_attn(self):
        nc, s = self.nc, self
        with (
            nc.sbuf_tensor("wk_s", [128, 2 * 16 * 512], BF16) as wk_s,
            nc.sbuf_tensor("wv_s", [128, 2 * 16 * 512], BF16) as wv_s,
            nc.sbuf_tensor("k_sb", [128, 512], BF16) as k_sb,
            nc.sbuf_tensor("v_sb", [128, 512], BF16) as v_sb,
            nc.psum_tensor("kp", [128, 512], F32) as kp,
            nc.psum_tensor("vp", [128, 512], F32) as vp,
            nc.psum_tensor("kvp", [128, 512], F32) as kvp,
            nc.psum_tensor("ksp", [128, 16], F32) as ksp,
            nc.Block("l0a_A") as blk,
        ):
            HB = 16 * 512
            pe0 = s.cnt["pe"]
            pe20 = s.cnt["pe2"]
            ve0 = s.cnt["ve"]
            ve20 = s.cnt["ve2"]

            ld_tgt = {}

            @blk.gpsimd
            def _(e):
                for c4 in range(4):
                    if c4 >= 2:
                        e.wait_ge(s.sem["pe"], pe0 + (c4 - 1) * 16)
                    q = ("da", "db")[c4 % 2]
                    for (w_h, w_s) in ((s.Wk, wk_s), (s.Wv, wv_s)):
                        src = w_h[:, c4 * 512:(c4 + 1) * 512] \
                            .rearrange("(dt p) i -> p dt i", p=128)
                        dst = w_s[:, (c4 % 2) * HB:(c4 % 2 + 1) * HB] \
                            .rearrange("p (dt i) -> p dt i", dt=NDT)
                        s.dma(e, dst, src, q)
                    ld_tgt[c4] = (q, s.cnt[q])

            @blk.tensor
            def _(e):
                for c4 in range(4):
                    e.wait_ge(s.sem[ld_tgt[c4][0]], ld_tgt[c4][1])
                    if c4 > 0:
                        e.wait_ge(s.sem["ve2"], ve20 + c4)
                    wb = (c4 % 2) * HB
                    for tt in range(16):
                        it = c4 * 16 + tt
                        if it > 0:
                            e.wait_ge(s.sem["ve"], ve0 + it)
                        for dt in range(NDT):
                            xt = s.xt(dt, tt * 128, 128)
                            e.matmul(kp[:, :], xt,
                                     wk_s[:, wb + dt * 512: wb + (dt + 1) * 512],
                                     start=(dt == 0), stop=(dt == NDT - 1),
                                     skip_group_check=True)
                            mm = e.matmul(vp[:, :], xt,
                                          wv_s[:, wb + dt * 512: wb + (dt + 1) * 512],
                                          start=(dt == 0), stop=(dt == NDT - 1),
                                          skip_group_check=True)
                        mm.then_inc(s.sem["pe"], 1)
                        s.cnt["pe"] += 1
                        e.wait_ge(s.sem["ve"], ve0 + it + 1)
                        for ps_i in range(4):
                            pair = c4 * 4 + ps_i
                            sl = slice(ps_i * 128, (ps_i + 1) * 128)
                            e.matmul(kvp[:, sl], k_sb[:, sl], v_sb[:, sl],
                                     start=(tt == 0 and ps_i == 0),
                                     stop=(tt == 15),
                                     skip_group_check=True)
                            mm = e.matmul(ksp[:, pair:pair + 1], k_sb[:, sl],
                                          s.ones[:, 0:1],
                                          start=(tt == 0 and ps_i == 0),
                                          stop=(tt == 15),
                                          skip_group_check=True)
                        mm.then_inc(s.sem["pe2"], 1)
                        s.cnt["pe2"] += 1

            @blk.vector
            def _(e):
                for c4 in range(4):
                    for tt in range(16):
                        it = c4 * 16 + tt
                        e.wait_ge(s.sem["pe"], pe0 + it + 1)
                        e.tensor_scalar(k_sb[:, :], kp[:, :], 0.0, None,
                                        op0=ALU.max)
                        e.tensor_copy(v_sb[:, :], vp[:, :]).then_inc(
                            s.sem["ve"], 1)
                        s.cnt["ve"] += 1
                    e.wait_ge(s.sem["pe2"], pe20 + (c4 + 1) * 16)
                    for ps_i in range(4):
                        pair = c4 * 4 + ps_i
                        for half in range(2):
                            h0 = half * 64
                            e.tensor_copy(
                                s.kvbd[h0:h0 + 64,
                                       pair * 128 + h0: pair * 128 + h0 + 64],
                                kvp[h0:h0 + 64,
                                    ps_i * 128 + h0: ps_i * 128 + h0 + 64])
                    e.tensor_copy(s.ks_sb[:, c4 * 4:(c4 + 1) * 4],
                                  ksp[:, c4 * 4:(c4 + 1) * 4]).then_inc(
                        s.sem["ve2"], 1)
                    s.cnt["ve2"] += 1

        if s.kq is None:
            s.kq = s.ctx.enter_context(nc.sbuf_tensor("kq", [128, 16 * T], BF16))
        with (
            nc.sbuf_tensor("wq_s", [128, 2 * 16 * 128], BF16) as wq_s,
            nc.sbuf_tensor("q_sb", [128, 512], BF16) as q_sb,
            nc.sbuf_tensor("tq_sb", [128, 512], BF16) as tq_sb,
            nc.sbuf_tensor("rec_sb", [128, 512], F32) as rec_sb,
            nc.psum_tensor("qp", [128, 512], F32) as qp,
            nc.psum_tensor("npp", [128, 512], F32) as npp,
            nc.psum_tensor("qkp", [128, 512], F32) as qkp,
            nc.Block("l0a_B1") as blk,
        ):
            WB = 16 * 128
            pe0 = s.cnt["pe"]
            ve0 = s.cnt["ve"]
            ld_tgt = {}

            @blk.gpsimd
            def _(e):
                for pair in range(16):
                    if pair >= 2:
                        e.wait_ge(s.sem["pe"], pe0 + (pair - 1) * 12)
                    q = ("da", "db")[pair % 2]
                    src = s.Wq[:, pair * 128:(pair + 1) * 128] \
                        .rearrange("(dt p) i -> p dt i", p=128)
                    dst = wq_s[:, (pair % 2) * WB:(pair % 2 + 1) * WB] \
                        .rearrange("p (dt i) -> p dt i", dt=NDT)
                    s.dma(e, dst, src, q)
                    ld_tgt[pair] = (q, s.cnt[q])

            @blk.tensor
            def _(e):
                e.wait_ge(s.sem["ve2"], s.tgt("ve2"))
                for pair in range(16):
                    e.wait_ge(s.sem[ld_tgt[pair][0]], ld_tgt[pair][1])
                    wb = (pair % 2) * WB
                    for ch in range(4):
                        it = pair * 4 + ch
                        if it > 0:
                            e.wait_ge(s.sem["ve"], ve0 + (it - 1) * 3 + 3)
                        for dt in range(NDT):
                            mm = e.matmul(
                                qp[:, :], wq_s[:, wb + dt * 128: wb + (dt + 1) * 128],
                                s.xt(dt, ch * 512, 512),
                                start=(dt == 0), stop=(dt == NDT - 1))
                        mm.then_inc(s.sem["pe"], 1)
                        s.cnt["pe"] += 1
                        e.wait_ge(s.sem["ve"], ve0 + it * 3 + 1)
                        e.matmul(npp[:, :], s.gsel[:, :], tq_sb[:, :],
                                 start=True, stop=True).then_inc(s.sem["pe"], 1)
                        s.cnt["pe"] += 1
                        e.wait_ge(s.sem["ve"], ve0 + it * 3 + 2)
                        e.matmul(qkp[:, :], s.kvbd[:, pair * 128:(pair + 1) * 128],
                                 q_sb[:, :], start=True, stop=True,
                                 skip_group_check=True).then_inc(s.sem["pe"], 1)
                        s.cnt["pe"] += 1

            @blk.vector
            def _(e):
                for pair in range(16):
                    for ch in range(4):
                        it = pair * 4 + ch
                        e.wait_ge(s.sem["pe"], pe0 + it * 3 + 1)
                        e.tensor_scalar(q_sb[:, :], qp[:, :], 0.0, None,
                                        op0=ALU.max)
                        s.vsync(e)
                        e.tensor_scalar(tq_sb[:, :], q_sb[:, :],
                                        s.ks_sb[:, pair:pair + 1], None,
                                        op0=ALU.mult).then_inc(s.sem["ve"], 1)
                        s.cnt["ve"] += 1
                        e.wait_ge(s.sem["pe"], pe0 + it * 3 + 2)
                        e.tensor_scalar(rec_sb[:, :], npp[:, :], 1e-6, None,
                                        op0=ALU.max)
                        s.vsync(e)
                        e.reciprocal(rec_sb[:, :], rec_sb[:, :]).then_inc(
                            s.sem["ve"], 1)
                        s.cnt["ve"] += 1
                        e.wait_ge(s.sem["pe"], pe0 + it * 3 + 3)
                        e.tensor_tensor(
                            s.kqt(pair, ch * 512, 512),
                            qkp[:, :], rec_sb[:, :], op=ALU.mult).then_inc(
                            s.sem["ve"], 1)
                        s.cnt["ve"] += 1

        with (
            nc.sbuf_tensor("wo_s", [128, 2 * 16 * 128], BF16) as wo_s,
            nc.psum_tensor("op_ps", [128, 1024], F32) as op_ps,
            nc.Block("l0a_B2") as blk,
        ):
            WB = 16 * 128
            pe0 = s.cnt["pe"]
            ve0 = s.cnt["ve"]
            ld_tgt = {}

            @blk.gpsimd
            def _(e):
                for ot in range(16):
                    if ot >= 2:
                        e.wait_ge(s.sem["pe"], pe0 + (ot - 1) * 4)
                    q = ("da", "db")[ot % 2]
                    src = s.Wo[:, ot * 128:(ot + 1) * 128] \
                        .rearrange("(it p) o -> p it o", p=128)
                    dst = wo_s[:, (ot % 2) * WB:(ot % 2 + 1) * WB] \
                        .rearrange("p (it o) -> p it o", it=NDT)
                    s.dma(e, dst, src, q)
                    ld_tgt[ot] = (q, s.cnt[q])

            @blk.tensor
            def _(e):
                for ot in range(16):
                    e.wait_ge(s.sem[ld_tgt[ot][0]], ld_tgt[ot][1])
                    wb = (ot % 2) * WB
                    for ch in range(4):
                        it = ot * 4 + ch
                        pslice = op_ps[:, (it % 2) * 512:(it % 2 + 1) * 512]
                        if it >= 2:
                            e.wait_ge(s.sem["ve"], ve0 + it - 1)
                        for itile in range(16):
                            mm = e.matmul(
                                pslice, wo_s[:, wb + itile * 128: wb + (itile + 1) * 128],
                                s.kqt(itile, ch * 512, 512),
                                start=(itile == 0), stop=(itile == NDT - 1),
                                skip_group_check=True)
                        mm.then_inc(s.sem["pe"], 1)
                        s.cnt["pe"] += 1

            @blk.vector
            def _(e):
                for ot in range(16):
                    for ch in range(4):
                        it = ot * 4 + ch
                        pslice = op_ps[:, (it % 2) * 512:(it % 2 + 1) * 512]
                        e.wait_ge(s.sem["pe"], pe0 + it + 1)
                        xt = s.xt(ot, ch * 512, 512)
                        e.scalar_tensor_tensor(
                            xt, pslice, s.bo_sb[:, ot:ot + 1], xt,
                            op0=ALU.add, op1=ALU.add).then_inc(s.sem["ve"], 1)
                        s.cnt["ve"] += 1

        for ch in range(4):
            self.layernorm_chunk(ch * 512)

    def l0_ffn_wquant(self):
        nc, s = self.nc, self
        s.wu_m = s.ctx.enter_context(nc.sbuf_tensor("wu_m", [128, 1], F32))
        s.wd_m = s.ctx.enter_context(nc.sbuf_tensor("wd_m", [128, 1], F32))
        s.wu_r = s.ctx.enter_context(nc.sbuf_tensor("wu_r", [128, 1], F32))
        s.wd_r = s.ctx.enter_context(nc.sbuf_tensor("wd_r", [128, 1], F32))
        specs = [
            (s.wup, s.wup_q, NOC1, NDT * 128, s.wu_m, s.wu_r),
            (s.wdn, s.wdn_q, 16, NOC1 * 128, s.wd_m, s.wd_r),
        ]
        with (
            nc.sbuf_tensor("wqb", [128, 2 * 4096], F32) as wqb,
            nc.sbuf_tensor("wqt", [128, 2 * 4096], BF16) as wqt,
            nc.sbuf_tensor("wqi", [128, 4096], I32) as wqi,
            nc.sbuf_tensor("wqa", [128, 48], F32) as wqa,
            nc.sbuf_tensor("wqa_bf", [128, 1], BF16) as wqa_bf,
            nc.psum_tensor("wq_ps", [128, 1], F32) as wq_ps,
        ):
            with nc.Block("wq1") as blk:
                ve0 = s.cnt["ve"]
                ld_tgt = {}

                @blk.gpsimd
                def _(e):
                    i = 0
                    for (wh, _sc, n, epp, _m, _r) in specs:
                        for ch in range(n):
                            if i >= 2:
                                e.wait_ge(s.sem["ve"], ve0 + i - 1)
                            q = ("da", "db")[i % 2]
                            src = wh[ch].rearrange("(a p) o -> p a o", p=128)
                            dst = wqb[:, (i % 2) * 4096:(i % 2) * 4096 + epp] \
                                .rearrange("p (a o) -> p a o", o=128)
                            s.dma(e, dst, src, q)
                            ld_tgt[i] = (q, s.cnt[q])
                            i += 1

                @blk.vector
                def _(e):
                    i = 0
                    for (wh, _sc, n, epp, _m, _r) in specs:
                        for ch in range(n):
                            e.wait_ge(s.sem[ld_tgt[i][0]], ld_tgt[i][1])
                            e.reduce_sum(wqa[:, i:i + 1],
                                         wqb[:, (i % 2) * 4096:(i % 2) * 4096 + epp],
                                         axis=AX.X,
                                         apply_absolute_value=True).then_inc(
                                s.sem["ve"], 1)
                            s.cnt["ve"] += 1
                            i += 1

            for wi, (wh, _sc, n, epp, m_sb, r_sb) in enumerate(specs):
                lo = 0 if wi == 0 else NOC1
                cnt_el = float(n * epp * 128)
                with nc.Block(s.name("wqs")) as blk:
                    @blk.vector
                    def _(e):
                        e.reduce_sum(s.gup[:, :], wqa[:, lo:lo + n], axis=AX.X)
                        s.vsync(e)
                        e.tensor_copy(wqa_bf[:, :], s.gup[:, :]).then_inc(
                            s.sem["ve"], 1)
                        s.cnt["ve"] += 1

                    @blk.tensor
                    def _(e):
                        e.wait_ge(s.sem["ve"], s.tgt("ve"))
                        e.matmul(wq_ps[:, :], s.ones[:, :], wqa_bf[:, :],
                                 start=True, stop=True).then_inc(s.sem["pe"], 1)
                        s.cnt["pe"] += 1

                    @blk.vector
                    def _(e2):
                        e2.wait_ge(s.sem["pe"], s.tgt("pe"))
                        e2.tensor_scalar(m_sb[:, :], wq_ps[:, :], 1.0 / cnt_el,
                                         1e-5, op0=ALU.mult, op1=ALU.max)
                        s.vsync(e2)
                        e2.reciprocal(r_sb[:, :], m_sb[:, :])

            with nc.Block("wqg") as blk:
                @blk.vector
                def _(e):
                    e.tensor_mul(s.gup[:, :], s.wu_m[:, :], s.wu_m[:, :])
                    s.vsync(e)
                    e.tensor_mul(s.gup[:, :], s.gup[:, :], s.wd_m[:, :])
                    s.vsync(e)
                    e.tensor_scalar_mul(s.gup[:, :], s.gup[:, :], 1.0 / 131072.0)
                    s.vsync(e)

            with nc.Block("wq2") as blk:
                se0 = s.cnt["se"]
                ve0 = s.cnt["ve"]
                sa0 = s.cnt["sa"]
                sb0 = s.cnt["sb"]
                ld_tgt = {}

                @blk.gpsimd
                def _(e):
                    i = 0
                    for (wh, _sc, n, epp, _m, r_sb) in specs:
                        for ch in range(n):
                            if i >= 2:
                                e.wait_ge(s.sem["ve"], ve0 + i - 1)
                            q = ("da", "db")[i % 2]
                            src = wh[ch].rearrange("(a p) o -> p a o", p=128)
                            dst = wqb[:, (i % 2) * 4096:(i % 2) * 4096 + epp] \
                                .rearrange("p (a o) -> p a o", o=128)
                            s.dma(e, dst, src, q)
                            ld_tgt[i] = (q, s.cnt[q])
                            i += 1

                @blk.scalar
                def _(e):
                    i = 0
                    for (wh, _sc, n, epp, _m, r_sb) in specs:
                        for ch in range(n):
                            e.wait_ge(s.sem[ld_tgt[i][0]], ld_tgt[i][1])
                            if i >= 2:
                                e.wait_ge(s.sem["ve"], ve0 + i - 1)
                            sl = wqb[:, (i % 2) * 4096:(i % 2) * 4096 + epp]
                            e.activation(sl, sl, AF.Identity,
                                         scale=r_sb[:, :]).then_inc(s.sem["se"], 1)
                            s.cnt["se"] += 1
                            i += 1

                @blk.vector
                def _(e):
                    i = 0
                    for (wh, _sc, n, epp, _m, _r) in specs:
                        for ch in range(n):
                            e.wait_ge(s.sem["se"], se0 + i + 1)
                            if i >= 2:
                                par = i % 2
                                base = sa0 if par == 0 else sb0
                                e.wait_ge(s.sem[("sa", "sb")[par]],
                                          base + 16 * ((i - 2) // 2 + 1))
                            fsl = wqb[:, (i % 2) * 4096:(i % 2) * 4096 + epp]
                            tsl = wqt[:, (i % 2) * 4096:(i % 2) * 4096 + epp]
                            risl = wqi[:, 0:epp]
                            e.tensor_scalar(fsl, fsl, -1.0, 1.0,
                                            op0=ALU.max, op1=ALU.min)
                            s.vsync(e)
                            e.tensor_scalar(risl, fsl, 512.0, None, op0=ALU.add)
                            s.vsync(e)
                            e.scalar_tensor_tensor(fsl, risl, -1.0, fsl,
                                                   op0=ALU.mult, op1=ALU.add)
                            s.vsync(e)
                            e.tensor_scalar(fsl, fsl, -511.5, None, op0=ALU.is_ge)
                            s.vsync(e)
                            e.scalar_tensor_tensor(tsl, risl, -512.0, fsl,
                                                   op0=ALU.add, op1=ALU.add) \
                                .then_inc(s.sem["ve"], 1)
                            s.cnt["ve"] += 1
                            i += 1

                @blk.sync
                def _(e):
                    i = 0
                    for (wh, scr, n, epp, _m, _r) in specs:
                        for ch in range(n):
                            e.wait_ge(s.sem["ve"], ve0 + i + 1)
                            dst = scr[ch].rearrange("(a p) o -> p a o", p=128)
                            src = wqt[:, (i % 2) * 4096:(i % 2) * 4096 + epp] \
                                .rearrange("p (a o) -> p a o", o=128)
                            s.dma(e, dst, src, ("sa", "sb")[i % 2])
                            i += 1
                    e.wait_ge(s.sem["sa"], s.tgt("sa"))
                    e.wait_ge(s.sem["sb"], s.tgt("sb"))

    def _ffn_sl(self, i):
        return self.kq[:, i * 512:(i + 1) * 512]

    def _fwht_butterflies(self, slots, spares, nbits):
        nc, s = self.nc, self
        spc = list(spares)
        with nc.Block(s.name("bfly")) as blk:
            @blk.vector
            def _(e):
                n = len(slots)
                for lvl in range(nbits):
                    st = 1 << lvl
                    pairs = [(a, a | st) for a in range(n) if not (a & st)]
                    for gi in range(0, len(pairs), 4):
                        if lvl > 0 or gi > 0:
                            s.vsync(e)
                        freed = []
                        for (a, b) in pairs[gi:gi + 4]:
                            sa, sb_ = slots[a], slots[b]
                            sp1 = spc.pop(0)
                            sp2 = spc.pop(0)
                            e.tensor_add(s._ffn_sl(sp1), s._ffn_sl(sa),
                                         s._ffn_sl(sb_))
                            e.tensor_sub(s._ffn_sl(sp2), s._ffn_sl(sa),
                                         s._ffn_sl(sb_))
                            slots[a], slots[b] = sp1, sp2
                            freed += [sa, sb_]
                        spc.extend(freed)
        return slots, spc

    def _ffn_h128(self, slots, ps, src_of=None):
        nc, s = self.nc, self
        n = len(slots)
        with nc.Block(s.name("h128")) as blk:
            pe0 = s.cnt["pe"]
            ve0 = s.cnt["ve"]

            @blk.tensor
            def _(e):
                for i in range(n):
                    if i >= 2:
                        e.wait_ge(s.sem["ve"], ve0 + i - 1)
                    src = src_of(i) if src_of else s._ffn_sl(slots[i])
                    e.matmul(ps[:, (i % 2) * 512:(i % 2 + 1) * 512],
                             s.h128[:, :], src, start=True, stop=True,
                             skip_group_check=True).then_inc(s.sem["pe"], 1)
                    s.cnt["pe"] += 1

            @blk.vector
            def _(e):
                for i in range(n):
                    e.wait_ge(s.sem["pe"], pe0 + i + 1)
                    e.tensor_copy(s._ffn_sl(slots[i]),
                                  ps[:, (i % 2) * 512:(i % 2 + 1) * 512]) \
                        .then_inc(s.sem["ve"], 1)
                    s.cnt["ve"] += 1

    def _ffn_quant(self, slots, pt, tm_sb, tmpf):
        nc, s = self.nc, self
        n = len(slots)
        nw = (n + 15) // 16
        ng = (n * 128) // 512
        nh = (ng + 3) // 4
        nhalves = (ng + nh - 1) // nh
        with (
            nc.sbuf_tensor(s.name("qam"), [128, 2], F32) as am,
            nc.sbuf_tensor(s.name("qrb"), [128, 2048], I32) as rb,
            nc.sbuf_tensor(s.name("qcb"), [128, 2048], BF16) as cb,
            nc.sbuf_tensor(s.name("qc2"), [128, 2048], BF16) as c2,
            nc.Block(s.name("qnt")) as blk,
        ):
            pe0 = s.cnt["pe"]
            ve0 = s.cnt["ve"]
            se0 = s.cnt["se"]
            ve_sub = lambda sub: ve0 + sub * (2 * nw + 1 + nhalves)
            pe_sub = lambda sub: pe0 + sub * (2 * nw)
            se_sub = lambda sub: se0 + sub * ng

            @blk.scalar
            def _(e):
                for sub in range(4):
                    for hv in range(nhalves):
                        g0, g1 = hv * nh, min(ng, (hv + 1) * nh)
                        for g in range(g0, g1):
                            e.wait_ge(s.sem["ve"], ve_sub(sub) + nw + 1 + hv)
                            sl = tm_sb[:, g * 512:(g + 1) * 512]
                            e.activation(tmpf[:, (g - g0) * 512:(g - g0 + 1) * 512],
                                         sl, AF.Identity, bias=s.mg[:, :],
                                         scale=am[:, 0:1]).then_inc(s.sem["se"], 1)
                            s.cnt["se"] += 1

            @blk.tensor
            def _(e):
                for sub in range(4):
                    for w in range(nw):
                        if sub > 0 or w > 0:
                            e.wait_ge(s.sem["ve"], ve_sub(sub) + w)
                        lo = w * 16
                        hi = min(n, lo + 16)
                        for i2 in range(lo, hi):
                            mm = e.transpose(
                                pt[:, (i2 - lo) * 128:(i2 - lo + 1) * 128],
                                s._ffn_sl(slots[i2])[:, sub * 128:(sub + 1) * 128],
                                s.ident[:, :])
                        mm.then_inc(s.sem["pe"], 1)
                        s.cnt["pe"] += 1
                    for w in range(nw):
                        e.wait_ge(s.sem["ve"], ve_sub(sub) + nw + 1 + nhalves + w)
                        lo = w * 16
                        hi = min(n, lo + 16)
                        for i2 in range(lo, hi):
                            mm = e.transpose(
                                pt[:, (i2 - lo) * 128:(i2 - lo + 1) * 128],
                                tm_sb[:, i2 * 128:(i2 + 1) * 128],
                                s.ident[:, :])
                        mm.then_inc(s.sem["pe"], 1)
                        s.cnt["pe"] += 1

            @blk.vector
            def _(e):
                for sub in range(4):
                    for w in range(nw):
                        e.wait_ge(s.sem["pe"], pe_sub(sub) + w + 1)
                        lo = w * 16
                        hi = min(n, lo + 16)
                        e.tensor_copy(tm_sb[:, lo * 128:hi * 128],
                                      pt[:, 0:(hi - lo) * 128])
                        if w == 0:
                            e.reduce_max(am[:, 0:1], pt[:, 0:(hi - lo) * 128],
                                         axis=AX.X, apply_absolute_value=True)
                        else:
                            e.reduce_max(am[:, 1:2], pt[:, 0:(hi - lo) * 128],
                                         axis=AX.X, apply_absolute_value=True)
                            s.vsync(e)
                            e.tensor_max(am[:, 0:1], am[:, 0:1], am[:, 1:2])
                        e.engine_nop().then_inc(s.sem["ve"], 1)
                        s.cnt["ve"] += 1
                    s.vsync(e)
                    e.tensor_scalar(am[:, 0:1], am[:, 0:1], 1e-5, None,
                                    op0=ALU.max)
                    s.vsync(e)
                    e.tensor_scalar(am[:, 1:2], am[:, 0:1], 1.0 / 127.0, None,
                                    op0=ALU.mult)
                    s.vsync(e)
                    e.reciprocal(am[:, 0:1], am[:, 0:1])
                    s.vsync(e)
                    e.tensor_scalar(am[:, 0:1], am[:, 0:1], 127.0, None,
                                    op0=ALU.mult)
                    s.vsync(e)
                    e.engine_nop().then_inc(s.sem["ve"], 1)
                    s.cnt["ve"] += 1
                    for hv in range(nhalves):
                        g0, g1 = hv * nh, min(ng, (hv + 1) * nh)
                        w512 = (g1 - g0) * 512
                        e.wait_ge(s.sem["se"], se_sub(sub) + g1)
                        e.tensor_copy(rb[:, 0:w512], tmpf[:, 0:w512])
                        s.vsync(e)
                        e.tensor_tensor(tmpf[:, 0:w512], tmpf[:, 0:w512],
                                        rb[:, 0:w512], op=ALU.subtract)
                        s.vsync(e)
                        e.tensor_scalar(cb[:, 0:w512], tmpf[:, 0:w512], 0.5,
                                        None, op0=ALU.is_ge)
                        s.vsync(e)
                        e.tensor_scalar(c2[:, 0:w512], tmpf[:, 0:w512], 1.5,
                                        None, op0=ALU.is_ge)
                        s.vsync(e)
                        e.tensor_tensor(cb[:, 0:w512], cb[:, 0:w512],
                                        c2[:, 0:w512], op=ALU.add)
                        s.vsync(e)
                        e.tensor_scalar(c2[:, 0:w512], tmpf[:, 0:w512], -0.5,
                                        None, op0=ALU.is_lt)
                        s.vsync(e)
                        e.tensor_tensor(cb[:, 0:w512], cb[:, 0:w512],
                                        c2[:, 0:w512], op=ALU.subtract)
                        s.vsync(e)
                        e.tensor_scalar(c2[:, 0:w512], tmpf[:, 0:w512], -1.5,
                                        None, op0=ALU.is_lt)
                        s.vsync(e)
                        e.tensor_tensor(cb[:, 0:w512], cb[:, 0:w512],
                                        c2[:, 0:w512], op=ALU.subtract)
                        s.vsync(e)
                        e.tensor_scalar(c2[:, 0:w512], tmpf[:, 0:w512], 2.5,
                                        None, op0=ALU.is_ge)
                        s.vsync(e)
                        e.tensor_tensor(cb[:, 0:w512], cb[:, 0:w512],
                                        c2[:, 0:w512], op=ALU.add)
                        s.vsync(e)
                        e.tensor_scalar(c2[:, 0:w512], tmpf[:, 0:w512], -2.5,
                                        None, op0=ALU.is_lt)
                        s.vsync(e)
                        e.tensor_tensor(cb[:, 0:w512], cb[:, 0:w512],
                                        c2[:, 0:w512], op=ALU.subtract)
                        s.vsync(e)
                        e.tensor_tensor(tmpf[:, 0:w512], rb[:, 0:w512],
                                        cb[:, 0:w512], op=ALU.add)
                        s.vsync(e)
                        for g in range(g0, g1):
                            sl = tm_sb[:, g * 512:(g + 1) * 512]
                            e.tensor_scalar(sl,
                                            tmpf[:, (g - g0) * 512:(g - g0 + 1) * 512],
                                            512.0, am[:, 1:2],
                                            op0=ALU.subtract, op1=ALU.mult)
                        e.engine_nop().then_inc(s.sem["ve"], 1)
                        s.cnt["ve"] += 1
                    for w in range(nw):
                        e.wait_ge(s.sem["pe"], pe_sub(sub) + nw + w + 1)
                        lo = w * 16
                        hi = min(n, lo + 16)
                        for i2 in range(lo, hi):
                            e.tensor_copy(
                                s._ffn_sl(slots[i2])[:, sub * 128:(sub + 1) * 128],
                                pt[:, (i2 - lo) * 128:(i2 - lo + 1) * 128])
                        e.engine_nop().then_inc(s.sem["ve"], 1)
                        s.cnt["ve"] += 1

    def _dbg_tm(self, slots, pt, tm_sb):
        nc, s = self.nc, self
        n = len(slots)
        nw = (n + 15) // 16
        with (
            nc.sbuf_tensor(s.name("qam"), [128, 2], F32) as am,
            nc.Block(s.name("dbgtm")) as blk,
        ):
            pe0 = s.cnt["pe"]
            ve0 = s.cnt["ve"]

            @blk.tensor
            def _(e):
                for w in range(nw):
                    if w > 0:
                        e.wait_ge(s.sem["ve"], ve0 + w)
                    lo = w * 16
                    hi = min(n, lo + 16)
                    for i in range(lo, hi):
                        mm = e.transpose(
                            pt[:, (i - lo) * 128:(i - lo + 1) * 128],
                            s._ffn_sl(slots[i])[:, 0:128],
                            s.ident[:, :])
                    mm.then_inc(s.sem["pe"], 1)
                    s.cnt["pe"] += 1

            @blk.vector
            def _(e):
                for w in range(nw):
                    e.wait_ge(s.sem["pe"], pe0 + w + 1)
                    lo = w * 16
                    hi = min(n, lo + 16)
                    e.tensor_copy(tm_sb[:, lo * 128:hi * 128],
                                  pt[:, 0:(hi - lo) * 128])
                    if w == 0:
                        e.reduce_max(am[:, 0:1], pt[:, 0:(hi - lo) * 128],
                                     axis=AX.X, apply_absolute_value=True)
                    else:
                        e.reduce_max(am[:, 1:2], pt[:, 0:(hi - lo) * 128],
                                     axis=AX.X, apply_absolute_value=True)
                        e.tensor_max(am[:, 0:1], am[:, 0:1], am[:, 1:2])
                    e.engine_nop().then_inc(s.sem["ve"], 1)
                    s.cnt["ve"] += 1
                if self.upto == "ffn_tm2":
                    tmpf = self._dbg_tmpf
                    e.tensor_scalar(am[:, 0:1], am[:, 0:1], 1e-5, None,
                                    op0=ALU.max)
                    e.tensor_scalar(am[:, 1:2], am[:, 0:1], 1.0 / 127.0, None,
                                    op0=ALU.mult)
                    e.reciprocal(am[:, 0:1], am[:, 0:1])
                    e.tensor_scalar(am[:, 0:1], am[:, 0:1], 127.0, None,
                                    op0=ALU.mult)
                    for g in range((n * 128) // 512):
                        sl = tm_sb[:, g * 512:(g + 1) * 512]
                        e.tensor_scalar(tmpf[:, :], sl, am[:, 0:1], MAGIC,
                                        op0=ALU.mult, op1=ALU.add)
                        e.tensor_scalar(sl, tmpf[:, :], MAGIC, am[:, 1:2],
                                        op0=ALU.subtract, op1=ALU.mult)
                e.tensor_copy(s.kq[:, 27000:27000 + n * 128], tm_sb[:, 0:n * 128])
                e.tensor_copy(s.kq[:, 31500:31502], am[:, 0:2])

    def l0_ffn(self):
        nc, s = self.nc, self
        if s.kq is None:
            s.kq = s.ctx.enter_context(nc.sbuf_tensor("kq", [128, 16 * T], BF16))
        if self.upto.startswith("ffn_"):
            with nc.Block(s.name("dbg0")) as blk:
                @blk.gpsimd
                def _(e):
                    e.memset(s.kq[:, :], 0)
        self.l0_ffn_wquant()
        if self.upto == "ffn_w":
            with nc.Block("dbgw") as blk:
                @blk.vector
                def _(e):
                    e.tensor_copy(s.kq[:, 27000:27001], s.wu_m[:, :])
                    e.tensor_copy(s.kq[:, 27001:27002], s.wd_m[:, :])
                    e.tensor_copy(s.kq[:, 27002:27003], s.wu_r[:, :])
                    e.tensor_copy(s.kq[:, 27003:27004], s.wd_r[:, :])
                    e.tensor_copy(s.kq[:, 27004:27005], s.gup[:, :])

                @blk.sync
                def _(e):
                    dst = s.kq[:, 27856:27856 + 2048] \
                        .rearrange("p (a o) -> p a o", o=128)
                    src2 = s.wup_q[0].rearrange("(a p) o -> p a o", p=128)
                    s.dma(e, dst, src2, "qs")

            with nc.Block("dbgw2") as blk:
                @blk.vector
                def _(e):
                    e.wait_ge(s.sem["qs"], s.tgt("qs"))
            return
        FW = list(range(24))
        HS = list(range(24, 64))
        with (
            nc.sbuf_tensor("f_wb", [128, 2 * NDT * 128], BF16) as f_wb,
            nc.sbuf_tensor("f_db", [128, 2 * NOC1 * 128], BF16) as f_db,
            nc.sbuf_tensor("f_tmp", [128, 2048], F32) as f_tmp,
            nc.sbuf_tensor("f_rl", [128, 1024], BF16) as f_rl,
            nc.sbuf_tensor("f_tm", [128, NOC1 * 128], BF16) as f_tm,
            nc.psum_tensor("f_pa", [128, 1024], F32) as f_pa,
            nc.psum_tensor("f_pt", [128, 2048], BF16) as f_pt,
        ):
            for cc in range(4):
                c0 = cc * 512
                fw = FW[:16]
                sp = FW[16:24]
                self._ffn_h128(fw, f_pa, src_of=lambda i: s.xt(i, c0, 512))
                fw, sp = self._fwht_butterflies(fw, sp, 4)
                if self.upto in ("ffn_tm1", "ffn_tm2") and cc == 0:
                    self._dbg_tmpf = f_tmp
                    self._dbg_tm(fw, f_pt, f_tm)
                    self.dbg_slots = list(fw)
                    return
                if self.upto == "ffn_fwht1" and cc == 0:
                    self.dbg_slots = list(fw)
                    return
                self._ffn_quant(fw, f_pt, f_tm, f_tmp)
                if self.upto == "ffn_q1" and cc == 0:
                    self.dbg_slots = list(fw)
                    return

                hs = HS[:32]
                hsp = HS[32:40]
                with nc.Block(s.name("fup")) as blk:
                    pe0 = s.cnt["pe"]
                    ve0 = s.cnt["ve"]
                    ld_tgt = {}

                    @blk.scalar
                    def _(e):
                        for oc in range(NOC1):
                            if oc >= 2:
                                e.wait_ge(s.sem["pe"], pe0 + oc - 1)
                            q = ("dc", "dd")[oc % 2]
                            src = s.wup_q[oc].rearrange("(a p) o -> p a o", p=128)
                            dst = f_wb[:, (oc % 2) * 2048:(oc % 2 + 1) * 2048] \
                                .rearrange("p (a o) -> p a o", o=128)
                            s.dma(e, dst, src, q)
                            ld_tgt[oc] = (q, s.cnt[q])

                    @blk.tensor
                    def _(e):
                        for oc in range(NOC1):
                            e.wait_ge(s.sem[ld_tgt[oc][0]], ld_tgt[oc][1])
                            if oc >= 2:
                                e.wait_ge(s.sem["ve"], ve0 + oc - 1)
                            wb = (oc % 2) * 2048
                            for dt in range(NDT):
                                mm = e.matmul(
                                    f_pa[:, (oc % 2) * 512:(oc % 2 + 1) * 512],
                                    f_wb[:, wb + dt * 128: wb + (dt + 1) * 128],
                                    s._ffn_sl(fw[dt]),
                                    start=(dt == 0), stop=(dt == NDT - 1),
                                    skip_group_check=True)
                            mm.then_inc(s.sem["pe"], 1)
                            s.cnt["pe"] += 1

                    @blk.vector
                    def _(e):
                        for oc in range(NOC1):
                            e.wait_ge(s.sem["pe"], pe0 + oc + 1)
                            psl = f_pa[:, (oc % 2) * 512:(oc % 2 + 1) * 512]
                            rl = f_rl[:, (oc % 2) * 512:(oc % 2 + 1) * 512]
                            e.tensor_scalar(rl, psl, 0.0, None, op0=ALU.max)
                            s.vsync(e)
                            e.tensor_tensor(s._ffn_sl(hs[oc]), rl,
                                            rl, op=ALU.mult).then_inc(
                                s.sem["ve"], 1)
                            s.cnt["ve"] += 1

                if self.upto == "ffn_up" and cc == 0:
                    self.dbg_slots = list(hs)
                    return
                self._ffn_h128(hs, f_pa)
                hs, hsp = self._fwht_butterflies(hs, hsp, 5)
                self._ffn_quant(hs, f_pt, f_tm, f_tmp)

                with nc.Block(s.name("fdn")) as blk:
                    pe0 = s.cnt["pe"]
                    ve0 = s.cnt["ve"]
                    ld_tgt = {}

                    @blk.scalar
                    def _(e):
                        for ct in range(16):
                            if ct >= 2:
                                e.wait_ge(s.sem["pe"], pe0 + ct - 1)
                            q = ("dc", "dd")[ct % 2]
                            src = s.wdn_q[ct].rearrange("(a p) o -> p a o", p=128)
                            dst = f_db[:, (ct % 2) * 4096:(ct % 2 + 1) * 4096] \
                                .rearrange("p (a o) -> p a o", o=128)
                            s.dma(e, dst, src, q)
                            ld_tgt[ct] = (q, s.cnt[q])

                    @blk.tensor
                    def _(e):
                        for ct in range(16):
                            e.wait_ge(s.sem[ld_tgt[ct][0]], ld_tgt[ct][1])
                            if ct >= 2:
                                e.wait_ge(s.sem["ve"], ve0 + ct - 1)
                            wb = (ct % 2) * 4096
                            for ot in range(NOC1):
                                mm = e.matmul(
                                    f_pa[:, (ct % 2) * 512:(ct % 2 + 1) * 512],
                                    f_db[:, wb + ot * 128: wb + (ot + 1) * 128],
                                    s._ffn_sl(hs[ot]),
                                    start=(ot == 0), stop=(ot == NOC1 - 1),
                                    skip_group_check=True)
                            mm.then_inc(s.sem["pe"], 1)
                            s.cnt["pe"] += 1

                    @blk.vector
                    def _(e):
                        for ct in range(16):
                            e.wait_ge(s.sem["pe"], pe0 + ct + 1)
                            psl = f_pa[:, (ct % 2) * 512:(ct % 2 + 1) * 512]
                            xt = s.xt(ct, c0, 512)
                            e.scalar_tensor_tensor(
                                xt, psl, s.gup[:, :], xt,
                                op0=ALU.mult, op1=ALU.add).then_inc(s.sem["ve"], 1)
                            s.cnt["ve"] += 1

        for cc in range(4):
            self.layernorm_chunk(cc * 512)

    def _l1_rotary(self, tgt, c0, ncols, qsh):
        nc, s = self.nc, self
        for hg in range(2):
            with nc.Block(s.name("rot")) as blk:
                qs0 = s.cnt["qs"]

                @blk.sync
                def _(e):
                    for h8 in range(8):
                        h = hg * 8 + h8
                        sl = slice(h * T + c0, h * T + c0 + ncols)
                        d = slice(h8 * ncols, (h8 + 1) * ncols)
                        s.dma(e, qsh[0:16, d], tgt[16:32, sl], "qs")
                        s.dma(e, qsh[16:32, d], tgt[0:16, sl], "qs")

                @blk.vector
                def _(e):
                    e.wait_ge(s.sem["qs"], s.tgt("qs"))
                    sview = s.rotS[:, c0:c0 + ncols].unsqueeze(1) \
                        .broadcast_to([32, 8, ncols])
                    qview = qsh[:, 0:8 * ncols].rearrange("p (h t) -> p h t", h=8)
                    e.tensor_tensor(qview, qview, sview, op=ALU.mult)
                    for h8 in range(8):
                        h = hg * 8 + h8
                        sl = slice(h * T + c0, h * T + c0 + ncols)
                        rot = tgt[0:32, sl]
                        e.tensor_tensor(rot, rot, s.rotC[:, c0:c0 + ncols],
                                        op=ALU.mult)
                    s.vsync(e)
                    for h8 in range(8):
                        h = hg * 8 + h8
                        sl = slice(h * T + c0, h * T + c0 + ncols)
                        rot = tgt[0:32, sl]
                        e.tensor_tensor(rot, rot,
                                        qsh[:, h8 * ncols:(h8 + 1) * ncols],
                                        op=ALU.add)

    def _l1_proj_fm(self, wcol0, scale, store_scr):
        nc, s = self.nc, self
        with (
            nc.sbuf_tensor(s.name("pw"), [128, 2 * 16 * 128], BF16) as pw,
            nc.psum_tensor(s.name("pp"), [128, 4096], F32) as pp,
            nc.Block(s.name("proj")) as blk,
        ):
            WB = 16 * 128
            pe0 = s.cnt["pe"]
            ve0 = s.cnt["ve"]
            ld_tgt = {}

            @blk.gpsimd
            def _(e):
                for h in range(16):
                    if h >= 2:
                        e.wait_ge(s.sem["pe"], pe0 + h - 1)
                    q = ("da", "db")[h % 2]
                    src = s.wqkv[:, wcol0 + h * 128: wcol0 + (h + 1) * 128] \
                        .rearrange("(dt p) i -> p dt i", p=128)
                    dst = pw[:, (h % 2) * WB:(h % 2 + 1) * WB] \
                        .rearrange("p (dt i) -> p dt i", dt=NDT)
                    s.dma(e, dst, src, q)
                    ld_tgt[h] = (q, s.cnt[q])

            @blk.tensor
            def _(e):
                for h in range(16):
                    e.wait_ge(s.sem[ld_tgt[h][0]], ld_tgt[h][1])
                    if h >= 2:
                        e.wait_ge(s.sem["ve"], ve0 + h - 1)
                    wb = (h % 2) * WB
                    pb = (h % 2) * 2048
                    for dt in range(NDT):
                        for jc in range(4):
                            mm = e.matmul(
                                pp[:, pb + jc * 512: pb + (jc + 1) * 512],
                                pw[:, wb + dt * 128: wb + (dt + 1) * 128],
                                s.xt(dt, jc * 512, 512),
                                start=(dt == 0), stop=(dt == NDT - 1),
                                skip_group_check=True)
                    mm.then_inc(s.sem["pe"], 1)
                    s.cnt["pe"] += 1

            @blk.vector
            def _(e):
                for h in range(16):
                    e.wait_ge(s.sem["pe"], pe0 + h + 1)
                    pb = (h % 2) * 2048
                    if scale is None:
                        e.tensor_copy(s.kq[:, h * T:(h + 1) * T],
                                      pp[:, pb:pb + 2048])
                    else:
                        e.tensor_scalar(s.kq[:, h * T:(h + 1) * T],
                                        pp[:, pb:pb + 2048],
                                        s.qscl[:, 0:1], None, op0=ALU.mult)
                    e.engine_nop().then_inc(s.sem["ve"], 1)
                    s.cnt["ve"] += 1

        if not self.upto.endswith("_norot"):
            with nc.sbuf_tensor(s.name("qsh"), [32, 8 * T], BF16) as qsh:
                self._l1_rotary(s.kq, 0, T, qsh)

        if store_scr and not self.upto.startswith("l1x_kraw"):
            with nc.Block(s.name("kst")) as blk:
                @blk.sync
                def _(e):
                    for h in range(16):
                        s.dma(e, s.k_scr[h], s.kq[:, h * T:(h + 1) * T], "qs")
                    e.wait_ge(s.sem["qs"], s.tgt("qs"))

    def l1_attn(self):
        nc, s = self.nc, self
        QSC = 1.0 / np.sqrt(128.0)
        rot_ctx = ExitStack()
        s.rotC = rot_ctx.enter_context(nc.sbuf_tensor("rotC", [32, T], F32))
        s.rotS = rot_ctx.enter_context(nc.sbuf_tensor("rotS", [32, T], F32))
        s.mask4 = rot_ctx.enter_context(nc.sbuf_tensor("mask4", [128, 4 * 512], BF16))
        lo_ctx = ExitStack()
        rotCl = lo_ctx.enter_context(nc.sbuf_tensor("rotCl", [32, T], F32))
        rotSl = lo_ctx.enter_context(nc.sbuf_tensor("rotSl", [32, T], F32))
        with nc.Block("l1ld") as blk:
            @blk.sync
            def _(e):
                s.dma(e, s.rotC[:, :], s.rotC_h[:, :], "qs")
                s.dma(e, s.rotS[:, :], s.rotS_h[:, :], "qs")
                s.dma(e, rotCl[:, :], s.rotCl_h[:, :], "qs")
                s.dma(e, rotSl[:, :], s.rotSl_h[:, :], "qs")
                s.dma(e, s.mask4[:, :], s.mask_h[:, :], "qs")

            @blk.vector
            def _(e):
                e.wait_ge(s.sem["qs"], s.tgt("qs"))
                e.tensor_tensor(s.rotC[:, :], s.rotC[:, :], rotCl[:, :],
                                op=ALU.add)
                e.tensor_tensor(s.rotS[:, :], s.rotS[:, :], rotSl[:, :],
                                op=ALU.add)
        lo_ctx.close()

        self._l1_proj_fm(D, None, True)
        if self.upto.startswith("l1x_k"):
            return
        self._l1_proj_fm(0, 'qscl', False)
        if self.upto == "l1x_q":
            return

        with (
            nc.sbuf_tensor("vw", [128, 2 * 16 * 512], BF16) as vw,
            nc.sbuf_tensor("vtile", [128, 2 * 512], BF16) as vtile,
            nc.psum_tensor("vps", [128, 1024], F32) as vps,
            nc.Block("l1v") as blk,
        ):
            VB = 16 * 512
            pe0 = s.cnt["pe"]
            ve0 = s.cnt["ve"]
            sa0 = s.cnt["sa"]
            sb0 = s.cnt["sb"]
            ld_tgt = {}

            @blk.gpsimd
            def _(e):
                for ic in range(4):
                    if ic >= 2:
                        e.wait_ge(s.sem["pe"], pe0 + (ic - 1) * 16)
                    q = ("da", "db")[ic % 2]
                    src = s.wqkv[:, 4096 + ic * 512: 4096 + (ic + 1) * 512] \
                        .rearrange("(dt p) i -> p dt i", p=128)
                    dst = vw[:, (ic % 2) * VB:(ic % 2 + 1) * VB] \
                        .rearrange("p (dt i) -> p dt i", dt=NDT)
                    s.dma(e, dst, src, q)
                    ld_tgt[ic] = (q, s.cnt[q])
                for dt in range(NDT):
                    s.dma(e, s.x_scr[dt], s.xs[:, dt * T:(dt + 1) * T], "qg")

            @blk.tensor
            def _(e):
                for ic in range(4):
                    e.wait_ge(s.sem[ld_tgt[ic][0]], ld_tgt[ic][1])
                    wb = (ic % 2) * VB
                    for tt in range(16):
                        it = ic * 16 + tt
                        if it >= 2:
                            e.wait_ge(s.sem["ve"], ve0 + it - 1)
                        for dt in range(NDT):
                            mm = e.matmul(
                                vps[:, (it % 2) * 512:(it % 2 + 1) * 512],
                                s.xt(dt, tt * 128, 128),
                                vw[:, wb + dt * 512: wb + (dt + 1) * 512],
                                start=(dt == 0), stop=(dt == NDT - 1),
                                skip_group_check=True)
                        mm.then_inc(s.sem["pe"], 1)
                        s.cnt["pe"] += 1

            @blk.vector
            def _(e):
                for ic in range(4):
                    for tt in range(16):
                        it = ic * 16 + tt
                        e.wait_ge(s.sem["pe"], pe0 + it + 1)
                        if it >= 2:
                            par = it % 2
                            base = sa0 if par == 0 else sb0
                            e.wait_ge(s.sem[("sa", "sb")[par]],
                                      base + 16 * ((it - 2) // 2 + 1))
                        e.tensor_copy(vtile[:, (it % 2) * 512:(it % 2 + 1) * 512],
                                      vps[:, (it % 2) * 512:(it % 2 + 1) * 512]) \
                            .then_inc(s.sem["ve"], 1)
                        s.cnt["ve"] += 1

            @blk.sync
            def _(e):
                for ic in range(4):
                    for tt in range(16):
                        it = ic * 16 + tt
                        e.wait_ge(s.sem["ve"], ve0 + it + 1)
                        s.dma(e, s.v_scr[tt][:, ic * 512:(ic + 1) * 512],
                              vtile[:, (it % 2) * 512:(it % 2 + 1) * 512],
                              ("sa", "sb")[it % 2])
                e.wait_ge(s.sem["sa"], s.tgt("sa"))
                e.wait_ge(s.sem["sb"], s.tgt("sb"))
                e.wait_ge(s.sem["qg"], s.tgt("qg"))

        if self.upto == "l1x_v":
            return
        kbuf = lambda h8, j: s.xs[:, h8 * T + j * 128: h8 * T + (j + 1) * 128]
        vbuf = lambda j, h8: s.xs[:, 16 * 1024 + j * 1024 + h8 * 128:
                                  16 * 1024 + j * 1024 + (h8 + 1) * 128]
        with (
            nc.sbuf_tensor("pbuf", [128, 2 * 16 * 512], BF16) as pbuf,
            nc.sbuf_tensor("rec_c", [128, 2 * 512], F32) as rec_c,
            nc.psum_tensor("sc_ps", [128, 2048], F32) as sc_ps,
            nc.psum_tensor("sum_ps", [128, 512], F32) as sum_ps,
            nc.psum_tensor("av_ps", [128, 512], F32) as av_ps,
            nc.Block("l1att") as blk,
        ):
            PB = 16 * 512
            pe0 = s.cnt["pe"]
            ve0 = s.cnt["ve"]
            se0 = s.cnt["se"]
            qs0 = s.cnt["qs"]
            qc0 = s.cnt["qc"]
            kload = []
            vload = []

            @blk.sync
            def _(e):
                for c in range(4):
                    lim = 4 * (c + 1)
                    for hg in range(2):
                        idx = c * 2 + hg
                        if idx >= 1:
                            e.wait_ge(s.sem["pe"], pe0 + sum(
                                (4 * (cc + 1) + 3) // 4 + 2
                                for cc in range(4) for gg in range(2)
                                for _h in range(8)
                                if cc * 2 + gg < idx))
                        for h8 in range(8):
                            h = hg * 8 + h8
                            s.dma(e, s.xs[:, h8 * T: h8 * T + lim * 128],
                                  s.k_scr[h][:, 0:lim * 128], "qs")
                        kload.append(s.tgt("qs"))

            @blk.scalar
            def _(e):
                for c in range(4):
                    lim = 4 * (c + 1)
                    nwave = (lim + 3) // 4
                    for hg in range(2):
                        idx = c * 2 + hg
                        if idx >= 1:
                            e.wait_ge(s.sem["pe"], pe0 + sum(
                                (4 * (cc + 1) + 3) // 4 + 2
                                for cc in range(4) for gg in range(2)
                                for _h in range(8)
                                if cc * 2 + gg < idx))
                        for j in range(lim):
                            src = s.v_scr[j][:, hg * 1024:(hg + 1) * 1024]
                            dst = s.xs[:, 16 * 1024 + j * 1024:
                                       16 * 1024 + (j + 1) * 1024]
                            s.dma(e, dst, src, "qc")
                        vload.append(s.tgt("qc"))
                        for h8 in range(8):
                            hi = idx * 8 + h8
                            if hi >= 2:
                                pc, pg, ph = self._l1_head_seq(hi - 2)
                                e.wait_ge(s.sem["pe"], pe0 +
                                          self._l1_pe_before(pc, pg, ph) +
                                          (pc + 1) + 2)
                            for w in range(nwave):
                                e.wait_ge(s.sem["pe"],
                                          pe0 + self._l1_pe_before(c, hg, h8) + w + 1)
                                j0 = w * 4
                                j1 = min(lim, j0 + 4)
                                e.activation(
                                    pbuf[:, (h8 % 2) * PB + j0 * 512:
                                         (h8 % 2) * PB + j1 * 512],
                                    sc_ps[:, 0:(j1 - j0) * 512],
                                    AF.Exp).then_inc(s.sem["se"], 1)
                                s.cnt["se"] += 1

            @blk.tensor
            def _(e):
                for c in range(4):
                    lim = 4 * (c + 1)
                    nwave = (lim + 3) // 4
                    for hg in range(2):
                        idx = c * 2 + hg
                        e.wait_ge(s.sem["qs"], kload[idx])
                        e.wait_ge(s.sem["qc"], vload[idx])
                        for h8 in range(8):
                            h = hg * 8 + h8
                            for w in range(nwave):
                                j0 = w * 4
                                j1 = min(lim, j0 + 4)
                                if not (idx == 0 and h8 == 0 and w == 0):
                                    e.wait_ge(s.sem["se"], se0 +
                                              self._l1_se_before(c, hg, h8) + w)
                                for j in range(j0, j1):
                                    mm = e.matmul(
                                        sc_ps[:, (j - j0) * 512:(j - j0 + 1) * 512],
                                        kbuf(h8, j),
                                        s.kq[:, h * T + c * 512: h * T + c * 512 + 512],
                                        start=True, stop=True,
                                        skip_group_check=True)
                                mm.then_inc(s.sem["pe"], 1)
                                s.cnt["pe"] += 1
                            e.wait_ge(s.sem["ve"], ve0 + idx * 16 + h8 * 2 + 1)
                            for j in range(lim):
                                mm = e.matmul(
                                    sum_ps[:, :], s.ones[:, :],
                                    pbuf[:, (h8 % 2) * PB + j * 512:
                                         (h8 % 2) * PB + (j + 1) * 512],
                                    start=(j == 0), stop=(j == lim - 1),
                                    skip_group_check=True)
                            mm.then_inc(s.sem["pe"], 1)
                            s.cnt["pe"] += 1
                            for j in range(lim):
                                mm = e.matmul(
                                    av_ps[:, :], vbuf(j, h8),
                                    pbuf[:, (h8 % 2) * PB + j * 512:
                                         (h8 % 2) * PB + (j + 1) * 512],
                                    start=(j == 0), stop=(j == lim - 1),
                                    skip_group_check=True)
                            mm.then_inc(s.sem["pe"], 1)
                            s.cnt["pe"] += 1

            @blk.vector
            def _(e):
                for c in range(4):
                    lim = 4 * (c + 1)
                    nwave = (lim + 3) // 4
                    for hg in range(2):
                        idx = c * 2 + hg
                        for h8 in range(8):
                            h = hg * 8 + h8
                            e.wait_ge(s.sem["se"], se0 +
                                      self._l1_se_before(c, hg, h8) + nwave)
                            for r in range(4):
                                j = 4 * c + r
                                if j < lim:
                                    pslice = pbuf[:, (h8 % 2) * PB + j * 512:
                                                  (h8 % 2) * PB + (j + 1) * 512]
                                    e.tensor_tensor(
                                        pslice, pslice,
                                        s.mask4[:, r * 512:(r + 1) * 512],
                                        op=ALU.mult)
                            e.engine_nop().then_inc(s.sem["ve"], 1)
                            s.cnt["ve"] += 1
                            e.wait_ge(s.sem["pe"], pe0 +
                                      self._l1_pe_before(c, hg, h8) + nwave + 1)
                            e.reciprocal(rec_c[:, (h8 % 2) * 512:
                                               (h8 % 2 + 1) * 512], sum_ps[:, :])
                            s.vsync(e)
                            e.wait_ge(s.sem["pe"], pe0 +
                                      self._l1_pe_before(c, hg, h8) + nwave + 2)
                            e.tensor_tensor(
                                s.kq[:, h * T + c * 512: h * T + c * 512 + 512],
                                av_ps[:, :],
                                rec_c[:, (h8 % 2) * 512:(h8 % 2 + 1) * 512],
                                op=ALU.mult)
                            e.engine_nop().then_inc(s.sem["ve"], 1)
                            s.cnt["ve"] += 1

        if self.upto == "l1x_att":
            return
        with (
            nc.sbuf_tensor("ow_s", [128, 2 * 16 * 128], BF16) as ow_s,
            nc.psum_tensor("op2_ps", [128, 1024], F32) as op2_ps,
            nc.Block("l1o") as blk,
        ):
            WB = 16 * 128
            pe0 = s.cnt["pe"]
            ve0 = s.cnt["ve"]
            sa0 = s.cnt["sa"]
            sb0 = s.cnt["sb"]
            ld_tgt = {}

            @blk.gpsimd
            def _(e):
                for ot in range(16):
                    if ot >= 2:
                        e.wait_ge(s.sem["pe"], pe0 + (ot - 1) * 4)
                    q = ("da", "db")[ot % 2]
                    src = s.wout[:, ot * 128:(ot + 1) * 128] \
                        .rearrange("(it p) o -> p it o", p=128)
                    dst = ow_s[:, (ot % 2) * WB:(ot % 2 + 1) * WB] \
                        .rearrange("p (it o) -> p it o", it=NDT)
                    s.dma(e, dst, src, q)
                    ld_tgt[ot] = (q, s.cnt[q])

            @blk.sync
            def _(e):
                for ot in range(16):
                    if ot >= 2:
                        e.wait_ge(s.sem["ve"], ve0 + (ot - 1) * 4)
                    s.dma(e, s.xs[:, ot * T:(ot + 1) * T], s.x_scr[ot],
                          ("sa", "sb")[ot % 2])

            @blk.tensor
            def _(e):
                for ot in range(16):
                    e.wait_ge(s.sem[ld_tgt[ot][0]], ld_tgt[ot][1])
                    wb = (ot % 2) * WB
                    for ch in range(4):
                        it2 = ot * 4 + ch
                        pslice = op2_ps[:, (it2 % 2) * 512:(it2 % 2 + 1) * 512]
                        if it2 >= 2:
                            e.wait_ge(s.sem["ve"], ve0 + it2 - 1)
                        for itile in range(16):
                            mm = e.matmul(
                                pslice,
                                ow_s[:, wb + itile * 128: wb + (itile + 1) * 128],
                                s.kqt(itile, ch * 512, 512),
                                start=(itile == 0), stop=(itile == NDT - 1),
                                skip_group_check=True)
                        mm.then_inc(s.sem["pe"], 1)
                        s.cnt["pe"] += 1

            @blk.vector
            def _(e):
                for ot in range(16):
                    par = ot % 2
                    base = sa0 if par == 0 else sb0
                    e.wait_ge(s.sem[("sa", "sb")[par]],
                              base + 16 * (ot // 2 + 1))
                    for ch in range(4):
                        it2 = ot * 4 + ch
                        pslice = op2_ps[:, (it2 % 2) * 512:(it2 % 2 + 1) * 512]
                        e.wait_ge(s.sem["pe"], pe0 + it2 + 1)
                        xt = s.xt(ot, ch * 512, 512)
                        e.tensor_add(xt, pslice, xt).then_inc(s.sem["ve"], 1)
                        s.cnt["ve"] += 1

        for ch in range(4):
            self.layernorm_chunk(ch * 512)
        rot_ctx.close()

    def _l1_pe_before(self, c, hg, h8):
        tot = 0
        for cc in range(4):
            nw = cc + 1
            for gg in range(2):
                for hh in range(8):
                    if (cc, gg, hh) == (c, hg, h8):
                        return tot
                    tot += nw + 2
        return tot

    @staticmethod
    def _l1_head_seq(k):
        seq = [(cc, gg, hh) for cc in range(4) for gg in range(2)
               for hh in range(8)]
        return seq[k]

    def _l1_se_before(self, c, hg, h8):
        tot = 0
        for cc in range(4):
            nw = cc + 1
            for gg in range(2):
                for hh in range(8):
                    if (cc, gg, hh) == (c, hg, h8):
                        return tot
                    tot += nw
        return tot

    def l1_ffn(self):
        nc, s = self.nc, self

        def gsl(oc, lo, ncols):
            if oc < 32:
                return s.kq[:, oc * 1024 + lo: oc * 1024 + lo + ncols]
            return self._gx[:, (oc - 32) * 1024 + lo: (oc - 32) * 1024 + lo + ncols]

        with (
            nc.sbuf_tensor("gx", [128, 12 * 1024], BF16) as gx,
            nc.sbuf_tensor("gw", [128, 2 * 2048], BF16) as gw,
            nc.sbuf_tensor("uw", [128, 2 * 2048], BF16) as uw,
            nc.sbuf_tensor("dw", [128, 2 * NOC2 * 128], BF16) as dw,
            nc.sbuf_tensor("gt", [128, 2 * 1024], BF16) as gt,
            nc.sbuf_tensor("ut", [128, 2 * 1024], BF16) as ut,
            nc.psum_tensor("gps", [128, 1024], F32) as gps,
            nc.psum_tensor("ups", [128, 1024], F32) as ups,
            nc.psum_tensor("dps", [128, 1024], F32) as dps,
        ):
            self._gx = gx
            for tc in range(2):
                t0 = tc * 1024
                with nc.Block(s.name("f2gu")) as blk:
                    pe0 = s.cnt["pe"]
                    ve0 = s.cnt["ve"]
                    se0 = s.cnt["se"]
                    s0_ve2 = s.cnt["ve2"]
                    ld_tgt = {}

                    @blk.gpsimd
                    def _(e):
                        for oc in range(NOC2):
                            if oc >= 2:
                                e.wait_ge(s.sem["pe"], pe0 + 2 * (oc - 1))
                            q = ("da", "db")[oc % 2]
                            for (wh, wsb) in ((s.wg, gw), (s.wu, uw)):
                                src = wh[oc].rearrange("(dt p) o -> p dt o", p=128)
                                dst = wsb[:, (oc % 2) * 2048:(oc % 2 + 1) * 2048] \
                                    .rearrange("p (dt o) -> p dt o", dt=NDT)
                                s.dma(e, dst, src, q)
                            ld_tgt[oc] = (q, s.cnt[q])

                    @blk.vector
                    def _(e):
                        for oc in range(NOC2):
                            e.wait_ge(s.sem[ld_tgt[oc][0]], ld_tgt[oc][1])
                            if oc >= 2:
                                e.wait_ge(s.sem["pe"], pe0 + 2 * (oc - 1))
                            for wsb in (gw, uw):
                                sl = wsb[:, (oc % 2) * 2048:(oc % 2 + 1) * 2048]
                                e.tensor_scalar(sl, sl, 128.0, None,
                                                op0=ALU.subtract)
                            e.engine_nop().then_inc(s.sem["ve"], 1)
                            s.cnt["ve"] += 1
                            e.wait_ge(s.sem["se"], se0 + 2 * (oc + 1))
                            e.tensor_tensor(gsl(oc, 0, 1024),
                                            gt[:, (oc % 2) * 1024:(oc % 2 + 1) * 1024],
                                            ut[:, (oc % 2) * 1024:(oc % 2 + 1) * 1024],
                                            op=ALU.mult)
                            e.engine_nop().then_inc(s.sem["ve2"], 1)
                            s.cnt["ve2"] += 1

                    @blk.tensor
                    def _(e):
                        for oc in range(NOC2):
                            e.wait_ge(s.sem["ve"], ve0 + oc + 1)
                            if oc >= 1:
                                e.wait_ge(s.sem["se"], se0 + 2 * oc)
                            wb = (oc % 2) * 2048
                            for dt in range(NDT):
                                for nn in range(2):
                                    e.matmul(
                                        gps[:, nn * 512:(nn + 1) * 512],
                                        gw[:, wb + dt * 128: wb + (dt + 1) * 128],
                                        s.xt(dt, t0 + nn * 512, 512),
                                        start=(dt == 0), stop=(dt == NDT - 1),
                                        skip_group_check=True)
                                    mm = e.matmul(
                                        ups[:, nn * 512:(nn + 1) * 512],
                                        uw[:, wb + dt * 128: wb + (dt + 1) * 128],
                                        s.xt(dt, t0 + nn * 512, 512),
                                        start=(dt == 0), stop=(dt == NDT - 1),
                                        skip_group_check=True)
                            mm.then_inc(s.sem["pe"], 2)
                            s.cnt["pe"] += 2

                    @blk.scalar
                    def _(e):
                        for oc in range(NOC2):
                            e.wait_ge(s.sem["pe"], pe0 + 2 * (oc + 1))
                            if oc >= 2:
                                e.wait_ge(s.sem["ve2"], s0_ve2 + oc - 1)
                            e.activation(gt[:, (oc % 2) * 1024:(oc % 2 + 1) * 1024],
                                         gps[:, :], AF.Silu,
                                         scale=s.sg_sb[:, oc:oc + 1]).then_inc(
                                s.sem["se"], 1)
                            s.cnt["se"] += 1
                            e.activation(ut[:, (oc % 2) * 1024:(oc % 2 + 1) * 1024],
                                         ups[:, :], AF.Identity,
                                         scale=s.su_sb[:, oc:oc + 1]).then_inc(
                                s.sem["se"], 1)
                            s.cnt["se"] += 1

                with nc.Block(s.name("f2dn")) as blk:
                    pe0 = s.cnt["pe"]
                    ve0 = s.cnt["ve"]
                    ld_tgt = {}

                    @blk.gpsimd
                    def _(e):
                        for ct in range(16):
                            if ct >= 2:
                                e.wait_ge(s.sem["pe"], pe0 + ct - 1)
                            q = ("da", "db")[ct % 2]
                            src = s.wd[ct].rearrange("(ot p) o -> p ot o", p=128)
                            dst = dw[:, (ct % 2) * NOC2 * 128:
                                     (ct % 2 + 1) * NOC2 * 128] \
                                .rearrange("p (ot o) -> p ot o", ot=NOC2)
                            s.dma(e, dst, src, q)
                            ld_tgt[ct] = (q, s.cnt[q])

                    @blk.vector
                    def _(e):
                        for ct in range(16):
                            e.wait_ge(s.sem[ld_tgt[ct][0]], ld_tgt[ct][1])
                            if ct >= 2:
                                e.wait_ge(s.sem["pe"], pe0 + ct - 1)
                            sl = dw[:, (ct % 2) * NOC2 * 128:
                                    (ct % 2 + 1) * NOC2 * 128]
                            e.tensor_scalar(sl, sl, 128.0, None,
                                            op0=ALU.subtract)
                            e.engine_nop().then_inc(s.sem["ve"], 1)
                            s.cnt["ve"] += 1
                            e.wait_ge(s.sem["pe"], pe0 + ct + 1)
                            for nn in range(2):
                                xt = s.xt(ct, t0 + nn * 512, 512)
                                e.scalar_tensor_tensor(
                                    xt, dps[:, nn * 512:(nn + 1) * 512],
                                    s.sd_sb[:, ct:ct + 1], xt,
                                    op0=ALU.mult, op1=ALU.add)
                            e.engine_nop().then_inc(s.sem["ve2"], 1)
                            s.cnt["ve2"] += 1

                    @blk.tensor
                    def _(e):
                        for ct in range(16):
                            e.wait_ge(s.sem["ve"], ve0 + ct + 1)
                            if ct >= 1:
                                e.wait_ge(s.sem["ve2"], s.cnt["ve2"] - 16 + ct)
                            wb = (ct % 2) * NOC2 * 128
                            for ot in range(NOC2):
                                for nn in range(2):
                                    mm = e.matmul(
                                        dps[:, nn * 512:(nn + 1) * 512],
                                        dw[:, wb + ot * 128: wb + (ot + 1) * 128],
                                        gsl(ot, nn * 512, 512),
                                        start=(ot == 0), stop=(ot == NOC2 - 1),
                                        skip_group_check=True)
                            mm.then_inc(s.sem["pe"], 1)
                            s.cnt["pe"] += 1

            self._gx = None

    def final_out(self):
        nc, s = self.nc, self
        with nc.sbuf_tensor("ostage", [128, 16 * 512], F32) as ostage:
            for ch in range(4):
                self.layernorm_chunk(ch * 512, final=True, out_stage=ostage)
                with nc.Block(s.name("fo")) as blk:
                    @blk.sync
                    def _(e):
                        for dt in range(NDT):
                            s.dma(e, s.outT[dt * 128:(dt + 1) * 128,
                                            ch * 512:(ch + 1) * 512],
                                  ostage[:, dt * 512:(dt + 1) * 512],
                                  ("sa", "sb")[ch % 2])
                        e.wait_ge(s.sem[("sa", "sb")[ch % 2]],
                                  s.tgt(("sa", "sb")[ch % 2]))

    def write_out_from_xs(self, srct=None):
        nc, s = self.nc, self
        if srct is None:
            srct = s.xs
        with (
            nc.sbuf_tensor(s.name("ostage"), [128, 2 * T], F32) as ostage,
            nc.Block(s.name("wout")) as blk,
        ):
            ve0 = s.cnt["ve"]
            sa0 = s.cnt["sa"]
            sb0 = s.cnt["sb"]

            @blk.vector
            def _(e):
                for dt in range(NDT):
                    if dt >= 2:
                        par = dt % 2
                        base = sa0 if par == 0 else sb0
                        e.wait_ge(s.sem[("sa", "sb")[par]],
                                  base + 16 * ((dt - 2) // 2 + 1))
                    e.tensor_copy(ostage[:, (dt % 2) * T:(dt % 2 + 1) * T],
                                  srct[:, dt * T:(dt + 1) * T]).then_inc(
                        s.sem["ve"], 1)
                    s.cnt["ve"] += 1

            @blk.sync
            def _(e):
                for dt in range(NDT):
                    e.wait_ge(s.sem["ve"], ve0 + dt + 1)
                    s.dma(e, s.outT[dt * 128:(dt + 1) * 128, :],
                          ostage[:, (dt % 2) * T:(dt % 2 + 1) * T],
                          ("sa", "sb")[dt % 2])
                e.wait_ge(s.sem["sa"], s.tgt("sa"))
                e.wait_ge(s.sem["sb"], s.tgt("sb"))

    def build(self):
        self.prologue()
        if self.upto.startswith("ffn_"):
            self.l0_ffn()
            self.write_out_from_xs(self.kq)
            self.ctx.close()
            return self.nc
        if self.upto == "l1a_only" or self.upto.startswith("l1x_"):
            if self.kq is None:
                self.kq = self.ctx.enter_context(
                    self.nc.sbuf_tensor("kq", [128, 16 * T], BF16))
            self.l1_attn()
            if self.upto.startswith("l1x_"):
                self.write_out_from_xs(self.kq)
            else:
                self.write_out_from_xs()
            self.ctx.close()
            return self.nc
        self.l0_attn()
        if self.upto == "l0a":
            self.write_out_from_xs()
            self.ctx.close()
            return self.nc
        self.l0_ffn()
        if self.upto == "l0f":
            self.write_out_from_xs()
            self.ctx.close()
            return self.nc
        self.l1_attn()
        if self.upto == "l1a":
            self.write_out_from_xs()
            self.ctx.close()
            return self.nc
        self.l1_ffn()
        self.final_out()
        self.ctx.close()
        return self.nc


def _host_consts():
    ones = np.ones((128, 128), dtype=ml_dtypes.bfloat16)
    gsel = np.zeros((128, 128), dtype=np.float32)
    gsel[0:64, 0:64] = 1.0
    gsel[64:128, 64:128] = 1.0
    ident = np.eye(128).astype(ml_dtypes.bfloat16)
    h128 = _sylvester(128).astype(ml_dtypes.bfloat16)
    import jax
    import jax.numpy as jnp
    cpu = jax.devices("cpu")[0]
    with jax.default_device(cpu):
        inv_freq = 1.0 / 10000.0 ** (jnp.arange(0, 32, 2,
                                                dtype=jnp.float32) / 32)
        fr_j = jnp.arange(T, dtype=jnp.float32)[:, None] * inv_freq[None, :]
        pos = jnp.concatenate((fr_j, fr_j), axis=-1)
        cosv = np.asarray(jnp.cos(pos)).astype(np.float32)
        sinv = np.asarray(jnp.sin(pos)).astype(np.float32)
    rotC = np.ascontiguousarray(cosv.T)
    sT = sinv.T
    rotS = np.concatenate([-sT[0:16], sT[16:32]], 0).astype(np.float32)
    rotS = np.ascontiguousarray(np.concatenate(
        [-sinv.T[16:32] * 0 - sinv.T[0:16], sinv.T[16:32]], 0)).astype(np.float32)
    rotCl = np.zeros_like(rotC)
    rotSl = np.zeros_like(rotS)
    qs_hi = np.float32(1.0 / np.sqrt(np.float64(128.0)))
    qs_lo = np.float32(1.0 / np.sqrt(np.float64(128.0)) - np.float64(qs_hi))
    misc = np.zeros((128, 4), dtype=np.float32)
    misc[:, 0] = qs_hi
    misc[:, 1] = qs_lo
    m = np.zeros((128, 4, 4, 128), dtype=np.float32)
    for r in range(4):
        for itl in range(4):
            if r < itl:
                m[:, r, itl, :] = 1.0
        m[:, r, r, :] = (np.arange(128)[:, None] <= np.arange(128)[None, :])
    mask = m.reshape(128, 4 * 512).astype(ml_dtypes.bfloat16)
    return dict(ones_h=ones, gsel_h=gsel.astype(ml_dtypes.bfloat16),
                ident_h=ident, h128_h=h128,
                rotC_h=rotC, rotS_h=rotS, rotCl_h=rotCl, rotSl_h=rotSl,
                misc_h=misc, mask_h=mask)


def _prep_core_inputs(b, inp, consts):
    c = lambda a: np.ascontiguousarray(a)
    x = np.asarray(inp["L0_attn_hidden_states"])
    d = dict(
        xT=c(np.asarray(x[b]).T.astype(np.float32)),
        Wq=c(np.asarray(inp["L0_attn_Wq"])), Wk=c(np.asarray(inp["L0_attn_Wk"])),
        Wv=c(np.asarray(inp["L0_attn_Wv"])), Wo=c(np.asarray(inp["L0_attn_Wo"])),
        bo=c(np.asarray(inp["L0_attn_bo"]).reshape(16, 128).T),
        wup=c(np.asarray(inp["L0_ffn_w_up"]).T.reshape(D, NOC1, 128)
              .transpose(1, 0, 2)),
        wdn=c(np.asarray(inp["L0_ffn_w_down"]).T.reshape(HID1, 16, 128)
              .transpose(1, 0, 2)),
        wqkv=c(np.asarray(inp["L1_attn_w_qkv"]).T),
        wout=c(np.asarray(inp["L1_attn_w_out"]).T),
        wg=c(np.asarray(inp["L1_ffn_wq_gate"]).T.reshape(D, NOC2, 128)
             .transpose(1, 0, 2)),
        wu=c(np.asarray(inp["L1_ffn_wq_up"]).T.reshape(D, NOC2, 128)
             .transpose(1, 0, 2)),
        wd=c(np.asarray(inp["L1_ffn_wq_down"]).T.reshape(HID2, 16, 128)
             .transpose(1, 0, 2)),
        sg=c(np.asarray(inp["L1_ffn_s_gate"]).reshape(NOC2, 128).T),
        su=c(np.asarray(inp["L1_ffn_s_up"]).reshape(NOC2, 128).T),
        sd=c(np.asarray(inp["L1_ffn_s_down"]).reshape(16, 128).T),
    )
    d.update(consts)
    return d


def kernel(**inputs):
    nc = B().build()
    consts = _host_consts()
    in_maps = [_prep_core_inputs(b, inputs, consts) for b in range(8)]
    res = run_bass_kernel_spmd(nc, in_maps, core_ids=list(range(8)))
    out = np.stack([np.ascontiguousarray(r["outT"].T) for r in res.results])
    return out.astype(np.float32)


# version 05
# speedup vs baseline: 1.0274x; 1.0274x over previous
import numpy as np
import ml_dtypes
import concourse.bass as bass
import concourse.mybir as mybir
from concourse.bass_utils import run_bass_kernel_spmd
from contextlib import ExitStack

F32 = mybir.dt.float32
BF16 = mybir.dt.bfloat16
I32 = mybir.dt.int32
AF = mybir.ActivationFunctionType
ALU = mybir.AluOpType
AX = mybir.AxisListType

MAGIC = 12582912.0
T = 2048
D = 2048
NDT = 16
HID1 = 4096
HID2 = 5632
NOC1 = HID1 // 128
NOC2 = HID2 // 128


def _sylvester(n):
    h = np.array([[1.0]])
    while h.shape[0] < n:
        h = np.block([[h, h], [h, -h]])
    return h


class B:

    def __init__(self, upto="all"):
        self.upto = upto
        self.uid = 0
        nc = bass.Bass("TRN2", target_bir_lowering=False)
        self.nc = nc
        self.ctx = ExitStack()

        dram = nc.dram_tensor
        self.xT = dram("xT", [D, T], F32, kind="ExternalInput")
        self.Wq = dram("Wq", [D, D], F32, kind="ExternalInput")
        self.Wk = dram("Wk", [D, D], F32, kind="ExternalInput")
        self.Wv = dram("Wv", [D, D], F32, kind="ExternalInput")
        self.Wo = dram("Wo", [D, D], F32, kind="ExternalInput")
        self.bo = dram("bo", [128, 16], F32, kind="ExternalInput")
        self.wup = dram("wup", [NOC1, D, 128], F32, kind="ExternalInput")
        self.wdn = dram("wdn", [16, HID1, 128], F32, kind="ExternalInput")
        self.wqkv = dram("wqkv", [D, 3 * D], F32, kind="ExternalInput")
        self.wout = dram("wout", [D, D], F32, kind="ExternalInput")
        self.wg = dram("wg", [NOC2, D, 128], I32, kind="ExternalInput")
        self.wu = dram("wu", [NOC2, D, 128], I32, kind="ExternalInput")
        self.wd = dram("wd", [16, HID2, 128], I32, kind="ExternalInput")
        self.sg = dram("sg", [128, NOC2], F32, kind="ExternalInput")
        self.su = dram("su", [128, NOC2], F32, kind="ExternalInput")
        self.sd = dram("sd", [128, 16], F32, kind="ExternalInput")
        self.ones_h = dram("ones_h", [128, 128], BF16, kind="ExternalInput")
        self.gsel_h = dram("gsel_h", [128, 128], BF16, kind="ExternalInput")
        self.ident_h = dram("ident_h", [128, 128], BF16, kind="ExternalInput")
        self.h128_h = dram("h128_h", [128, 128], BF16, kind="ExternalInput")
        self.rotC_h = dram("rotC_h", [32, T], F32, kind="ExternalInput")
        self.rotS_h = dram("rotS_h", [32, T], F32, kind="ExternalInput")
        self.rotCl_h = dram("rotCl_h", [32, T], F32, kind="ExternalInput")
        self.rotSl_h = dram("rotSl_h", [32, T], F32, kind="ExternalInput")
        self.misc_h = dram("misc_h", [128, 4], F32, kind="ExternalInput")
        self.mask_h = dram("mask_h", [128, 4 * 512], BF16, kind="ExternalInput")

        self.outT = dram("outT", [D, T], F32, kind="ExternalOutput")

        self.wup_q = dram("wup_q", [NOC1, D, 128], BF16)
        self.wdn_q = dram("wdn_q", [16, HID1, 128], BF16)
        self.k_scr = dram("k_scr", [16, 128, T], BF16)
        self.v_scr = dram("v_scr", [16, 128, D], BF16)
        self.x_scr = dram("x_scr", [16, 128, T], BF16)

        sb = lambda *a: self.ctx.enter_context(nc.sbuf_tensor(*a))
        self.xs = sb("xs", [128, NDT * T], BF16)
        self.kq = None
        self.ones = sb("ones", [128, 128], BF16)
        self.gsel = sb("gsel", [128, 128], BF16)
        self.ident = sb("ident", [128, 128], BF16)
        self.h128 = sb("h128", [128, 128], BF16)
        self.rotC = None
        self.rotS = None
        self.mask4 = None
        self.bo_sb = sb("bo_sb", [128, 16], F32)
        self.sg_sb = sb("sg_sb", [128, NOC2], F32)
        self.su_sb = sb("su_sb", [128, NOC2], F32)
        self.sd_sb = sb("sd_sb", [128, 16], F32)
        self.kvbd = sb("kvbd", [128, 16 * 128], BF16)
        self.ks_sb = sb("ks_sb", [128, 16], F32)
        self.gup = sb("gup", [128, 1], F32)
        self.mg = sb("mg", [128, 1], F32)
        self.misc = sb("misc", [128, 4], F32)
        self.qscl = sb("qscl", [128, 1], F32)
        self.mB = sb("mB", [128, 512], F32)
        self.rB = sb("rB", [128, 512], F32)

        self.sem = {}
        self.cnt = {}
        for name in ["qs", "qg", "qc", "da", "db", "dc", "dd", "sa", "sb",
                     "pe", "ve", "se", "pe2", "ve2", "se2", "vs"]:
            self.sem[name] = self.ctx.enter_context(nc.semaphore(name))
            self.cnt[name] = 0

    def name(self, prefix):
        self.uid += 1
        return f"{prefix}_{self.uid}"

    def dma(self, eng, dst, src, q):
        eng.dma_start(dst, src).then_inc(self.sem[q], 16)
        self.cnt[q] += 16
        return self.cnt[q]

    def tgt(self, name):
        return self.cnt[name]

    def vsync(self, e):
        self.cnt["vs"] += 1
        e.engine_nop().then_inc(self.sem["vs"], 1)
        e.wait_ge(self.sem["vs"], self.cnt["vs"])

    def xt(self, dt, lo, n):
        return self.xs[:, dt * T + lo: dt * T + lo + n]

    def kqt(self, it, lo, n):
        return self.kq[:, it * T + lo: it * T + lo + n]

    def prologue(self):
        nc, s = self.nc, self
        with nc.Block("prologue") as blk:
            @blk.sync
            def _(e):
                for dst, src in [
                    (s.ones, s.ones_h), (s.gsel, s.gsel_h), (s.ident, s.ident_h),
                    (s.h128, s.h128_h), (s.bo_sb, s.bo), (s.sg_sb, s.sg),
                    (s.su_sb, s.su), (s.sd_sb, s.sd), (s.misc, s.misc_h),
                ]:
                    s.dma(e, dst[:, :], src[:, :], "qs")

            @blk.gpsimd
            def _(e):
                xv = s.xT.rearrange("(dt p) t -> p dt t", p=128)
                xd = s.xs[:, :].rearrange("p (dt t) -> p dt t", dt=NDT)
                s.dma(e, xd, xv, "qg")
                e.memset(s.kvbd[:, :], 0)
                e.memset(s.mg[:, :], 512.0)

            @blk.vector
            def _(e):
                e.wait_ge(s.sem["qs"], s.tgt("qs"))
                e.wait_ge(s.sem["qg"], s.tgt("qg"))
                e.tensor_tensor(s.qscl[:, :], s.misc[:, 0:1], s.misc[:, 1:2],
                                op=ALU.add)

    def layernorm_chunk(self, c0, final=False, out_stage=None):
        nc, s = self.nc, self
        with (
            nc.sbuf_tensor(s.name("ln_sq"), [128, 1024], BF16) as sq,
            nc.psum_tensor(s.name("ln_ps"), [128, 1024], F32) as ps,
        ):
            psum_s = ps[:, 0:512]
            psum_q = ps[:, 512:1024]
            with nc.Block(s.name("ln1")) as blk:
                ve0 = s.cnt["ve"]
                pe0 = s.cnt["pe"]

                @blk.vector
                def _(e):
                    for dt in range(NDT):
                        if dt > 1:
                            e.wait_ge(s.sem["pe"], pe0 + dt - 1)
                        xt = s.xt(dt, c0, 512)
                        e.tensor_mul(sq[:, (dt % 2) * 512:(dt % 2 + 1) * 512],
                                     xt, xt).then_inc(s.sem["ve"], 1)
                        s.cnt["ve"] += 1

                @blk.tensor
                def _(e):
                    for dt in range(NDT):
                        xt = s.xt(dt, c0, 512)
                        e.wait_ge(s.sem["ve"], ve0 + dt + 1)
                        e.matmul(psum_s, s.ones[:, :], xt,
                                 start=(dt == 0), stop=(dt == NDT - 1),
                                 skip_group_check=True)
                        e.matmul(psum_q, s.ones[:, :],
                                 sq[:, (dt % 2) * 512:(dt % 2 + 1) * 512],
                                 start=(dt == 0), stop=(dt == NDT - 1),
                                 skip_group_check=True).then_inc(s.sem["pe"], 1)
                        s.cnt["pe"] += 1

            with nc.Block(s.name("ln2")) as blk:
                @blk.vector
                def _(e):
                    e.tensor_scalar(s.mB[:, :], psum_s, 1.0 / D, None, op0=ALU.mult)
                    e.tensor_scalar(s.rB[:, :], psum_q, 1.0 / D, None, op0=ALU.mult)
                    s.vsync(e)
                    e.tensor_tensor(sq[:, 0:512], s.mB[:, :], s.mB[:, :],
                                    op=ALU.mult)
                    s.vsync(e)
                    e.tensor_tensor(s.rB[:, :], s.rB[:, :], sq[:, 0:512],
                                    op=ALU.subtract)
                    s.vsync(e)
                    e.tensor_scalar(s.rB[:, :], s.rB[:, :], 1e-5, None,
                                    op0=ALU.add).then_inc(s.sem["ve"], 1)
                    s.cnt["ve"] += 1

                @blk.scalar
                def _(e):
                    e.wait_ge(s.sem["ve"], s.tgt("ve"))
                    e.activation(s.rB[:, :], s.rB[:, :], AF.Sqrt).then_inc(
                        s.sem["se"], 1)
                    s.cnt["se"] += 1

            with nc.Block(s.name("ln3")) as blk:
                @blk.vector
                def _(e):
                    e.wait_ge(s.sem["se"], s.tgt("se"))
                    e.reciprocal(s.rB[:, :], s.rB[:, :])
                    s.vsync(e)
                    for dt in range(NDT):
                        xt = s.xt(dt, c0, 512)
                        sqp = sq[:, (dt % 2) * 512:(dt % 2 + 1) * 512]
                        e.tensor_sub(sqp, xt, s.mB[:, :])
                        s.vsync(e)
                        if final:
                            e.tensor_mul(out_stage[:, dt * 512:(dt + 1) * 512],
                                         sqp, s.rB[:, :])
                        else:
                            e.tensor_mul(xt, sqp, s.rB[:, :])

    def l0_attn(self):
        nc, s = self.nc, self
        scopeA = nc.named_scope("l0a_A"); scopeA.__enter__()
        with (
            nc.sbuf_tensor("wk_s", [128, 2 * 16 * 512], BF16) as wk_s,
            nc.sbuf_tensor("wv_s", [128, 2 * 16 * 512], BF16) as wv_s,
            nc.sbuf_tensor("k_sb", [128, 512], BF16) as k_sb,
            nc.sbuf_tensor("v_sb", [128, 512], BF16) as v_sb,
            nc.psum_tensor("kp", [128, 512], F32) as kp,
            nc.psum_tensor("vp", [128, 512], F32) as vp,
            nc.psum_tensor("kvp", [128, 512], F32) as kvp,
            nc.psum_tensor("ksp", [128, 16], F32) as ksp,
            nc.Block("l0a_A") as blk,
        ):
            HB = 16 * 512
            pe0 = s.cnt["pe"]
            pe20 = s.cnt["pe2"]
            ve0 = s.cnt["ve"]
            ve20 = s.cnt["ve2"]

            ld_tgt = {}

            @blk.gpsimd
            def _(e):
                for c4 in range(4):
                    if c4 >= 2:
                        e.wait_ge(s.sem["pe"], pe0 + (c4 - 1) * 16)
                    q = ("da", "db")[c4 % 2]
                    for (w_h, w_s) in ((s.Wk, wk_s), (s.Wv, wv_s)):
                        src = w_h[:, c4 * 512:(c4 + 1) * 512] \
                            .rearrange("(dt p) i -> p dt i", p=128)
                        dst = w_s[:, (c4 % 2) * HB:(c4 % 2 + 1) * HB] \
                            .rearrange("p (dt i) -> p dt i", dt=NDT)
                        s.dma(e, dst, src, q)
                    ld_tgt[c4] = (q, s.cnt[q])

            @blk.tensor
            def _(e):
                for c4 in range(4):
                    e.wait_ge(s.sem[ld_tgt[c4][0]], ld_tgt[c4][1])
                    if c4 > 0:
                        e.wait_ge(s.sem["ve2"], ve20 + c4)
                    wb = (c4 % 2) * HB
                    for tt in range(16):
                        it = c4 * 16 + tt
                        if it > 0:
                            e.wait_ge(s.sem["ve"], ve0 + it)
                        for dt in range(NDT):
                            xt = s.xt(dt, tt * 128, 128)
                            e.matmul(kp[:, :], xt,
                                     wk_s[:, wb + dt * 512: wb + (dt + 1) * 512],
                                     start=(dt == 0), stop=(dt == NDT - 1),
                                     skip_group_check=True)
                            mm = e.matmul(vp[:, :], xt,
                                          wv_s[:, wb + dt * 512: wb + (dt + 1) * 512],
                                          start=(dt == 0), stop=(dt == NDT - 1),
                                          skip_group_check=True)
                        mm.then_inc(s.sem["pe"], 1)
                        s.cnt["pe"] += 1
                        e.wait_ge(s.sem["ve"], ve0 + it + 1)
                        for ps_i in range(4):
                            pair = c4 * 4 + ps_i
                            sl = slice(ps_i * 128, (ps_i + 1) * 128)
                            e.matmul(kvp[:, sl], k_sb[:, sl], v_sb[:, sl],
                                     start=(tt == 0 and ps_i == 0),
                                     stop=(tt == 15),
                                     skip_group_check=True)
                            mm = e.matmul(ksp[:, pair:pair + 1], k_sb[:, sl],
                                          s.ones[:, 0:1],
                                          start=(tt == 0 and ps_i == 0),
                                          stop=(tt == 15),
                                          skip_group_check=True)
                        mm.then_inc(s.sem["pe2"], 1)
                        s.cnt["pe2"] += 1

            @blk.vector
            def _(e):
                for c4 in range(4):
                    for tt in range(16):
                        it = c4 * 16 + tt
                        e.wait_ge(s.sem["pe"], pe0 + it + 1)
                        e.tensor_scalar(k_sb[:, :], kp[:, :], 0.0, None,
                                        op0=ALU.max)
                        e.tensor_copy(v_sb[:, :], vp[:, :]).then_inc(
                            s.sem["ve"], 1)
                        s.cnt["ve"] += 1
                    e.wait_ge(s.sem["pe2"], pe20 + (c4 + 1) * 16)
                    for ps_i in range(4):
                        pair = c4 * 4 + ps_i
                        for half in range(2):
                            h0 = half * 64
                            e.tensor_copy(
                                s.kvbd[h0:h0 + 64,
                                       pair * 128 + h0: pair * 128 + h0 + 64],
                                kvp[h0:h0 + 64,
                                    ps_i * 128 + h0: ps_i * 128 + h0 + 64])
                    e.tensor_copy(s.ks_sb[:, c4 * 4:(c4 + 1) * 4],
                                  ksp[:, c4 * 4:(c4 + 1) * 4]).then_inc(
                        s.sem["ve2"], 1)
                    s.cnt["ve2"] += 1

        scopeA.__exit__(None, None, None)
        scopeB = nc.named_scope("l0a_B1"); scopeB.__enter__()
        if s.kq is None:
            s.kq = s.ctx.enter_context(nc.sbuf_tensor("kq", [128, 16 * T], BF16))
        with (
            nc.sbuf_tensor("wq_s", [128, 2 * 16 * 128], BF16) as wq_s,
            nc.sbuf_tensor("q_sb", [128, 2 * 512], BF16) as q_sb,
            nc.sbuf_tensor("tq_sb", [128, 512], BF16) as tq_sb,
            nc.sbuf_tensor("rec_sb", [128, 2 * 512], F32) as rec_sb,
            nc.psum_tensor("qp", [128, 1024], F32) as qp,
            nc.psum_tensor("npp", [128, 1024], F32) as npp,
            nc.psum_tensor("qkp", [128, 1024], F32) as qkp,
            nc.Block("l0a_B1") as blk,
        ):
            WB = 16 * 128
            pe0 = s.cnt["pe"]
            ve0 = s.cnt["ve"]
            ld_tgt = {}

            @blk.gpsimd
            def _(e):
                for pair in range(16):
                    if pair >= 2:
                        e.wait_ge(s.sem["pe"], pe0 + (pair - 1) * 12)
                    q = ("da", "db")[pair % 2]
                    src = s.Wq[:, pair * 128:(pair + 1) * 128] \
                        .rearrange("(dt p) i -> p dt i", p=128)
                    dst = wq_s[:, (pair % 2) * WB:(pair % 2 + 1) * WB] \
                        .rearrange("p (dt i) -> p dt i", dt=NDT)
                    s.dma(e, dst, src, q)
                    ld_tgt[pair] = (q, s.cnt[q])

            @blk.tensor
            def _(e):
                e.wait_ge(s.sem["ve2"], s.tgt("ve2"))
                for pair in range(16):
                    e.wait_ge(s.sem[ld_tgt[pair][0]], ld_tgt[pair][1])
                    wb = (pair % 2) * WB
                    for ch in range(4):
                        it = pair * 4 + ch
                        par = it % 2
                        if it >= 2:
                            e.wait_ge(s.sem["ve"], ve0 + (it - 2) * 3 + 1)
                        for dt in range(NDT):
                            mm = e.matmul(
                                qp[:, par * 512:(par + 1) * 512],
                                wq_s[:, wb + dt * 128: wb + (dt + 1) * 128],
                                s.xt(dt, ch * 512, 512),
                                start=(dt == 0), stop=(dt == NDT - 1),
                                skip_group_check=True)
                        mm.then_inc(s.sem["pe"], 1)
                        s.cnt["pe"] += 1
                        e.wait_ge(s.sem["ve"], ve0 + it * 3 + 1)
                        if it >= 2:
                            e.wait_ge(s.sem["ve"], ve0 + (it - 2) * 3 + 2)
                        e.matmul(npp[:, par * 512:(par + 1) * 512],
                                 s.gsel[:, :], tq_sb[:, :],
                                 start=True, stop=True,
                                 skip_group_check=True).then_inc(s.sem["pe"], 1)
                        s.cnt["pe"] += 1
                        if it >= 2:
                            e.wait_ge(s.sem["ve"], ve0 + (it - 2) * 3 + 3)
                        e.matmul(qkp[:, par * 512:(par + 1) * 512],
                                 s.kvbd[:, pair * 128:(pair + 1) * 128],
                                 q_sb[:, par * 512:(par + 1) * 512],
                                 start=True, stop=True,
                                 skip_group_check=True).then_inc(s.sem["pe"], 1)
                        s.cnt["pe"] += 1

            @blk.vector
            def _(e):
                for pair in range(16):
                    for ch in range(4):
                        it = pair * 4 + ch
                        par = it % 2
                        e.wait_ge(s.sem["pe"], pe0 + it * 3 + 1)
                        e.tensor_scalar(q_sb[:, par * 512:(par + 1) * 512],
                                        qp[:, par * 512:(par + 1) * 512],
                                        0.0, None, op0=ALU.max)
                        s.vsync(e)
                        e.tensor_scalar(tq_sb[:, :],
                                        q_sb[:, par * 512:(par + 1) * 512],
                                        s.ks_sb[:, pair:pair + 1], None,
                                        op0=ALU.mult).then_inc(s.sem["ve"], 1)
                        s.cnt["ve"] += 1
                        e.wait_ge(s.sem["pe"], pe0 + it * 3 + 2)
                        e.tensor_scalar(rec_sb[:, par * 512:(par + 1) * 512],
                                        npp[:, par * 512:(par + 1) * 512],
                                        1e-6, None, op0=ALU.max)
                        s.vsync(e)
                        e.reciprocal(rec_sb[:, par * 512:(par + 1) * 512],
                                     rec_sb[:, par * 512:(par + 1) * 512]) \
                            .then_inc(s.sem["ve"], 1)
                        s.cnt["ve"] += 1
                        s.vsync(e)
                        e.wait_ge(s.sem["pe"], pe0 + it * 3 + 3)
                        e.tensor_tensor(
                            s.kqt(pair, ch * 512, 512),
                            qkp[:, par * 512:(par + 1) * 512],
                            rec_sb[:, par * 512:(par + 1) * 512],
                            op=ALU.mult).then_inc(s.sem["ve"], 1)
                        s.cnt["ve"] += 1

        scopeB.__exit__(None, None, None)
        scopeB2 = nc.named_scope("l0a_B2"); scopeB2.__enter__()
        with (
            nc.sbuf_tensor("wo_s", [128, 2 * 16 * 128], BF16) as wo_s,
            nc.psum_tensor("op_ps", [128, 1024], F32) as op_ps,
            nc.Block("l0a_B2") as blk,
        ):
            WB = 16 * 128
            pe0 = s.cnt["pe"]
            ve0 = s.cnt["ve"]
            ld_tgt = {}

            @blk.gpsimd
            def _(e):
                for ot in range(16):
                    if ot >= 2:
                        e.wait_ge(s.sem["pe"], pe0 + (ot - 1) * 4)
                    q = ("da", "db")[ot % 2]
                    src = s.Wo[:, ot * 128:(ot + 1) * 128] \
                        .rearrange("(it p) o -> p it o", p=128)
                    dst = wo_s[:, (ot % 2) * WB:(ot % 2 + 1) * WB] \
                        .rearrange("p (it o) -> p it o", it=NDT)
                    s.dma(e, dst, src, q)
                    ld_tgt[ot] = (q, s.cnt[q])

            @blk.tensor
            def _(e):
                for ot in range(16):
                    e.wait_ge(s.sem[ld_tgt[ot][0]], ld_tgt[ot][1])
                    wb = (ot % 2) * WB
                    for ch in range(4):
                        it = ot * 4 + ch
                        pslice = op_ps[:, (it % 2) * 512:(it % 2 + 1) * 512]
                        if it >= 2:
                            e.wait_ge(s.sem["ve"], ve0 + it - 1)
                        for itile in range(16):
                            mm = e.matmul(
                                pslice, wo_s[:, wb + itile * 128: wb + (itile + 1) * 128],
                                s.kqt(itile, ch * 512, 512),
                                start=(itile == 0), stop=(itile == NDT - 1),
                                skip_group_check=True)
                        mm.then_inc(s.sem["pe"], 1)
                        s.cnt["pe"] += 1

            @blk.vector
            def _(e):
                for ot in range(16):
                    for ch in range(4):
                        it = ot * 4 + ch
                        pslice = op_ps[:, (it % 2) * 512:(it % 2 + 1) * 512]
                        e.wait_ge(s.sem["pe"], pe0 + it + 1)
                        xt = s.xt(ot, ch * 512, 512)
                        e.scalar_tensor_tensor(
                            xt, pslice, s.bo_sb[:, ot:ot + 1], xt,
                            op0=ALU.add, op1=ALU.add).then_inc(s.sem["ve"], 1)
                        s.cnt["ve"] += 1

        scopeB2.__exit__(None, None, None)
        with nc.named_scope("l0a_ln"):
            for ch in range(4):
                self.layernorm_chunk(ch * 512)

    def l0_ffn_wquant(self):
        nc, s = self.nc, self
        s.wu_m = s.ctx.enter_context(nc.sbuf_tensor("wu_m", [128, 1], F32))
        s.wd_m = s.ctx.enter_context(nc.sbuf_tensor("wd_m", [128, 1], F32))
        s.wu_r = s.ctx.enter_context(nc.sbuf_tensor("wu_r", [128, 1], F32))
        s.wd_r = s.ctx.enter_context(nc.sbuf_tensor("wd_r", [128, 1], F32))
        specs = [
            (s.wup, s.wup_q, NOC1, NDT * 128, s.wu_m, s.wu_r),
            (s.wdn, s.wdn_q, 16, NOC1 * 128, s.wd_m, s.wd_r),
        ]
        with (
            nc.sbuf_tensor("wqb", [128, 2 * 4096], F32) as wqb,
            nc.sbuf_tensor("wqt", [128, 2 * 4096], BF16) as wqt,
            nc.sbuf_tensor("wqi", [128, 4096], I32) as wqi,
            nc.sbuf_tensor("wqa", [128, 48], F32) as wqa,
            nc.sbuf_tensor("wqa_bf", [128, 1], BF16) as wqa_bf,
            nc.psum_tensor("wq_ps", [128, 1], F32) as wq_ps,
        ):
            with nc.Block("wq1") as blk:
                ve0 = s.cnt["ve"]
                ld_tgt = {}

                @blk.gpsimd
                def _(e):
                    i = 0
                    for (wh, _sc, n, epp, _m, _r) in specs:
                        for ch in range(n):
                            if i >= 2:
                                e.wait_ge(s.sem["ve"], ve0 + i - 1)
                            q = ("da", "db")[i % 2]
                            src = wh[ch].rearrange("(a p) o -> p a o", p=128)
                            dst = wqb[:, (i % 2) * 4096:(i % 2) * 4096 + epp] \
                                .rearrange("p (a o) -> p a o", o=128)
                            s.dma(e, dst, src, q)
                            ld_tgt[i] = (q, s.cnt[q])
                            i += 1

                @blk.vector
                def _(e):
                    i = 0
                    for (wh, _sc, n, epp, _m, _r) in specs:
                        for ch in range(n):
                            e.wait_ge(s.sem[ld_tgt[i][0]], ld_tgt[i][1])
                            e.reduce_sum(wqa[:, i:i + 1],
                                         wqb[:, (i % 2) * 4096:(i % 2) * 4096 + epp],
                                         axis=AX.X,
                                         apply_absolute_value=True).then_inc(
                                s.sem["ve"], 1)
                            s.cnt["ve"] += 1
                            i += 1

            for wi, (wh, _sc, n, epp, m_sb, r_sb) in enumerate(specs):
                lo = 0 if wi == 0 else NOC1
                cnt_el = float(n * epp * 128)
                with nc.Block(s.name("wqs")) as blk:
                    @blk.vector
                    def _(e):
                        e.reduce_sum(s.gup[:, :], wqa[:, lo:lo + n], axis=AX.X)
                        s.vsync(e)
                        e.tensor_copy(wqa_bf[:, :], s.gup[:, :]).then_inc(
                            s.sem["ve"], 1)
                        s.cnt["ve"] += 1

                    @blk.tensor
                    def _(e):
                        e.wait_ge(s.sem["ve"], s.tgt("ve"))
                        e.matmul(wq_ps[:, :], s.ones[:, :], wqa_bf[:, :],
                                 start=True, stop=True).then_inc(s.sem["pe"], 1)
                        s.cnt["pe"] += 1

                    @blk.vector
                    def _(e2):
                        e2.wait_ge(s.sem["pe"], s.tgt("pe"))
                        e2.tensor_scalar(m_sb[:, :], wq_ps[:, :], 1.0 / cnt_el,
                                         1e-5, op0=ALU.mult, op1=ALU.max)
                        s.vsync(e2)
                        e2.reciprocal(r_sb[:, :], m_sb[:, :])

            with nc.Block("wqg") as blk:
                @blk.vector
                def _(e):
                    e.tensor_mul(s.gup[:, :], s.wu_m[:, :], s.wu_m[:, :])
                    s.vsync(e)
                    e.tensor_mul(s.gup[:, :], s.gup[:, :], s.wd_m[:, :])
                    s.vsync(e)
                    e.tensor_scalar_mul(s.gup[:, :], s.gup[:, :], 1.0 / 131072.0)
                    s.vsync(e)

            with nc.Block("wq2") as blk:
                se0 = s.cnt["se"]
                ve0 = s.cnt["ve"]
                sa0 = s.cnt["sa"]
                sb0 = s.cnt["sb"]
                ld_tgt = {}

                @blk.gpsimd
                def _(e):
                    i = 0
                    for (wh, _sc, n, epp, _m, r_sb) in specs:
                        for ch in range(n):
                            if i >= 2:
                                e.wait_ge(s.sem["ve"], ve0 + i - 1)
                            q = ("da", "db")[i % 2]
                            src = wh[ch].rearrange("(a p) o -> p a o", p=128)
                            dst = wqb[:, (i % 2) * 4096:(i % 2) * 4096 + epp] \
                                .rearrange("p (a o) -> p a o", o=128)
                            s.dma(e, dst, src, q)
                            ld_tgt[i] = (q, s.cnt[q])
                            i += 1

                @blk.scalar
                def _(e):
                    i = 0
                    for (wh, _sc, n, epp, _m, r_sb) in specs:
                        for ch in range(n):
                            e.wait_ge(s.sem[ld_tgt[i][0]], ld_tgt[i][1])
                            if i >= 2:
                                e.wait_ge(s.sem["ve"], ve0 + i - 1)
                            sl = wqb[:, (i % 2) * 4096:(i % 2) * 4096 + epp]
                            e.activation(sl, sl, AF.Identity,
                                         scale=r_sb[:, :]).then_inc(s.sem["se"], 1)
                            s.cnt["se"] += 1
                            i += 1

                @blk.vector
                def _(e):
                    i = 0
                    for (wh, _sc, n, epp, _m, _r) in specs:
                        for ch in range(n):
                            e.wait_ge(s.sem["se"], se0 + i + 1)
                            if i >= 2:
                                par = i % 2
                                base = sa0 if par == 0 else sb0
                                e.wait_ge(s.sem[("sa", "sb")[par]],
                                          base + 16 * ((i - 2) // 2 + 1))
                            fsl = wqb[:, (i % 2) * 4096:(i % 2) * 4096 + epp]
                            tsl = wqt[:, (i % 2) * 4096:(i % 2) * 4096 + epp]
                            risl = wqi[:, 0:epp]
                            e.tensor_scalar(fsl, fsl, -1.0, 1.0,
                                            op0=ALU.max, op1=ALU.min)
                            s.vsync(e)
                            e.tensor_scalar(risl, fsl, 512.0, None, op0=ALU.add)
                            s.vsync(e)
                            e.scalar_tensor_tensor(fsl, risl, -1.0, fsl,
                                                   op0=ALU.mult, op1=ALU.add)
                            s.vsync(e)
                            e.tensor_scalar(fsl, fsl, -511.5, None, op0=ALU.is_ge)
                            s.vsync(e)
                            e.scalar_tensor_tensor(tsl, risl, -512.0, fsl,
                                                   op0=ALU.add, op1=ALU.add) \
                                .then_inc(s.sem["ve"], 1)
                            s.cnt["ve"] += 1
                            i += 1

                @blk.sync
                def _(e):
                    i = 0
                    for (wh, scr, n, epp, _m, _r) in specs:
                        for ch in range(n):
                            e.wait_ge(s.sem["ve"], ve0 + i + 1)
                            dst = scr[ch].rearrange("(a p) o -> p a o", p=128)
                            src = wqt[:, (i % 2) * 4096:(i % 2) * 4096 + epp] \
                                .rearrange("p (a o) -> p a o", o=128)
                            s.dma(e, dst, src, ("sa", "sb")[i % 2])
                            i += 1
                    e.wait_ge(s.sem["sa"], s.tgt("sa"))
                    e.wait_ge(s.sem["sb"], s.tgt("sb"))

    def _ffn_sl(self, i):
        return self.kq[:, i * 512:(i + 1) * 512]

    def _fwht_butterflies(self, slots, spares, nbits):
        nc, s = self.nc, self
        spc = list(spares)
        with nc.Block(s.name("bfly")) as blk:
            @blk.vector
            def _(e):
                n = len(slots)
                for lvl in range(nbits):
                    st = 1 << lvl
                    pairs = [(a, a | st) for a in range(n) if not (a & st)]
                    for gi in range(0, len(pairs), 4):
                        if lvl > 0 or gi > 0:
                            s.vsync(e)
                        freed = []
                        for (a, b) in pairs[gi:gi + 4]:
                            sa, sb_ = slots[a], slots[b]
                            sp1 = spc.pop(0)
                            sp2 = spc.pop(0)
                            e.tensor_add(s._ffn_sl(sp1), s._ffn_sl(sa),
                                         s._ffn_sl(sb_))
                            e.tensor_sub(s._ffn_sl(sp2), s._ffn_sl(sa),
                                         s._ffn_sl(sb_))
                            slots[a], slots[b] = sp1, sp2
                            freed += [sa, sb_]
                        spc.extend(freed)
        return slots, spc

    def _ffn_h128(self, slots, ps, src_of=None):
        nc, s = self.nc, self
        n = len(slots)
        with nc.Block(s.name("h128")) as blk:
            pe0 = s.cnt["pe"]
            ve0 = s.cnt["ve"]

            @blk.tensor
            def _(e):
                for i in range(n):
                    if i >= 2:
                        e.wait_ge(s.sem["ve"], ve0 + i - 1)
                    src = src_of(i) if src_of else s._ffn_sl(slots[i])
                    e.matmul(ps[:, (i % 2) * 512:(i % 2 + 1) * 512],
                             s.h128[:, :], src, start=True, stop=True,
                             skip_group_check=True).then_inc(s.sem["pe"], 1)
                    s.cnt["pe"] += 1

            @blk.vector
            def _(e):
                for i in range(n):
                    e.wait_ge(s.sem["pe"], pe0 + i + 1)
                    e.tensor_copy(s._ffn_sl(slots[i]),
                                  ps[:, (i % 2) * 512:(i % 2 + 1) * 512]) \
                        .then_inc(s.sem["ve"], 1)
                    s.cnt["ve"] += 1

    def _ffn_quant(self, slots, pt, tm_sb, tmpf):
        nc, s = self.nc, self
        n = len(slots)
        nw = (n + 15) // 16
        ng = (n * 128) // 512
        nh = (ng + 3) // 4
        nhalves = (ng + nh - 1) // nh
        with (
            nc.sbuf_tensor(s.name("qam"), [128, 2], F32) as am,
            nc.sbuf_tensor(s.name("qrb"), [128, 2048], I32) as rb,
            nc.sbuf_tensor(s.name("qcb"), [128, 2048], BF16) as cb,
            nc.sbuf_tensor(s.name("qc2"), [128, 2048], BF16) as c2,
            nc.Block(s.name("qnt")) as blk,
        ):
            pe0 = s.cnt["pe"]
            ve0 = s.cnt["ve"]
            se0 = s.cnt["se"]
            ve_sub = lambda sub: ve0 + sub * (2 * nw + 1 + nhalves)
            pe_sub = lambda sub: pe0 + sub * (2 * nw)
            se_sub = lambda sub: se0 + sub * ng

            @blk.scalar
            def _(e):
                for sub in range(4):
                    for hv in range(nhalves):
                        g0, g1 = hv * nh, min(ng, (hv + 1) * nh)
                        for g in range(g0, g1):
                            e.wait_ge(s.sem["ve"], ve_sub(sub) + nw + 1 + hv)
                            sl = tm_sb[:, g * 512:(g + 1) * 512]
                            e.activation(tmpf[:, (g - g0) * 512:(g - g0 + 1) * 512],
                                         sl, AF.Identity, bias=s.mg[:, :],
                                         scale=am[:, 0:1]).then_inc(s.sem["se"], 1)
                            s.cnt["se"] += 1

            @blk.tensor
            def _(e):
                for sub in range(4):
                    for w in range(nw):
                        if sub > 0 or w > 0:
                            e.wait_ge(s.sem["ve"], ve_sub(sub) + w)
                        lo = w * 16
                        hi = min(n, lo + 16)
                        for i2 in range(lo, hi):
                            mm = e.transpose(
                                pt[:, (i2 - lo) * 128:(i2 - lo + 1) * 128],
                                s._ffn_sl(slots[i2])[:, sub * 128:(sub + 1) * 128],
                                s.ident[:, :])
                        mm.then_inc(s.sem["pe"], 1)
                        s.cnt["pe"] += 1
                    for w in range(nw):
                        e.wait_ge(s.sem["ve"], ve_sub(sub) + nw + 1 + nhalves + w)
                        lo = w * 16
                        hi = min(n, lo + 16)
                        for i2 in range(lo, hi):
                            mm = e.transpose(
                                pt[:, (i2 - lo) * 128:(i2 - lo + 1) * 128],
                                tm_sb[:, i2 * 128:(i2 + 1) * 128],
                                s.ident[:, :])
                        mm.then_inc(s.sem["pe"], 1)
                        s.cnt["pe"] += 1

            @blk.vector
            def _(e):
                for sub in range(4):
                    for w in range(nw):
                        e.wait_ge(s.sem["pe"], pe_sub(sub) + w + 1)
                        lo = w * 16
                        hi = min(n, lo + 16)
                        e.tensor_copy(tm_sb[:, lo * 128:hi * 128],
                                      pt[:, 0:(hi - lo) * 128])
                        if w == 0:
                            e.reduce_max(am[:, 0:1], pt[:, 0:(hi - lo) * 128],
                                         axis=AX.X, apply_absolute_value=True)
                        else:
                            e.reduce_max(am[:, 1:2], pt[:, 0:(hi - lo) * 128],
                                         axis=AX.X, apply_absolute_value=True)
                            s.vsync(e)
                            e.tensor_max(am[:, 0:1], am[:, 0:1], am[:, 1:2])
                        e.engine_nop().then_inc(s.sem["ve"], 1)
                        s.cnt["ve"] += 1
                    s.vsync(e)
                    e.tensor_scalar(am[:, 0:1], am[:, 0:1], 1e-5, None,
                                    op0=ALU.max)
                    s.vsync(e)
                    e.tensor_scalar(am[:, 1:2], am[:, 0:1], 1.0 / 127.0, None,
                                    op0=ALU.mult)
                    s.vsync(e)
                    e.reciprocal(am[:, 0:1], am[:, 0:1])
                    s.vsync(e)
                    e.tensor_scalar(am[:, 0:1], am[:, 0:1], 127.0, None,
                                    op0=ALU.mult)
                    s.vsync(e)
                    e.engine_nop().then_inc(s.sem["ve"], 1)
                    s.cnt["ve"] += 1
                    for hv in range(nhalves):
                        g0, g1 = hv * nh, min(ng, (hv + 1) * nh)
                        w512 = (g1 - g0) * 512
                        e.wait_ge(s.sem["se"], se_sub(sub) + g1)
                        e.tensor_copy(rb[:, 0:w512], tmpf[:, 0:w512])
                        s.vsync(e)
                        e.tensor_tensor(tmpf[:, 0:w512], tmpf[:, 0:w512],
                                        rb[:, 0:w512], op=ALU.subtract)
                        s.vsync(e)
                        e.tensor_scalar(cb[:, 0:w512], tmpf[:, 0:w512], 0.5,
                                        None, op0=ALU.is_ge)
                        s.vsync(e)
                        e.tensor_scalar(c2[:, 0:w512], tmpf[:, 0:w512], 1.5,
                                        None, op0=ALU.is_ge)
                        s.vsync(e)
                        e.tensor_tensor(cb[:, 0:w512], cb[:, 0:w512],
                                        c2[:, 0:w512], op=ALU.add)
                        s.vsync(e)
                        e.tensor_scalar(c2[:, 0:w512], tmpf[:, 0:w512], -0.5,
                                        None, op0=ALU.is_lt)
                        s.vsync(e)
                        e.tensor_tensor(cb[:, 0:w512], cb[:, 0:w512],
                                        c2[:, 0:w512], op=ALU.subtract)
                        s.vsync(e)
                        e.tensor_scalar(c2[:, 0:w512], tmpf[:, 0:w512], -1.5,
                                        None, op0=ALU.is_lt)
                        s.vsync(e)
                        e.tensor_tensor(cb[:, 0:w512], cb[:, 0:w512],
                                        c2[:, 0:w512], op=ALU.subtract)
                        s.vsync(e)
                        e.tensor_scalar(c2[:, 0:w512], tmpf[:, 0:w512], 2.5,
                                        None, op0=ALU.is_ge)
                        s.vsync(e)
                        e.tensor_tensor(cb[:, 0:w512], cb[:, 0:w512],
                                        c2[:, 0:w512], op=ALU.add)
                        s.vsync(e)
                        e.tensor_scalar(c2[:, 0:w512], tmpf[:, 0:w512], -2.5,
                                        None, op0=ALU.is_lt)
                        s.vsync(e)
                        e.tensor_tensor(cb[:, 0:w512], cb[:, 0:w512],
                                        c2[:, 0:w512], op=ALU.subtract)
                        s.vsync(e)
                        e.tensor_tensor(tmpf[:, 0:w512], rb[:, 0:w512],
                                        cb[:, 0:w512], op=ALU.add)
                        s.vsync(e)
                        for g in range(g0, g1):
                            sl = tm_sb[:, g * 512:(g + 1) * 512]
                            e.tensor_scalar(sl,
                                            tmpf[:, (g - g0) * 512:(g - g0 + 1) * 512],
                                            512.0, am[:, 1:2],
                                            op0=ALU.subtract, op1=ALU.mult)
                        e.engine_nop().then_inc(s.sem["ve"], 1)
                        s.cnt["ve"] += 1
                    for w in range(nw):
                        e.wait_ge(s.sem["pe"], pe_sub(sub) + nw + w + 1)
                        lo = w * 16
                        hi = min(n, lo + 16)
                        for i2 in range(lo, hi):
                            e.tensor_copy(
                                s._ffn_sl(slots[i2])[:, sub * 128:(sub + 1) * 128],
                                pt[:, (i2 - lo) * 128:(i2 - lo + 1) * 128])
                        e.engine_nop().then_inc(s.sem["ve"], 1)
                        s.cnt["ve"] += 1

    def _dbg_tm(self, slots, pt, tm_sb):
        nc, s = self.nc, self
        n = len(slots)
        nw = (n + 15) // 16
        with (
            nc.sbuf_tensor(s.name("qam"), [128, 2], F32) as am,
            nc.Block(s.name("dbgtm")) as blk,
        ):
            pe0 = s.cnt["pe"]
            ve0 = s.cnt["ve"]

            @blk.tensor
            def _(e):
                for w in range(nw):
                    if w > 0:
                        e.wait_ge(s.sem["ve"], ve0 + w)
                    lo = w * 16
                    hi = min(n, lo + 16)
                    for i in range(lo, hi):
                        mm = e.transpose(
                            pt[:, (i - lo) * 128:(i - lo + 1) * 128],
                            s._ffn_sl(slots[i])[:, 0:128],
                            s.ident[:, :])
                    mm.then_inc(s.sem["pe"], 1)
                    s.cnt["pe"] += 1

            @blk.vector
            def _(e):
                for w in range(nw):
                    e.wait_ge(s.sem["pe"], pe0 + w + 1)
                    lo = w * 16
                    hi = min(n, lo + 16)
                    e.tensor_copy(tm_sb[:, lo * 128:hi * 128],
                                  pt[:, 0:(hi - lo) * 128])
                    if w == 0:
                        e.reduce_max(am[:, 0:1], pt[:, 0:(hi - lo) * 128],
                                     axis=AX.X, apply_absolute_value=True)
                    else:
                        e.reduce_max(am[:, 1:2], pt[:, 0:(hi - lo) * 128],
                                     axis=AX.X, apply_absolute_value=True)
                        e.tensor_max(am[:, 0:1], am[:, 0:1], am[:, 1:2])
                    e.engine_nop().then_inc(s.sem["ve"], 1)
                    s.cnt["ve"] += 1
                if self.upto == "ffn_tm2":
                    tmpf = self._dbg_tmpf
                    e.tensor_scalar(am[:, 0:1], am[:, 0:1], 1e-5, None,
                                    op0=ALU.max)
                    e.tensor_scalar(am[:, 1:2], am[:, 0:1], 1.0 / 127.0, None,
                                    op0=ALU.mult)
                    e.reciprocal(am[:, 0:1], am[:, 0:1])
                    e.tensor_scalar(am[:, 0:1], am[:, 0:1], 127.0, None,
                                    op0=ALU.mult)
                    for g in range((n * 128) // 512):
                        sl = tm_sb[:, g * 512:(g + 1) * 512]
                        e.tensor_scalar(tmpf[:, :], sl, am[:, 0:1], MAGIC,
                                        op0=ALU.mult, op1=ALU.add)
                        e.tensor_scalar(sl, tmpf[:, :], MAGIC, am[:, 1:2],
                                        op0=ALU.subtract, op1=ALU.mult)
                e.tensor_copy(s.kq[:, 27000:27000 + n * 128], tm_sb[:, 0:n * 128])
                e.tensor_copy(s.kq[:, 31500:31502], am[:, 0:2])

    def l0_ffn(self):
        nc, s = self.nc, self
        if s.kq is None:
            s.kq = s.ctx.enter_context(nc.sbuf_tensor("kq", [128, 16 * T], BF16))
        if self.upto.startswith("ffn_"):
            with nc.Block(s.name("dbg0")) as blk:
                @blk.gpsimd
                def _(e):
                    e.memset(s.kq[:, :], 0)
        self.l0_ffn_wquant()
        if self.upto == "ffn_w":
            with nc.Block("dbgw") as blk:
                @blk.vector
                def _(e):
                    e.tensor_copy(s.kq[:, 27000:27001], s.wu_m[:, :])
                    e.tensor_copy(s.kq[:, 27001:27002], s.wd_m[:, :])
                    e.tensor_copy(s.kq[:, 27002:27003], s.wu_r[:, :])
                    e.tensor_copy(s.kq[:, 27003:27004], s.wd_r[:, :])
                    e.tensor_copy(s.kq[:, 27004:27005], s.gup[:, :])

                @blk.sync
                def _(e):
                    dst = s.kq[:, 27856:27856 + 2048] \
                        .rearrange("p (a o) -> p a o", o=128)
                    src2 = s.wup_q[0].rearrange("(a p) o -> p a o", p=128)
                    s.dma(e, dst, src2, "qs")

            with nc.Block("dbgw2") as blk:
                @blk.vector
                def _(e):
                    e.wait_ge(s.sem["qs"], s.tgt("qs"))
            return
        FW = list(range(24))
        HS = list(range(24, 64))
        with (
            nc.sbuf_tensor("f_wb", [128, 2 * NDT * 128], BF16) as f_wb,
            nc.sbuf_tensor("f_db", [128, 2 * NOC1 * 128], BF16) as f_db,
            nc.sbuf_tensor("f_tmp", [128, 2048], F32) as f_tmp,
            nc.sbuf_tensor("f_rl", [128, 1024], BF16) as f_rl,
            nc.sbuf_tensor("f_tm", [128, NOC1 * 128], BF16) as f_tm,
            nc.psum_tensor("f_pa", [128, 1024], F32) as f_pa,
            nc.psum_tensor("f_pt", [128, 2048], BF16) as f_pt,
        ):
            for cc in range(4):
                c0 = cc * 512
                fw = FW[:16]
                sp = FW[16:24]
                self._ffn_h128(fw, f_pa, src_of=lambda i: s.xt(i, c0, 512))
                fw, sp = self._fwht_butterflies(fw, sp, 4)
                if self.upto in ("ffn_tm1", "ffn_tm2") and cc == 0:
                    self._dbg_tmpf = f_tmp
                    self._dbg_tm(fw, f_pt, f_tm)
                    self.dbg_slots = list(fw)
                    return
                if self.upto == "ffn_fwht1" and cc == 0:
                    self.dbg_slots = list(fw)
                    return
                self._ffn_quant(fw, f_pt, f_tm, f_tmp)
                if self.upto == "ffn_q1" and cc == 0:
                    self.dbg_slots = list(fw)
                    return

                hs = HS[:32]
                hsp = HS[32:40]
                with nc.Block(s.name("fup")) as blk:
                    pe0 = s.cnt["pe"]
                    ve0 = s.cnt["ve"]
                    ld_tgt = {}

                    @blk.scalar
                    def _(e):
                        for oc in range(NOC1):
                            if oc >= 2:
                                e.wait_ge(s.sem["pe"], pe0 + oc - 1)
                            q = ("dc", "dd")[oc % 2]
                            src = s.wup_q[oc].rearrange("(a p) o -> p a o", p=128)
                            dst = f_wb[:, (oc % 2) * 2048:(oc % 2 + 1) * 2048] \
                                .rearrange("p (a o) -> p a o", o=128)
                            s.dma(e, dst, src, q)
                            ld_tgt[oc] = (q, s.cnt[q])

                    @blk.tensor
                    def _(e):
                        for oc in range(NOC1):
                            e.wait_ge(s.sem[ld_tgt[oc][0]], ld_tgt[oc][1])
                            if oc >= 2:
                                e.wait_ge(s.sem["ve"], ve0 + oc - 1)
                            wb = (oc % 2) * 2048
                            for dt in range(NDT):
                                mm = e.matmul(
                                    f_pa[:, (oc % 2) * 512:(oc % 2 + 1) * 512],
                                    f_wb[:, wb + dt * 128: wb + (dt + 1) * 128],
                                    s._ffn_sl(fw[dt]),
                                    start=(dt == 0), stop=(dt == NDT - 1),
                                    skip_group_check=True)
                            mm.then_inc(s.sem["pe"], 1)
                            s.cnt["pe"] += 1

                    @blk.vector
                    def _(e):
                        for oc in range(NOC1):
                            e.wait_ge(s.sem["pe"], pe0 + oc + 1)
                            psl = f_pa[:, (oc % 2) * 512:(oc % 2 + 1) * 512]
                            rl = f_rl[:, (oc % 2) * 512:(oc % 2 + 1) * 512]
                            e.tensor_scalar(rl, psl, 0.0, None, op0=ALU.max)
                            s.vsync(e)
                            e.tensor_tensor(s._ffn_sl(hs[oc]), rl,
                                            rl, op=ALU.mult).then_inc(
                                s.sem["ve"], 1)
                            s.cnt["ve"] += 1

                if self.upto == "ffn_up" and cc == 0:
                    self.dbg_slots = list(hs)
                    return
                self._ffn_h128(hs, f_pa)
                hs, hsp = self._fwht_butterflies(hs, hsp, 5)
                self._ffn_quant(hs, f_pt, f_tm, f_tmp)

                with nc.Block(s.name("fdn")) as blk:
                    pe0 = s.cnt["pe"]
                    ve0 = s.cnt["ve"]
                    ld_tgt = {}

                    @blk.scalar
                    def _(e):
                        for ct in range(16):
                            if ct >= 2:
                                e.wait_ge(s.sem["pe"], pe0 + ct - 1)
                            q = ("dc", "dd")[ct % 2]
                            src = s.wdn_q[ct].rearrange("(a p) o -> p a o", p=128)
                            dst = f_db[:, (ct % 2) * 4096:(ct % 2 + 1) * 4096] \
                                .rearrange("p (a o) -> p a o", o=128)
                            s.dma(e, dst, src, q)
                            ld_tgt[ct] = (q, s.cnt[q])

                    @blk.tensor
                    def _(e):
                        for ct in range(16):
                            e.wait_ge(s.sem[ld_tgt[ct][0]], ld_tgt[ct][1])
                            if ct >= 2:
                                e.wait_ge(s.sem["ve"], ve0 + ct - 1)
                            wb = (ct % 2) * 4096
                            for ot in range(NOC1):
                                mm = e.matmul(
                                    f_pa[:, (ct % 2) * 512:(ct % 2 + 1) * 512],
                                    f_db[:, wb + ot * 128: wb + (ot + 1) * 128],
                                    s._ffn_sl(hs[ot]),
                                    start=(ot == 0), stop=(ot == NOC1 - 1),
                                    skip_group_check=True)
                            mm.then_inc(s.sem["pe"], 1)
                            s.cnt["pe"] += 1

                    @blk.vector
                    def _(e):
                        for ct in range(16):
                            e.wait_ge(s.sem["pe"], pe0 + ct + 1)
                            psl = f_pa[:, (ct % 2) * 512:(ct % 2 + 1) * 512]
                            xt = s.xt(ct, c0, 512)
                            e.scalar_tensor_tensor(
                                xt, psl, s.gup[:, :], xt,
                                op0=ALU.mult, op1=ALU.add).then_inc(s.sem["ve"], 1)
                            s.cnt["ve"] += 1

        for cc in range(4):
            self.layernorm_chunk(cc * 512)

    def _l1_rotary(self, tgt, c0, ncols, qsh):
        nc, s = self.nc, self
        for hg in range(2):
            with nc.Block(s.name("rot")) as blk:
                qs0 = s.cnt["qs"]

                @blk.sync
                def _(e):
                    for h8 in range(8):
                        h = hg * 8 + h8
                        sl = slice(h * T + c0, h * T + c0 + ncols)
                        d = slice(h8 * ncols, (h8 + 1) * ncols)
                        s.dma(e, qsh[0:16, d], tgt[16:32, sl], "qs")
                        s.dma(e, qsh[16:32, d], tgt[0:16, sl], "qs")

                @blk.vector
                def _(e):
                    e.wait_ge(s.sem["qs"], s.tgt("qs"))
                    sview = s.rotS[:, c0:c0 + ncols].unsqueeze(1) \
                        .broadcast_to([32, 8, ncols])
                    qview = qsh[:, 0:8 * ncols].rearrange("p (h t) -> p h t", h=8)
                    e.tensor_tensor(qview, qview, sview, op=ALU.mult)
                    for h8 in range(8):
                        h = hg * 8 + h8
                        sl = slice(h * T + c0, h * T + c0 + ncols)
                        rot = tgt[0:32, sl]
                        e.tensor_tensor(rot, rot, s.rotC[:, c0:c0 + ncols],
                                        op=ALU.mult)
                    s.vsync(e)
                    for h8 in range(8):
                        h = hg * 8 + h8
                        sl = slice(h * T + c0, h * T + c0 + ncols)
                        rot = tgt[0:32, sl]
                        e.tensor_tensor(rot, rot,
                                        qsh[:, h8 * ncols:(h8 + 1) * ncols],
                                        op=ALU.add)

    def _l1_proj_fm(self, wcol0, scale, store_scr):
        nc, s = self.nc, self
        with (
            nc.sbuf_tensor(s.name("pw"), [128, 2 * 16 * 128], BF16) as pw,
            nc.psum_tensor(s.name("pp"), [128, 4096], F32) as pp,
            nc.Block(s.name("proj")) as blk,
        ):
            WB = 16 * 128
            pe0 = s.cnt["pe"]
            ve0 = s.cnt["ve"]
            ld_tgt = {}

            @blk.gpsimd
            def _(e):
                for h in range(16):
                    if h >= 2:
                        e.wait_ge(s.sem["pe"], pe0 + h - 1)
                    q = ("da", "db")[h % 2]
                    src = s.wqkv[:, wcol0 + h * 128: wcol0 + (h + 1) * 128] \
                        .rearrange("(dt p) i -> p dt i", p=128)
                    dst = pw[:, (h % 2) * WB:(h % 2 + 1) * WB] \
                        .rearrange("p (dt i) -> p dt i", dt=NDT)
                    s.dma(e, dst, src, q)
                    ld_tgt[h] = (q, s.cnt[q])

            @blk.tensor
            def _(e):
                for h in range(16):
                    e.wait_ge(s.sem[ld_tgt[h][0]], ld_tgt[h][1])
                    if h >= 2:
                        e.wait_ge(s.sem["ve"], ve0 + h - 1)
                    wb = (h % 2) * WB
                    pb = (h % 2) * 2048
                    for dt in range(NDT):
                        for jc in range(4):
                            mm = e.matmul(
                                pp[:, pb + jc * 512: pb + (jc + 1) * 512],
                                pw[:, wb + dt * 128: wb + (dt + 1) * 128],
                                s.xt(dt, jc * 512, 512),
                                start=(dt == 0), stop=(dt == NDT - 1),
                                skip_group_check=True)
                    mm.then_inc(s.sem["pe"], 1)
                    s.cnt["pe"] += 1

            @blk.vector
            def _(e):
                for h in range(16):
                    e.wait_ge(s.sem["pe"], pe0 + h + 1)
                    pb = (h % 2) * 2048
                    if scale is None:
                        e.tensor_copy(s.kq[:, h * T:(h + 1) * T],
                                      pp[:, pb:pb + 2048])
                    else:
                        e.tensor_scalar(s.kq[:, h * T:(h + 1) * T],
                                        pp[:, pb:pb + 2048],
                                        s.qscl[:, 0:1], None, op0=ALU.mult)
                    e.engine_nop().then_inc(s.sem["ve"], 1)
                    s.cnt["ve"] += 1

        if not self.upto.endswith("_norot"):
            with nc.sbuf_tensor(s.name("qsh"), [32, 8 * T], BF16) as qsh:
                self._l1_rotary(s.kq, 0, T, qsh)

        if store_scr and not self.upto.startswith("l1x_kraw"):
            with nc.Block(s.name("kst")) as blk:
                @blk.sync
                def _(e):
                    for h in range(16):
                        s.dma(e, s.k_scr[h], s.kq[:, h * T:(h + 1) * T], "qs")
                    e.wait_ge(s.sem["qs"], s.tgt("qs"))

    def l1_attn(self):
        nc, s = self.nc, self
        QSC = 1.0 / np.sqrt(128.0)
        rot_ctx = ExitStack()
        s.rotC = rot_ctx.enter_context(nc.sbuf_tensor("rotC", [32, T], F32))
        s.rotS = rot_ctx.enter_context(nc.sbuf_tensor("rotS", [32, T], F32))
        s.mask4 = rot_ctx.enter_context(nc.sbuf_tensor("mask4", [128, 4 * 512], BF16))
        lo_ctx = ExitStack()
        rotCl = lo_ctx.enter_context(nc.sbuf_tensor("rotCl", [32, T], F32))
        rotSl = lo_ctx.enter_context(nc.sbuf_tensor("rotSl", [32, T], F32))
        with nc.Block("l1ld") as blk:
            @blk.sync
            def _(e):
                s.dma(e, s.rotC[:, :], s.rotC_h[:, :], "qs")
                s.dma(e, s.rotS[:, :], s.rotS_h[:, :], "qs")
                s.dma(e, rotCl[:, :], s.rotCl_h[:, :], "qs")
                s.dma(e, rotSl[:, :], s.rotSl_h[:, :], "qs")
                s.dma(e, s.mask4[:, :], s.mask_h[:, :], "qs")

            @blk.vector
            def _(e):
                e.wait_ge(s.sem["qs"], s.tgt("qs"))
                e.tensor_tensor(s.rotC[:, :], s.rotC[:, :], rotCl[:, :],
                                op=ALU.add)
                e.tensor_tensor(s.rotS[:, :], s.rotS[:, :], rotSl[:, :],
                                op=ALU.add)
        lo_ctx.close()

        self._l1_proj_fm(D, None, True)
        if self.upto.startswith("l1x_k"):
            return
        self._l1_proj_fm(0, 'qscl', False)
        if self.upto == "l1x_q":
            return

        with (
            nc.sbuf_tensor("vw", [128, 2 * 16 * 512], BF16) as vw,
            nc.sbuf_tensor("vtile", [128, 2 * 512], BF16) as vtile,
            nc.psum_tensor("vps", [128, 1024], F32) as vps,
            nc.Block("l1v") as blk,
        ):
            VB = 16 * 512
            pe0 = s.cnt["pe"]
            ve0 = s.cnt["ve"]
            sa0 = s.cnt["sa"]
            sb0 = s.cnt["sb"]
            ld_tgt = {}

            @blk.gpsimd
            def _(e):
                for ic in range(4):
                    if ic >= 2:
                        e.wait_ge(s.sem["pe"], pe0 + (ic - 1) * 16)
                    q = ("da", "db")[ic % 2]
                    src = s.wqkv[:, 4096 + ic * 512: 4096 + (ic + 1) * 512] \
                        .rearrange("(dt p) i -> p dt i", p=128)
                    dst = vw[:, (ic % 2) * VB:(ic % 2 + 1) * VB] \
                        .rearrange("p (dt i) -> p dt i", dt=NDT)
                    s.dma(e, dst, src, q)
                    ld_tgt[ic] = (q, s.cnt[q])
                for dt in range(NDT):
                    s.dma(e, s.x_scr[dt], s.xs[:, dt * T:(dt + 1) * T], "qg")

            @blk.tensor
            def _(e):
                for ic in range(4):
                    e.wait_ge(s.sem[ld_tgt[ic][0]], ld_tgt[ic][1])
                    wb = (ic % 2) * VB
                    for tt in range(16):
                        it = ic * 16 + tt
                        if it >= 2:
                            e.wait_ge(s.sem["ve"], ve0 + it - 1)
                        for dt in range(NDT):
                            mm = e.matmul(
                                vps[:, (it % 2) * 512:(it % 2 + 1) * 512],
                                s.xt(dt, tt * 128, 128),
                                vw[:, wb + dt * 512: wb + (dt + 1) * 512],
                                start=(dt == 0), stop=(dt == NDT - 1),
                                skip_group_check=True)
                        mm.then_inc(s.sem["pe"], 1)
                        s.cnt["pe"] += 1

            @blk.vector
            def _(e):
                for ic in range(4):
                    for tt in range(16):
                        it = ic * 16 + tt
                        e.wait_ge(s.sem["pe"], pe0 + it + 1)
                        if it >= 2:
                            par = it % 2
                            base = sa0 if par == 0 else sb0
                            e.wait_ge(s.sem[("sa", "sb")[par]],
                                      base + 16 * ((it - 2) // 2 + 1))
                        e.tensor_copy(vtile[:, (it % 2) * 512:(it % 2 + 1) * 512],
                                      vps[:, (it % 2) * 512:(it % 2 + 1) * 512]) \
                            .then_inc(s.sem["ve"], 1)
                        s.cnt["ve"] += 1

            @blk.sync
            def _(e):
                for ic in range(4):
                    for tt in range(16):
                        it = ic * 16 + tt
                        e.wait_ge(s.sem["ve"], ve0 + it + 1)
                        s.dma(e, s.v_scr[tt][:, ic * 512:(ic + 1) * 512],
                              vtile[:, (it % 2) * 512:(it % 2 + 1) * 512],
                              ("sa", "sb")[it % 2])
                e.wait_ge(s.sem["sa"], s.tgt("sa"))
                e.wait_ge(s.sem["sb"], s.tgt("sb"))
                e.wait_ge(s.sem["qg"], s.tgt("qg"))

        if self.upto == "l1x_v":
            return
        kbuf = lambda h8, j: s.xs[:, h8 * T + j * 128: h8 * T + (j + 1) * 128]
        vbuf = lambda j, h8: s.xs[:, 16 * 1024 + j * 1024 + h8 * 128:
                                  16 * 1024 + j * 1024 + (h8 + 1) * 128]
        with (
            nc.sbuf_tensor("pbuf", [128, 2 * 16 * 512], BF16) as pbuf,
            nc.sbuf_tensor("rec_c", [128, 2 * 512], F32) as rec_c,
            nc.psum_tensor("sc_ps", [128, 2048], F32) as sc_ps,
            nc.psum_tensor("sum_ps", [128, 512], F32) as sum_ps,
            nc.psum_tensor("av_ps", [128, 512], F32) as av_ps,
            nc.Block("l1att") as blk,
        ):
            PB = 16 * 512
            pe0 = s.cnt["pe"]
            ve0 = s.cnt["ve"]
            se0 = s.cnt["se"]
            qs0 = s.cnt["qs"]
            qc0 = s.cnt["qc"]
            kload = []
            vload = []

            @blk.sync
            def _(e):
                for c in range(4):
                    lim = 4 * (c + 1)
                    for hg in range(2):
                        idx = c * 2 + hg
                        if idx >= 1:
                            e.wait_ge(s.sem["pe"], pe0 + sum(
                                (4 * (cc + 1) + 3) // 4 + 2
                                for cc in range(4) for gg in range(2)
                                for _h in range(8)
                                if cc * 2 + gg < idx))
                        for h8 in range(8):
                            h = hg * 8 + h8
                            s.dma(e, s.xs[:, h8 * T: h8 * T + lim * 128],
                                  s.k_scr[h][:, 0:lim * 128], "qs")
                        kload.append(s.tgt("qs"))

            @blk.scalar
            def _(e):
                for c in range(4):
                    lim = 4 * (c + 1)
                    nwave = (lim + 3) // 4
                    for hg in range(2):
                        idx = c * 2 + hg
                        if idx >= 1:
                            e.wait_ge(s.sem["pe"], pe0 + sum(
                                (4 * (cc + 1) + 3) // 4 + 2
                                for cc in range(4) for gg in range(2)
                                for _h in range(8)
                                if cc * 2 + gg < idx))
                        for j in range(lim):
                            src = s.v_scr[j][:, hg * 1024:(hg + 1) * 1024]
                            dst = s.xs[:, 16 * 1024 + j * 1024:
                                       16 * 1024 + (j + 1) * 1024]
                            s.dma(e, dst, src, "qc")
                        vload.append(s.tgt("qc"))
                        for h8 in range(8):
                            hi = idx * 8 + h8
                            if hi >= 2:
                                pc, pg, ph = self._l1_head_seq(hi - 2)
                                e.wait_ge(s.sem["pe"], pe0 +
                                          self._l1_pe_before(pc, pg, ph) +
                                          (pc + 1) + 2)
                            for w in range(nwave):
                                e.wait_ge(s.sem["pe"],
                                          pe0 + self._l1_pe_before(c, hg, h8) + w + 1)
                                j0 = w * 4
                                j1 = min(lim, j0 + 4)
                                e.activation(
                                    pbuf[:, (h8 % 2) * PB + j0 * 512:
                                         (h8 % 2) * PB + j1 * 512],
                                    sc_ps[:, 0:(j1 - j0) * 512],
                                    AF.Exp).then_inc(s.sem["se"], 1)
                                s.cnt["se"] += 1

            @blk.tensor
            def _(e):
                for c in range(4):
                    lim = 4 * (c + 1)
                    nwave = (lim + 3) // 4
                    for hg in range(2):
                        idx = c * 2 + hg
                        e.wait_ge(s.sem["qs"], kload[idx])
                        e.wait_ge(s.sem["qc"], vload[idx])
                        for h8 in range(8):
                            h = hg * 8 + h8
                            for w in range(nwave):
                                j0 = w * 4
                                j1 = min(lim, j0 + 4)
                                if not (idx == 0 and h8 == 0 and w == 0):
                                    e.wait_ge(s.sem["se"], se0 +
                                              self._l1_se_before(c, hg, h8) + w)
                                for j in range(j0, j1):
                                    mm = e.matmul(
                                        sc_ps[:, (j - j0) * 512:(j - j0 + 1) * 512],
                                        kbuf(h8, j),
                                        s.kq[:, h * T + c * 512: h * T + c * 512 + 512],
                                        start=True, stop=True,
                                        skip_group_check=True)
                                mm.then_inc(s.sem["pe"], 1)
                                s.cnt["pe"] += 1
                            e.wait_ge(s.sem["ve"], ve0 + idx * 16 + h8 * 2 + 1)
                            for j in range(lim):
                                mm = e.matmul(
                                    sum_ps[:, :], s.ones[:, :],
                                    pbuf[:, (h8 % 2) * PB + j * 512:
                                         (h8 % 2) * PB + (j + 1) * 512],
                                    start=(j == 0), stop=(j == lim - 1),
                                    skip_group_check=True)
                            mm.then_inc(s.sem["pe"], 1)
                            s.cnt["pe"] += 1
                            for j in range(lim):
                                mm = e.matmul(
                                    av_ps[:, :], vbuf(j, h8),
                                    pbuf[:, (h8 % 2) * PB + j * 512:
                                         (h8 % 2) * PB + (j + 1) * 512],
                                    start=(j == 0), stop=(j == lim - 1),
                                    skip_group_check=True)
                            mm.then_inc(s.sem["pe"], 1)
                            s.cnt["pe"] += 1

            @blk.vector
            def _(e):
                for c in range(4):
                    lim = 4 * (c + 1)
                    nwave = (lim + 3) // 4
                    for hg in range(2):
                        idx = c * 2 + hg
                        for h8 in range(8):
                            h = hg * 8 + h8
                            e.wait_ge(s.sem["se"], se0 +
                                      self._l1_se_before(c, hg, h8) + nwave)
                            for r in range(4):
                                j = 4 * c + r
                                if j < lim:
                                    pslice = pbuf[:, (h8 % 2) * PB + j * 512:
                                                  (h8 % 2) * PB + (j + 1) * 512]
                                    e.tensor_tensor(
                                        pslice, pslice,
                                        s.mask4[:, r * 512:(r + 1) * 512],
                                        op=ALU.mult)
                            e.engine_nop().then_inc(s.sem["ve"], 1)
                            s.cnt["ve"] += 1
                            e.wait_ge(s.sem["pe"], pe0 +
                                      self._l1_pe_before(c, hg, h8) + nwave + 1)
                            e.reciprocal(rec_c[:, (h8 % 2) * 512:
                                               (h8 % 2 + 1) * 512], sum_ps[:, :])
                            s.vsync(e)
                            e.wait_ge(s.sem["pe"], pe0 +
                                      self._l1_pe_before(c, hg, h8) + nwave + 2)
                            e.tensor_tensor(
                                s.kq[:, h * T + c * 512: h * T + c * 512 + 512],
                                av_ps[:, :],
                                rec_c[:, (h8 % 2) * 512:(h8 % 2 + 1) * 512],
                                op=ALU.mult)
                            e.engine_nop().then_inc(s.sem["ve"], 1)
                            s.cnt["ve"] += 1

        if self.upto == "l1x_att":
            return
        with (
            nc.sbuf_tensor("ow_s", [128, 2 * 16 * 128], BF16) as ow_s,
            nc.psum_tensor("op2_ps", [128, 1024], F32) as op2_ps,
            nc.Block("l1o") as blk,
        ):
            WB = 16 * 128
            pe0 = s.cnt["pe"]
            ve0 = s.cnt["ve"]
            sa0 = s.cnt["sa"]
            sb0 = s.cnt["sb"]
            ld_tgt = {}

            @blk.gpsimd
            def _(e):
                for ot in range(16):
                    if ot >= 2:
                        e.wait_ge(s.sem["pe"], pe0 + (ot - 1) * 4)
                    q = ("da", "db")[ot % 2]
                    src = s.wout[:, ot * 128:(ot + 1) * 128] \
                        .rearrange("(it p) o -> p it o", p=128)
                    dst = ow_s[:, (ot % 2) * WB:(ot % 2 + 1) * WB] \
                        .rearrange("p (it o) -> p it o", it=NDT)
                    s.dma(e, dst, src, q)
                    ld_tgt[ot] = (q, s.cnt[q])

            @blk.sync
            def _(e):
                for ot in range(16):
                    if ot >= 2:
                        e.wait_ge(s.sem["ve"], ve0 + (ot - 1) * 4)
                    s.dma(e, s.xs[:, ot * T:(ot + 1) * T], s.x_scr[ot],
                          ("sa", "sb")[ot % 2])

            @blk.tensor
            def _(e):
                for ot in range(16):
                    e.wait_ge(s.sem[ld_tgt[ot][0]], ld_tgt[ot][1])
                    wb = (ot % 2) * WB
                    for ch in range(4):
                        it2 = ot * 4 + ch
                        pslice = op2_ps[:, (it2 % 2) * 512:(it2 % 2 + 1) * 512]
                        if it2 >= 2:
                            e.wait_ge(s.sem["ve"], ve0 + it2 - 1)
                        for itile in range(16):
                            mm = e.matmul(
                                pslice,
                                ow_s[:, wb + itile * 128: wb + (itile + 1) * 128],
                                s.kqt(itile, ch * 512, 512),
                                start=(itile == 0), stop=(itile == NDT - 1),
                                skip_group_check=True)
                        mm.then_inc(s.sem["pe"], 1)
                        s.cnt["pe"] += 1

            @blk.vector
            def _(e):
                for ot in range(16):
                    par = ot % 2
                    base = sa0 if par == 0 else sb0
                    e.wait_ge(s.sem[("sa", "sb")[par]],
                              base + 16 * (ot // 2 + 1))
                    for ch in range(4):
                        it2 = ot * 4 + ch
                        pslice = op2_ps[:, (it2 % 2) * 512:(it2 % 2 + 1) * 512]
                        e.wait_ge(s.sem["pe"], pe0 + it2 + 1)
                        xt = s.xt(ot, ch * 512, 512)
                        e.tensor_add(xt, pslice, xt).then_inc(s.sem["ve"], 1)
                        s.cnt["ve"] += 1

        for ch in range(4):
            self.layernorm_chunk(ch * 512)
        rot_ctx.close()

    def _l1_pe_before(self, c, hg, h8):
        tot = 0
        for cc in range(4):
            nw = cc + 1
            for gg in range(2):
                for hh in range(8):
                    if (cc, gg, hh) == (c, hg, h8):
                        return tot
                    tot += nw + 2
        return tot

    @staticmethod
    def _l1_head_seq(k):
        seq = [(cc, gg, hh) for cc in range(4) for gg in range(2)
               for hh in range(8)]
        return seq[k]

    def _l1_se_before(self, c, hg, h8):
        tot = 0
        for cc in range(4):
            nw = cc + 1
            for gg in range(2):
                for hh in range(8):
                    if (cc, gg, hh) == (c, hg, h8):
                        return tot
                    tot += nw
        return tot

    def l1_ffn(self):
        nc, s = self.nc, self

        def gsl(oc, lo, ncols):
            if oc < 32:
                return s.kq[:, oc * 1024 + lo: oc * 1024 + lo + ncols]
            return self._gx[:, (oc - 32) * 1024 + lo: (oc - 32) * 1024 + lo + ncols]

        with (
            nc.sbuf_tensor("gx", [128, 12 * 1024], BF16) as gx,
            nc.sbuf_tensor("gw", [128, 2 * 2048], BF16) as gw,
            nc.sbuf_tensor("uw", [128, 2 * 2048], BF16) as uw,
            nc.sbuf_tensor("dw", [128, 2 * NOC2 * 128], BF16) as dw,
            nc.sbuf_tensor("gt", [128, 2 * 1024], BF16) as gt,
            nc.sbuf_tensor("ut", [128, 2 * 1024], BF16) as ut,
            nc.psum_tensor("gps", [128, 1024], F32) as gps,
            nc.psum_tensor("ups", [128, 1024], F32) as ups,
            nc.psum_tensor("dps", [128, 1024], F32) as dps,
        ):
            self._gx = gx
            for tc in range(2):
                t0 = tc * 1024
                with nc.Block(s.name("f2gu")) as blk:
                    pe0 = s.cnt["pe"]
                    ve0 = s.cnt["ve"]
                    se0 = s.cnt["se"]
                    s0_ve2 = s.cnt["ve2"]
                    ld_tgt = {}

                    @blk.gpsimd
                    def _(e):
                        for oc in range(NOC2):
                            if oc >= 2:
                                e.wait_ge(s.sem["pe"], pe0 + 2 * (oc - 1))
                            q = ("da", "db")[oc % 2]
                            for (wh, wsb) in ((s.wg, gw), (s.wu, uw)):
                                src = wh[oc].rearrange("(dt p) o -> p dt o", p=128)
                                dst = wsb[:, (oc % 2) * 2048:(oc % 2 + 1) * 2048] \
                                    .rearrange("p (dt o) -> p dt o", dt=NDT)
                                s.dma(e, dst, src, q)
                            ld_tgt[oc] = (q, s.cnt[q])

                    @blk.vector
                    def _(e):
                        for oc in range(NOC2):
                            e.wait_ge(s.sem[ld_tgt[oc][0]], ld_tgt[oc][1])
                            if oc >= 2:
                                e.wait_ge(s.sem["pe"], pe0 + 2 * (oc - 1))
                            for wsb in (gw, uw):
                                sl = wsb[:, (oc % 2) * 2048:(oc % 2 + 1) * 2048]
                                e.tensor_scalar(sl, sl, 128.0, None,
                                                op0=ALU.subtract)
                            e.engine_nop().then_inc(s.sem["ve"], 1)
                            s.cnt["ve"] += 1
                            e.wait_ge(s.sem["se"], se0 + 2 * (oc + 1))
                            e.tensor_tensor(gsl(oc, 0, 1024),
                                            gt[:, (oc % 2) * 1024:(oc % 2 + 1) * 1024],
                                            ut[:, (oc % 2) * 1024:(oc % 2 + 1) * 1024],
                                            op=ALU.mult)
                            e.engine_nop().then_inc(s.sem["ve2"], 1)
                            s.cnt["ve2"] += 1

                    @blk.tensor
                    def _(e):
                        for oc in range(NOC2):
                            e.wait_ge(s.sem["ve"], ve0 + oc + 1)
                            if oc >= 1:
                                e.wait_ge(s.sem["se"], se0 + 2 * oc)
                            wb = (oc % 2) * 2048
                            for dt in range(NDT):
                                for nn in range(2):
                                    e.matmul(
                                        gps[:, nn * 512:(nn + 1) * 512],
                                        gw[:, wb + dt * 128: wb + (dt + 1) * 128],
                                        s.xt(dt, t0 + nn * 512, 512),
                                        start=(dt == 0), stop=(dt == NDT - 1),
                                        skip_group_check=True)
                                    mm = e.matmul(
                                        ups[:, nn * 512:(nn + 1) * 512],
                                        uw[:, wb + dt * 128: wb + (dt + 1) * 128],
                                        s.xt(dt, t0 + nn * 512, 512),
                                        start=(dt == 0), stop=(dt == NDT - 1),
                                        skip_group_check=True)
                            mm.then_inc(s.sem["pe"], 2)
                            s.cnt["pe"] += 2

                    @blk.scalar
                    def _(e):
                        for oc in range(NOC2):
                            e.wait_ge(s.sem["pe"], pe0 + 2 * (oc + 1))
                            if oc >= 2:
                                e.wait_ge(s.sem["ve2"], s0_ve2 + oc - 1)
                            e.activation(gt[:, (oc % 2) * 1024:(oc % 2 + 1) * 1024],
                                         gps[:, :], AF.Silu,
                                         scale=s.sg_sb[:, oc:oc + 1]).then_inc(
                                s.sem["se"], 1)
                            s.cnt["se"] += 1
                            e.activation(ut[:, (oc % 2) * 1024:(oc % 2 + 1) * 1024],
                                         ups[:, :], AF.Identity,
                                         scale=s.su_sb[:, oc:oc + 1]).then_inc(
                                s.sem["se"], 1)
                            s.cnt["se"] += 1

                with nc.Block(s.name("f2dn")) as blk:
                    pe0 = s.cnt["pe"]
                    ve0 = s.cnt["ve"]
                    ld_tgt = {}

                    @blk.gpsimd
                    def _(e):
                        for ct in range(16):
                            if ct >= 2:
                                e.wait_ge(s.sem["pe"], pe0 + ct - 1)
                            q = ("da", "db")[ct % 2]
                            src = s.wd[ct].rearrange("(ot p) o -> p ot o", p=128)
                            dst = dw[:, (ct % 2) * NOC2 * 128:
                                     (ct % 2 + 1) * NOC2 * 128] \
                                .rearrange("p (ot o) -> p ot o", ot=NOC2)
                            s.dma(e, dst, src, q)
                            ld_tgt[ct] = (q, s.cnt[q])

                    @blk.vector
                    def _(e):
                        for ct in range(16):
                            e.wait_ge(s.sem[ld_tgt[ct][0]], ld_tgt[ct][1])
                            if ct >= 2:
                                e.wait_ge(s.sem["pe"], pe0 + ct - 1)
                            sl = dw[:, (ct % 2) * NOC2 * 128:
                                    (ct % 2 + 1) * NOC2 * 128]
                            e.tensor_scalar(sl, sl, 128.0, None,
                                            op0=ALU.subtract)
                            e.engine_nop().then_inc(s.sem["ve"], 1)
                            s.cnt["ve"] += 1
                            e.wait_ge(s.sem["pe"], pe0 + ct + 1)
                            for nn in range(2):
                                xt = s.xt(ct, t0 + nn * 512, 512)
                                e.scalar_tensor_tensor(
                                    xt, dps[:, nn * 512:(nn + 1) * 512],
                                    s.sd_sb[:, ct:ct + 1], xt,
                                    op0=ALU.mult, op1=ALU.add)
                            e.engine_nop().then_inc(s.sem["ve2"], 1)
                            s.cnt["ve2"] += 1

                    @blk.tensor
                    def _(e):
                        for ct in range(16):
                            e.wait_ge(s.sem["ve"], ve0 + ct + 1)
                            if ct >= 1:
                                e.wait_ge(s.sem["ve2"], s.cnt["ve2"] - 16 + ct)
                            wb = (ct % 2) * NOC2 * 128
                            for ot in range(NOC2):
                                for nn in range(2):
                                    mm = e.matmul(
                                        dps[:, nn * 512:(nn + 1) * 512],
                                        dw[:, wb + ot * 128: wb + (ot + 1) * 128],
                                        gsl(ot, nn * 512, 512),
                                        start=(ot == 0), stop=(ot == NOC2 - 1),
                                        skip_group_check=True)
                            mm.then_inc(s.sem["pe"], 1)
                            s.cnt["pe"] += 1

            self._gx = None

    def final_out(self):
        nc, s = self.nc, self
        with nc.sbuf_tensor("ostage", [128, 16 * 512], F32) as ostage:
            for ch in range(4):
                self.layernorm_chunk(ch * 512, final=True, out_stage=ostage)
                with nc.Block(s.name("fo")) as blk:
                    @blk.sync
                    def _(e):
                        for dt in range(NDT):
                            s.dma(e, s.outT[dt * 128:(dt + 1) * 128,
                                            ch * 512:(ch + 1) * 512],
                                  ostage[:, dt * 512:(dt + 1) * 512],
                                  ("sa", "sb")[ch % 2])
                        e.wait_ge(s.sem[("sa", "sb")[ch % 2]],
                                  s.tgt(("sa", "sb")[ch % 2]))

    def write_out_from_xs(self, srct=None):
        nc, s = self.nc, self
        if srct is None:
            srct = s.xs
        with (
            nc.sbuf_tensor(s.name("ostage"), [128, 2 * T], F32) as ostage,
            nc.Block(s.name("wout")) as blk,
        ):
            ve0 = s.cnt["ve"]
            sa0 = s.cnt["sa"]
            sb0 = s.cnt["sb"]

            @blk.vector
            def _(e):
                for dt in range(NDT):
                    if dt >= 2:
                        par = dt % 2
                        base = sa0 if par == 0 else sb0
                        e.wait_ge(s.sem[("sa", "sb")[par]],
                                  base + 16 * ((dt - 2) // 2 + 1))
                    e.tensor_copy(ostage[:, (dt % 2) * T:(dt % 2 + 1) * T],
                                  srct[:, dt * T:(dt + 1) * T]).then_inc(
                        s.sem["ve"], 1)
                    s.cnt["ve"] += 1

            @blk.sync
            def _(e):
                for dt in range(NDT):
                    e.wait_ge(s.sem["ve"], ve0 + dt + 1)
                    s.dma(e, s.outT[dt * 128:(dt + 1) * 128, :],
                          ostage[:, (dt % 2) * T:(dt % 2 + 1) * T],
                          ("sa", "sb")[dt % 2])
                e.wait_ge(s.sem["sa"], s.tgt("sa"))
                e.wait_ge(s.sem["sb"], s.tgt("sb"))

    def build(self):
        with self.nc.named_scope("prologue"):
            self.prologue()
        if self.upto.startswith("ffn_"):
            self.l0_ffn()
            self.write_out_from_xs(self.kq)
            self.ctx.close()
            return self.nc
        if self.upto == "l1a_only" or self.upto.startswith("l1x_"):
            if self.kq is None:
                self.kq = self.ctx.enter_context(
                    self.nc.sbuf_tensor("kq", [128, 16 * T], BF16))
            self.l1_attn()
            if self.upto.startswith("l1x_"):
                self.write_out_from_xs(self.kq)
            else:
                self.write_out_from_xs()
            self.ctx.close()
            return self.nc
        with self.nc.named_scope("l0_attn"):
            self.l0_attn()
        if self.upto == "l0a":
            self.write_out_from_xs()
            self.ctx.close()
            return self.nc
        with self.nc.named_scope("l0_ffn"):
            self.l0_ffn()
        if self.upto == "l0f":
            self.write_out_from_xs()
            self.ctx.close()
            return self.nc
        with self.nc.named_scope("l1_attn"):
            self.l1_attn()
        if self.upto == "l1a":
            self.write_out_from_xs()
            self.ctx.close()
            return self.nc
        with self.nc.named_scope("l1_ffn"):
            self.l1_ffn()
        with self.nc.named_scope("final_out"):
            self.final_out()
        self.ctx.close()
        return self.nc


def _host_consts():
    ones = np.ones((128, 128), dtype=ml_dtypes.bfloat16)
    gsel = np.zeros((128, 128), dtype=np.float32)
    gsel[0:64, 0:64] = 1.0
    gsel[64:128, 64:128] = 1.0
    ident = np.eye(128).astype(ml_dtypes.bfloat16)
    h128 = _sylvester(128).astype(ml_dtypes.bfloat16)
    import jax
    import jax.numpy as jnp
    cpu = jax.devices("cpu")[0]
    with jax.default_device(cpu):
        inv_freq = 1.0 / 10000.0 ** (jnp.arange(0, 32, 2,
                                                dtype=jnp.float32) / 32)
        fr_j = jnp.arange(T, dtype=jnp.float32)[:, None] * inv_freq[None, :]
        pos = jnp.concatenate((fr_j, fr_j), axis=-1)
        cosv = np.asarray(jnp.cos(pos)).astype(np.float32)
        sinv = np.asarray(jnp.sin(pos)).astype(np.float32)
    rotC = np.ascontiguousarray(cosv.T)
    sT = sinv.T
    rotS = np.concatenate([-sT[0:16], sT[16:32]], 0).astype(np.float32)
    rotS = np.ascontiguousarray(np.concatenate(
        [-sinv.T[16:32] * 0 - sinv.T[0:16], sinv.T[16:32]], 0)).astype(np.float32)
    rotCl = np.zeros_like(rotC)
    rotSl = np.zeros_like(rotS)
    qs_hi = np.float32(1.0 / np.sqrt(np.float64(128.0)))
    qs_lo = np.float32(1.0 / np.sqrt(np.float64(128.0)) - np.float64(qs_hi))
    misc = np.zeros((128, 4), dtype=np.float32)
    misc[:, 0] = qs_hi
    misc[:, 1] = qs_lo
    m = np.zeros((128, 4, 4, 128), dtype=np.float32)
    for r in range(4):
        for itl in range(4):
            if r < itl:
                m[:, r, itl, :] = 1.0
        m[:, r, r, :] = (np.arange(128)[:, None] <= np.arange(128)[None, :])
    mask = m.reshape(128, 4 * 512).astype(ml_dtypes.bfloat16)
    return dict(ones_h=ones, gsel_h=gsel.astype(ml_dtypes.bfloat16),
                ident_h=ident, h128_h=h128,
                rotC_h=rotC, rotS_h=rotS, rotCl_h=rotCl, rotSl_h=rotSl,
                misc_h=misc, mask_h=mask)


def _prep_core_inputs(b, inp, consts):
    c = lambda a: np.ascontiguousarray(a)
    x = np.asarray(inp["L0_attn_hidden_states"])
    d = dict(
        xT=c(np.asarray(x[b]).T.astype(np.float32)),
        Wq=c(np.asarray(inp["L0_attn_Wq"])), Wk=c(np.asarray(inp["L0_attn_Wk"])),
        Wv=c(np.asarray(inp["L0_attn_Wv"])), Wo=c(np.asarray(inp["L0_attn_Wo"])),
        bo=c(np.asarray(inp["L0_attn_bo"]).reshape(16, 128).T),
        wup=c(np.asarray(inp["L0_ffn_w_up"]).T.reshape(D, NOC1, 128)
              .transpose(1, 0, 2)),
        wdn=c(np.asarray(inp["L0_ffn_w_down"]).T.reshape(HID1, 16, 128)
              .transpose(1, 0, 2)),
        wqkv=c(np.asarray(inp["L1_attn_w_qkv"]).T),
        wout=c(np.asarray(inp["L1_attn_w_out"]).T),
        wg=c(np.asarray(inp["L1_ffn_wq_gate"]).T.reshape(D, NOC2, 128)
             .transpose(1, 0, 2)),
        wu=c(np.asarray(inp["L1_ffn_wq_up"]).T.reshape(D, NOC2, 128)
             .transpose(1, 0, 2)),
        wd=c(np.asarray(inp["L1_ffn_wq_down"]).T.reshape(HID2, 16, 128)
             .transpose(1, 0, 2)),
        sg=c(np.asarray(inp["L1_ffn_s_gate"]).reshape(NOC2, 128).T),
        su=c(np.asarray(inp["L1_ffn_s_up"]).reshape(NOC2, 128).T),
        sd=c(np.asarray(inp["L1_ffn_s_down"]).reshape(16, 128).T),
    )
    d.update(consts)
    return d


def kernel(**inputs):
    nc = B().build()
    consts = _host_consts()
    in_maps = [_prep_core_inputs(b, inputs, consts) for b in range(8)]
    res = run_bass_kernel_spmd(nc, in_maps, core_ids=list(range(8)))
    out = np.stack([np.ascontiguousarray(r["outT"].T) for r in res.results])
    return out.astype(np.float32)


# version 06 (final)
# speedup vs baseline: 1.1433x; 1.1128x over previous
import numpy as np
import ml_dtypes
import concourse.bass as bass
import concourse.mybir as mybir
from concourse.bass_utils import run_bass_kernel_spmd
from contextlib import ExitStack

F32 = mybir.dt.float32
BF16 = mybir.dt.bfloat16
I32 = mybir.dt.int32
AF = mybir.ActivationFunctionType
ALU = mybir.AluOpType
AX = mybir.AxisListType

MAGIC = 12582912.0
T = 2048
D = 2048
NDT = 16
HID1 = 4096
HID2 = 5632
NOC1 = HID1 // 128
NOC2 = HID2 // 128


def _sylvester(n):
    h = np.array([[1.0]])
    while h.shape[0] < n:
        h = np.block([[h, h], [h, -h]])
    return h


class B:

    def __init__(self, upto="all"):
        self.upto = upto
        self.uid = 0
        nc = bass.Bass("TRN2", target_bir_lowering=False)
        self.nc = nc
        self.ctx = ExitStack()

        dram = nc.dram_tensor
        self.xT = dram("xT", [D, T], F32, kind="ExternalInput")
        self.Wq = dram("Wq", [D, D], F32, kind="ExternalInput")
        self.Wk = dram("Wk", [D, D], F32, kind="ExternalInput")
        self.Wv = dram("Wv", [D, D], F32, kind="ExternalInput")
        self.Wo = dram("Wo", [D, D], F32, kind="ExternalInput")
        self.bo = dram("bo", [128, 16], F32, kind="ExternalInput")
        self.wup = dram("wup", [NOC1, D, 128], F32, kind="ExternalInput")
        self.wdn = dram("wdn", [16, HID1, 128], F32, kind="ExternalInput")
        self.wqkv = dram("wqkv", [D, 3 * D], F32, kind="ExternalInput")
        self.wout = dram("wout", [D, D], F32, kind="ExternalInput")
        self.wg = dram("wg", [NOC2, D, 128], I32, kind="ExternalInput")
        self.wu = dram("wu", [NOC2, D, 128], I32, kind="ExternalInput")
        self.wd = dram("wd", [16, HID2, 128], I32, kind="ExternalInput")
        self.sg = dram("sg", [128, NOC2], F32, kind="ExternalInput")
        self.su = dram("su", [128, NOC2], F32, kind="ExternalInput")
        self.sd = dram("sd", [128, 16], F32, kind="ExternalInput")
        self.ones_h = dram("ones_h", [128, 128], BF16, kind="ExternalInput")
        self.gsel_h = dram("gsel_h", [128, 128], BF16, kind="ExternalInput")
        self.ident_h = dram("ident_h", [128, 128], BF16, kind="ExternalInput")
        self.h128_h = dram("h128_h", [128, 128], BF16, kind="ExternalInput")
        self.rotC_h = dram("rotC_h", [32, T], F32, kind="ExternalInput")
        self.rotS_h = dram("rotS_h", [32, T], F32, kind="ExternalInput")
        self.rotCl_h = dram("rotCl_h", [32, T], F32, kind="ExternalInput")
        self.rotSl_h = dram("rotSl_h", [32, T], F32, kind="ExternalInput")
        self.misc_h = dram("misc_h", [128, 4], F32, kind="ExternalInput")
        self.mask_h = dram("mask_h", [128, 4 * 512], BF16, kind="ExternalInput")

        self.outT = dram("outT", [D, T], F32, kind="ExternalOutput")

        self.wup_q = dram("wup_q", [NOC1, D, 128], BF16)
        self.wdn_q = dram("wdn_q", [16, HID1, 128], BF16)
        self.k_scr = dram("k_scr", [16, 128, T], BF16)
        self.v_scr = dram("v_scr", [16, 128, D], BF16)
        self.x_scr = dram("x_scr", [16, 128, T], BF16)

        sb = lambda *a: self.ctx.enter_context(nc.sbuf_tensor(*a))
        self.xs = sb("xs", [128, NDT * T], BF16)
        self.kq = None
        self.ones = sb("ones", [128, 128], BF16)
        self.gsel = sb("gsel", [128, 128], BF16)
        self.ident = sb("ident", [128, 128], BF16)
        self.h128 = sb("h128", [128, 128], BF16)
        self.rotC = None
        self.rotS = None
        self.mask4 = None
        self.bo_sb = sb("bo_sb", [128, 16], F32)
        self.sg_sb = sb("sg_sb", [128, NOC2], F32)
        self.su_sb = sb("su_sb", [128, NOC2], F32)
        self.sd_sb = sb("sd_sb", [128, 16], F32)
        self.kvbd = sb("kvbd", [128, 16 * 128], BF16)
        self.ks_sb = sb("ks_sb", [128, 16], F32)
        self.gup = sb("gup", [128, 1], F32)
        self.mg = sb("mg", [128, 1], F32)
        self.misc = sb("misc", [128, 4], F32)
        self.qscl = sb("qscl", [128, 1], F32)
        self.mB = sb("mB", [128, 512], F32)
        self.rB = sb("rB", [128, 512], F32)

        self.sem = {}
        self.cnt = {}
        for name in ["qs", "qg", "qc", "da", "db", "dc", "dd", "sa", "sb",
                     "pe", "ve", "se", "pe2", "ve2", "se2", "vs"]:
            self.sem[name] = self.ctx.enter_context(nc.semaphore(name))
            self.cnt[name] = 0

    def name(self, prefix):
        self.uid += 1
        return f"{prefix}_{self.uid}"

    def dma(self, eng, dst, src, q):
        eng.dma_start(dst, src).then_inc(self.sem[q], 16)
        self.cnt[q] += 16
        return self.cnt[q]

    def tgt(self, name):
        return self.cnt[name]

    def vsync(self, e):
        self.cnt["vs"] += 1
        e.engine_nop().then_inc(self.sem["vs"], 1)
        e.wait_ge(self.sem["vs"], self.cnt["vs"])

    def xt(self, dt, lo, n):
        return self.xs[:, dt * T + lo: dt * T + lo + n]

    def kqt(self, it, lo, n):
        return self.kq[:, it * T + lo: it * T + lo + n]

    def prologue(self):
        nc, s = self.nc, self
        with nc.Block("prologue") as blk:
            @blk.sync
            def _(e):
                for dst, src in [
                    (s.ones, s.ones_h), (s.gsel, s.gsel_h), (s.ident, s.ident_h),
                    (s.h128, s.h128_h), (s.bo_sb, s.bo), (s.sg_sb, s.sg),
                    (s.su_sb, s.su), (s.sd_sb, s.sd), (s.misc, s.misc_h),
                ]:
                    s.dma(e, dst[:, :], src[:, :], "qs")

            @blk.gpsimd
            def _(e):
                xv = s.xT.rearrange("(dt p) t -> p dt t", p=128)
                xd = s.xs[:, :].rearrange("p (dt t) -> p dt t", dt=NDT)
                s.dma(e, xd, xv, "qg")
                e.memset(s.kvbd[:, :], 0)
                e.memset(s.mg[:, :], 512.0)

            @blk.vector
            def _(e):
                e.wait_ge(s.sem["qs"], s.tgt("qs"))
                e.wait_ge(s.sem["qg"], s.tgt("qg"))
                e.tensor_tensor(s.qscl[:, :], s.misc[:, 0:1], s.misc[:, 1:2],
                                op=ALU.add)

    def layernorm_chunk(self, c0, final=False, out_stage=None):
        nc, s = self.nc, self
        with (
            nc.sbuf_tensor(s.name("ln_sq"), [128, 1024], BF16) as sq,
            nc.psum_tensor(s.name("ln_ps"), [128, 1024], F32) as ps,
        ):
            psum_s = ps[:, 0:512]
            psum_q = ps[:, 512:1024]
            with nc.Block(s.name("ln1")) as blk:
                ve0 = s.cnt["ve"]
                pe0 = s.cnt["pe"]

                @blk.vector
                def _(e):
                    for dt in range(NDT):
                        if dt > 1:
                            e.wait_ge(s.sem["pe"], pe0 + dt - 1)
                        xt = s.xt(dt, c0, 512)
                        e.tensor_mul(sq[:, (dt % 2) * 512:(dt % 2 + 1) * 512],
                                     xt, xt).then_inc(s.sem["ve"], 1)
                        s.cnt["ve"] += 1

                @blk.tensor
                def _(e):
                    for dt in range(NDT):
                        xt = s.xt(dt, c0, 512)
                        e.wait_ge(s.sem["ve"], ve0 + dt + 1)
                        e.matmul(psum_s, s.ones[:, :], xt,
                                 start=(dt == 0), stop=(dt == NDT - 1),
                                 skip_group_check=True)
                        e.matmul(psum_q, s.ones[:, :],
                                 sq[:, (dt % 2) * 512:(dt % 2 + 1) * 512],
                                 start=(dt == 0), stop=(dt == NDT - 1),
                                 skip_group_check=True).then_inc(s.sem["pe"], 1)
                        s.cnt["pe"] += 1

            with nc.Block(s.name("ln2")) as blk:
                @blk.vector
                def _(e):
                    e.tensor_scalar(s.mB[:, :], psum_s, 1.0 / D, None, op0=ALU.mult)
                    e.tensor_scalar(s.rB[:, :], psum_q, 1.0 / D, None, op0=ALU.mult)
                    s.vsync(e)
                    e.tensor_tensor(sq[:, 0:512], s.mB[:, :], s.mB[:, :],
                                    op=ALU.mult)
                    s.vsync(e)
                    e.tensor_tensor(s.rB[:, :], s.rB[:, :], sq[:, 0:512],
                                    op=ALU.subtract)
                    s.vsync(e)
                    e.tensor_scalar(s.rB[:, :], s.rB[:, :], 1e-5, None,
                                    op0=ALU.add).then_inc(s.sem["ve"], 1)
                    s.cnt["ve"] += 1

                @blk.scalar
                def _(e):
                    e.wait_ge(s.sem["ve"], s.tgt("ve"))
                    e.activation(s.rB[:, :], s.rB[:, :], AF.Sqrt).then_inc(
                        s.sem["se"], 1)
                    s.cnt["se"] += 1

            with nc.Block(s.name("ln3")) as blk:
                @blk.vector
                def _(e):
                    e.wait_ge(s.sem["se"], s.tgt("se"))
                    e.reciprocal(s.rB[:, :], s.rB[:, :])
                    s.vsync(e)
                    for dt in range(NDT):
                        xt = s.xt(dt, c0, 512)
                        sqp = sq[:, (dt % 2) * 512:(dt % 2 + 1) * 512]
                        e.tensor_sub(sqp, xt, s.mB[:, :])
                        s.vsync(e)
                        if final:
                            e.tensor_mul(out_stage[:, dt * 512:(dt + 1) * 512],
                                         sqp, s.rB[:, :])
                        else:
                            e.tensor_mul(xt, sqp, s.rB[:, :])

    def l0_attn(self):
        nc, s = self.nc, self
        scopeA = nc.named_scope("l0a_A"); scopeA.__enter__()
        with (
            nc.sbuf_tensor("wk_s", [128, 2 * 16 * 512], BF16) as wk_s,
            nc.sbuf_tensor("wv_s", [128, 2 * 16 * 512], BF16) as wv_s,
            nc.sbuf_tensor("k_sb", [128, 512], BF16) as k_sb,
            nc.sbuf_tensor("v_sb", [128, 512], BF16) as v_sb,
            nc.psum_tensor("kp", [128, 512], F32) as kp,
            nc.psum_tensor("vp", [128, 512], F32) as vp,
            nc.psum_tensor("kvp", [128, 512], F32) as kvp,
            nc.psum_tensor("ksp", [128, 16], F32) as ksp,
            nc.Block("l0a_A") as blk,
        ):
            HB = 16 * 512
            pe0 = s.cnt["pe"]
            pe20 = s.cnt["pe2"]
            ve0 = s.cnt["ve"]
            ve20 = s.cnt["ve2"]

            ld_tgt = {}

            @blk.gpsimd
            def _(e):
                for c4 in range(4):
                    if c4 >= 2:
                        e.wait_ge(s.sem["pe"], pe0 + (c4 - 1) * 16)
                    q = ("da", "db")[c4 % 2]
                    for (w_h, w_s) in ((s.Wk, wk_s), (s.Wv, wv_s)):
                        src = w_h[:, c4 * 512:(c4 + 1) * 512] \
                            .rearrange("(dt p) i -> p dt i", p=128)
                        dst = w_s[:, (c4 % 2) * HB:(c4 % 2 + 1) * HB] \
                            .rearrange("p (dt i) -> p dt i", dt=NDT)
                        s.dma(e, dst, src, q)
                    ld_tgt[c4] = (q, s.cnt[q])

            @blk.tensor
            def _(e):
                for c4 in range(4):
                    e.wait_ge(s.sem[ld_tgt[c4][0]], ld_tgt[c4][1])
                    if c4 > 0:
                        e.wait_ge(s.sem["ve2"], ve20 + c4)
                    wb = (c4 % 2) * HB
                    for tt in range(16):
                        it = c4 * 16 + tt
                        if it > 0:
                            e.wait_ge(s.sem["ve"], ve0 + it)
                        for dt in range(NDT):
                            xt = s.xt(dt, tt * 128, 128)
                            e.matmul(kp[:, :], xt,
                                     wk_s[:, wb + dt * 512: wb + (dt + 1) * 512],
                                     start=(dt == 0), stop=(dt == NDT - 1),
                                     skip_group_check=True)
                            mm = e.matmul(vp[:, :], xt,
                                          wv_s[:, wb + dt * 512: wb + (dt + 1) * 512],
                                          start=(dt == 0), stop=(dt == NDT - 1),
                                          skip_group_check=True)
                        mm.then_inc(s.sem["pe"], 1)
                        s.cnt["pe"] += 1
                        e.wait_ge(s.sem["ve"], ve0 + it + 1)
                        for ps_i in range(4):
                            pair = c4 * 4 + ps_i
                            sl = slice(ps_i * 128, (ps_i + 1) * 128)
                            e.matmul(kvp[:, sl], k_sb[:, sl], v_sb[:, sl],
                                     start=(tt == 0 and ps_i == 0),
                                     stop=(tt == 15),
                                     skip_group_check=True)
                            mm = e.matmul(ksp[:, pair:pair + 1], k_sb[:, sl],
                                          s.ones[:, 0:1],
                                          start=(tt == 0 and ps_i == 0),
                                          stop=(tt == 15),
                                          skip_group_check=True)
                        mm.then_inc(s.sem["pe2"], 1)
                        s.cnt["pe2"] += 1

            @blk.vector
            def _(e):
                for c4 in range(4):
                    for tt in range(16):
                        it = c4 * 16 + tt
                        e.wait_ge(s.sem["pe"], pe0 + it + 1)
                        e.tensor_scalar(k_sb[:, :], kp[:, :], 0.0, None,
                                        op0=ALU.max)
                        e.tensor_copy(v_sb[:, :], vp[:, :]).then_inc(
                            s.sem["ve"], 1)
                        s.cnt["ve"] += 1
                    e.wait_ge(s.sem["pe2"], pe20 + (c4 + 1) * 16)
                    for ps_i in range(4):
                        pair = c4 * 4 + ps_i
                        for half in range(2):
                            h0 = half * 64
                            e.tensor_copy(
                                s.kvbd[h0:h0 + 64,
                                       pair * 128 + h0: pair * 128 + h0 + 64],
                                kvp[h0:h0 + 64,
                                    ps_i * 128 + h0: ps_i * 128 + h0 + 64])
                    e.tensor_copy(s.ks_sb[:, c4 * 4:(c4 + 1) * 4],
                                  ksp[:, c4 * 4:(c4 + 1) * 4]).then_inc(
                        s.sem["ve2"], 1)
                    s.cnt["ve2"] += 1

        scopeA.__exit__(None, None, None)
        scopeB = nc.named_scope("l0a_B1"); scopeB.__enter__()
        if s.kq is None:
            s.kq = s.ctx.enter_context(nc.sbuf_tensor("kq", [128, 16 * T], BF16))
        with (
            nc.sbuf_tensor("wq_s", [128, 2 * 16 * 128], BF16) as wq_s,
            nc.sbuf_tensor("q_sb", [128, 2 * 512], BF16) as q_sb,
            nc.sbuf_tensor("tq_sb", [128, 512], BF16) as tq_sb,
            nc.sbuf_tensor("rec_sb", [128, 2 * 512], F32) as rec_sb,
            nc.psum_tensor("qp", [128, 1024], F32) as qp,
            nc.psum_tensor("npp", [128, 1024], F32) as npp,
            nc.psum_tensor("qkp", [128, 1024], F32) as qkp,
            nc.Block("l0a_B1") as blk,
        ):
            WB = 16 * 128
            pe0 = s.cnt["pe"]
            ve0 = s.cnt["ve"]
            ld_tgt = {}

            @blk.gpsimd
            def _(e):
                for pair in range(16):
                    if pair >= 2:
                        e.wait_ge(s.sem["pe"], pe0 + (pair - 1) * 12)
                    q = ("da", "db")[pair % 2]
                    src = s.Wq[:, pair * 128:(pair + 1) * 128] \
                        .rearrange("(dt p) i -> p dt i", p=128)
                    dst = wq_s[:, (pair % 2) * WB:(pair % 2 + 1) * WB] \
                        .rearrange("p (dt i) -> p dt i", dt=NDT)
                    s.dma(e, dst, src, q)
                    ld_tgt[pair] = (q, s.cnt[q])

            @blk.tensor
            def _(e):
                e.wait_ge(s.sem["ve2"], s.tgt("ve2"))
                for pair in range(16):
                    e.wait_ge(s.sem[ld_tgt[pair][0]], ld_tgt[pair][1])
                    wb = (pair % 2) * WB
                    for ch in range(4):
                        it = pair * 4 + ch
                        par = it % 2
                        if it >= 2:
                            e.wait_ge(s.sem["ve"], ve0 + (it - 2) * 3 + 1)
                        for dt in range(NDT):
                            mm = e.matmul(
                                qp[:, par * 512:(par + 1) * 512],
                                wq_s[:, wb + dt * 128: wb + (dt + 1) * 128],
                                s.xt(dt, ch * 512, 512),
                                start=(dt == 0), stop=(dt == NDT - 1),
                                skip_group_check=True)
                        mm.then_inc(s.sem["pe"], 1)
                        s.cnt["pe"] += 1
                        e.wait_ge(s.sem["ve"], ve0 + it * 3 + 1)
                        if it >= 2:
                            e.wait_ge(s.sem["ve"], ve0 + (it - 2) * 3 + 2)
                        e.matmul(npp[:, par * 512:(par + 1) * 512],
                                 s.gsel[:, :], tq_sb[:, :],
                                 start=True, stop=True,
                                 skip_group_check=True).then_inc(s.sem["pe"], 1)
                        s.cnt["pe"] += 1
                        if it >= 2:
                            e.wait_ge(s.sem["ve"], ve0 + (it - 2) * 3 + 3)
                        e.matmul(qkp[:, par * 512:(par + 1) * 512],
                                 s.kvbd[:, pair * 128:(pair + 1) * 128],
                                 q_sb[:, par * 512:(par + 1) * 512],
                                 start=True, stop=True,
                                 skip_group_check=True).then_inc(s.sem["pe"], 1)
                        s.cnt["pe"] += 1

            @blk.vector
            def _(e):
                for pair in range(16):
                    for ch in range(4):
                        it = pair * 4 + ch
                        par = it % 2
                        e.wait_ge(s.sem["pe"], pe0 + it * 3 + 1)
                        e.tensor_scalar(q_sb[:, par * 512:(par + 1) * 512],
                                        qp[:, par * 512:(par + 1) * 512],
                                        0.0, None, op0=ALU.max)
                        s.vsync(e)
                        e.tensor_scalar(tq_sb[:, :],
                                        q_sb[:, par * 512:(par + 1) * 512],
                                        s.ks_sb[:, pair:pair + 1], None,
                                        op0=ALU.mult).then_inc(s.sem["ve"], 1)
                        s.cnt["ve"] += 1
                        e.wait_ge(s.sem["pe"], pe0 + it * 3 + 2)
                        e.tensor_scalar(rec_sb[:, par * 512:(par + 1) * 512],
                                        npp[:, par * 512:(par + 1) * 512],
                                        1e-6, None, op0=ALU.max)
                        s.vsync(e)
                        e.reciprocal(rec_sb[:, par * 512:(par + 1) * 512],
                                     rec_sb[:, par * 512:(par + 1) * 512]) \
                            .then_inc(s.sem["ve"], 1)
                        s.cnt["ve"] += 1
                        s.vsync(e)
                        e.wait_ge(s.sem["pe"], pe0 + it * 3 + 3)
                        e.tensor_tensor(
                            s.kqt(pair, ch * 512, 512),
                            qkp[:, par * 512:(par + 1) * 512],
                            rec_sb[:, par * 512:(par + 1) * 512],
                            op=ALU.mult).then_inc(s.sem["ve"], 1)
                        s.cnt["ve"] += 1

        scopeB.__exit__(None, None, None)
        scopeB2 = nc.named_scope("l0a_B2"); scopeB2.__enter__()
        with (
            nc.sbuf_tensor("wo_s", [128, 2 * 16 * 128], BF16) as wo_s,
            nc.psum_tensor("op_ps", [128, 1024], F32) as op_ps,
            nc.Block("l0a_B2") as blk,
        ):
            WB = 16 * 128
            pe0 = s.cnt["pe"]
            ve0 = s.cnt["ve"]
            ld_tgt = {}

            @blk.gpsimd
            def _(e):
                for ot in range(16):
                    if ot >= 2:
                        e.wait_ge(s.sem["pe"], pe0 + (ot - 1) * 4)
                    q = ("da", "db")[ot % 2]
                    src = s.Wo[:, ot * 128:(ot + 1) * 128] \
                        .rearrange("(it p) o -> p it o", p=128)
                    dst = wo_s[:, (ot % 2) * WB:(ot % 2 + 1) * WB] \
                        .rearrange("p (it o) -> p it o", it=NDT)
                    s.dma(e, dst, src, q)
                    ld_tgt[ot] = (q, s.cnt[q])

            @blk.tensor
            def _(e):
                for ot in range(16):
                    e.wait_ge(s.sem[ld_tgt[ot][0]], ld_tgt[ot][1])
                    wb = (ot % 2) * WB
                    for ch in range(4):
                        it = ot * 4 + ch
                        pslice = op_ps[:, (it % 2) * 512:(it % 2 + 1) * 512]
                        if it >= 2:
                            e.wait_ge(s.sem["ve"], ve0 + it - 1)
                        for itile in range(16):
                            mm = e.matmul(
                                pslice, wo_s[:, wb + itile * 128: wb + (itile + 1) * 128],
                                s.kqt(itile, ch * 512, 512),
                                start=(itile == 0), stop=(itile == NDT - 1),
                                skip_group_check=True)
                        mm.then_inc(s.sem["pe"], 1)
                        s.cnt["pe"] += 1

            @blk.vector
            def _(e):
                for ot in range(16):
                    for ch in range(4):
                        it = ot * 4 + ch
                        pslice = op_ps[:, (it % 2) * 512:(it % 2 + 1) * 512]
                        e.wait_ge(s.sem["pe"], pe0 + it + 1)
                        xt = s.xt(ot, ch * 512, 512)
                        e.scalar_tensor_tensor(
                            xt, pslice, s.bo_sb[:, ot:ot + 1], xt,
                            op0=ALU.add, op1=ALU.add).then_inc(s.sem["ve"], 1)
                        s.cnt["ve"] += 1

        scopeB2.__exit__(None, None, None)
        with nc.named_scope("l0a_ln"):
            for ch in range(4):
                self.layernorm_chunk(ch * 512)

    def l0_ffn_wquant(self):
        nc, s = self.nc, self
        s.wu_m = s.ctx.enter_context(nc.sbuf_tensor("wu_m", [128, 1], F32))
        s.wd_m = s.ctx.enter_context(nc.sbuf_tensor("wd_m", [128, 1], F32))
        s.wu_r = s.ctx.enter_context(nc.sbuf_tensor("wu_r", [128, 1], F32))
        s.wd_r = s.ctx.enter_context(nc.sbuf_tensor("wd_r", [128, 1], F32))
        specs = [
            (s.wup, s.wup_q, NOC1, NDT * 128, s.wu_m, s.wu_r),
            (s.wdn, s.wdn_q, 16, NOC1 * 128, s.wd_m, s.wd_r),
        ]
        with (
            nc.sbuf_tensor("wqb", [128, 2 * 4096], F32) as wqb,
            nc.sbuf_tensor("wqt", [128, 2 * 4096], BF16) as wqt,
            nc.sbuf_tensor("wqi", [128, 4096], I32) as wqi,
            nc.sbuf_tensor("wqa", [128, 48], F32) as wqa,
            nc.sbuf_tensor("wqa_bf", [128, 1], BF16) as wqa_bf,
            nc.psum_tensor("wq_ps", [128, 1], F32) as wq_ps,
        ):
            with nc.Block("wq1") as blk:
                ve0 = s.cnt["ve"]
                ld_tgt = {}

                @blk.gpsimd
                def _(e):
                    i = 0
                    for (wh, _sc, n, epp, _m, _r) in specs:
                        for ch in range(n):
                            if i >= 2:
                                e.wait_ge(s.sem["ve"], ve0 + i - 1)
                            q = ("da", "db")[i % 2]
                            src = wh[ch].rearrange("(a p) o -> p a o", p=128)
                            dst = wqb[:, (i % 2) * 4096:(i % 2) * 4096 + epp] \
                                .rearrange("p (a o) -> p a o", o=128)
                            s.dma(e, dst, src, q)
                            ld_tgt[i] = (q, s.cnt[q])
                            i += 1

                @blk.vector
                def _(e):
                    i = 0
                    for (wh, _sc, n, epp, _m, _r) in specs:
                        for ch in range(n):
                            e.wait_ge(s.sem[ld_tgt[i][0]], ld_tgt[i][1])
                            e.reduce_sum(wqa[:, i:i + 1],
                                         wqb[:, (i % 2) * 4096:(i % 2) * 4096 + epp],
                                         axis=AX.X,
                                         apply_absolute_value=True).then_inc(
                                s.sem["ve"], 1)
                            s.cnt["ve"] += 1
                            i += 1

            for wi, (wh, _sc, n, epp, m_sb, r_sb) in enumerate(specs):
                lo = 0 if wi == 0 else NOC1
                cnt_el = float(n * epp * 128)
                with nc.Block(s.name("wqs")) as blk:
                    @blk.vector
                    def _(e):
                        e.reduce_sum(s.gup[:, :], wqa[:, lo:lo + n], axis=AX.X)
                        s.vsync(e)
                        e.tensor_copy(wqa_bf[:, :], s.gup[:, :]).then_inc(
                            s.sem["ve"], 1)
                        s.cnt["ve"] += 1

                    @blk.tensor
                    def _(e):
                        e.wait_ge(s.sem["ve"], s.tgt("ve"))
                        e.matmul(wq_ps[:, :], s.ones[:, :], wqa_bf[:, :],
                                 start=True, stop=True).then_inc(s.sem["pe"], 1)
                        s.cnt["pe"] += 1

                    @blk.vector
                    def _(e2):
                        e2.wait_ge(s.sem["pe"], s.tgt("pe"))
                        e2.tensor_scalar(m_sb[:, :], wq_ps[:, :], 1.0 / cnt_el,
                                         1e-5, op0=ALU.mult, op1=ALU.max)
                        s.vsync(e2)
                        e2.reciprocal(r_sb[:, :], m_sb[:, :])

            with nc.Block("wqg") as blk:
                @blk.vector
                def _(e):
                    e.tensor_mul(s.gup[:, :], s.wu_m[:, :], s.wu_m[:, :])
                    s.vsync(e)
                    e.tensor_mul(s.gup[:, :], s.gup[:, :], s.wd_m[:, :])
                    s.vsync(e)
                    e.tensor_scalar_mul(s.gup[:, :], s.gup[:, :], 1.0 / 131072.0)
                    s.vsync(e)

            with nc.Block("wq2") as blk:
                se0 = s.cnt["se"]
                ve0 = s.cnt["ve"]
                sa0 = s.cnt["sa"]
                sb0 = s.cnt["sb"]
                ld_tgt = {}

                @blk.gpsimd
                def _(e):
                    i = 0
                    for (wh, _sc, n, epp, _m, r_sb) in specs:
                        for ch in range(n):
                            if i >= 2:
                                e.wait_ge(s.sem["ve"], ve0 + i - 1)
                            q = ("da", "db")[i % 2]
                            src = wh[ch].rearrange("(a p) o -> p a o", p=128)
                            dst = wqb[:, (i % 2) * 4096:(i % 2) * 4096 + epp] \
                                .rearrange("p (a o) -> p a o", o=128)
                            s.dma(e, dst, src, q)
                            ld_tgt[i] = (q, s.cnt[q])
                            i += 1

                @blk.scalar
                def _(e):
                    i = 0
                    for (wh, _sc, n, epp, _m, r_sb) in specs:
                        for ch in range(n):
                            e.wait_ge(s.sem[ld_tgt[i][0]], ld_tgt[i][1])
                            if i >= 2:
                                e.wait_ge(s.sem["ve"], ve0 + i - 1)
                            sl = wqb[:, (i % 2) * 4096:(i % 2) * 4096 + epp]
                            e.activation(sl, sl, AF.Identity,
                                         scale=r_sb[:, :]).then_inc(s.sem["se"], 1)
                            s.cnt["se"] += 1
                            i += 1

                @blk.vector
                def _(e):
                    i = 0
                    for (wh, _sc, n, epp, _m, _r) in specs:
                        for ch in range(n):
                            e.wait_ge(s.sem["se"], se0 + i + 1)
                            if i >= 2:
                                par = i % 2
                                base = sa0 if par == 0 else sb0
                                e.wait_ge(s.sem[("sa", "sb")[par]],
                                          base + 16 * ((i - 2) // 2 + 1))
                            fsl = wqb[:, (i % 2) * 4096:(i % 2) * 4096 + epp]
                            tsl = wqt[:, (i % 2) * 4096:(i % 2) * 4096 + epp]
                            risl = wqi[:, 0:epp]
                            e.tensor_scalar(fsl, fsl, -1.0, 1.0,
                                            op0=ALU.max, op1=ALU.min)
                            s.vsync(e)
                            e.tensor_scalar(risl, fsl, 512.0, None, op0=ALU.add)
                            s.vsync(e)
                            e.scalar_tensor_tensor(fsl, risl, -1.0, fsl,
                                                   op0=ALU.mult, op1=ALU.add)
                            s.vsync(e)
                            e.tensor_scalar(fsl, fsl, -511.5, None, op0=ALU.is_ge)
                            s.vsync(e)
                            e.scalar_tensor_tensor(tsl, risl, -512.0, fsl,
                                                   op0=ALU.add, op1=ALU.add) \
                                .then_inc(s.sem["ve"], 1)
                            s.cnt["ve"] += 1
                            i += 1

                @blk.sync
                def _(e):
                    i = 0
                    for (wh, scr, n, epp, _m, _r) in specs:
                        for ch in range(n):
                            e.wait_ge(s.sem["ve"], ve0 + i + 1)
                            dst = scr[ch].rearrange("(a p) o -> p a o", p=128)
                            src = wqt[:, (i % 2) * 4096:(i % 2) * 4096 + epp] \
                                .rearrange("p (a o) -> p a o", o=128)
                            s.dma(e, dst, src, ("sa", "sb")[i % 2])
                            i += 1
                    e.wait_ge(s.sem["sa"], s.tgt("sa"))
                    e.wait_ge(s.sem["sb"], s.tgt("sb"))

    def _ffn_sl(self, i):
        return self.kq[:, i * 512:(i + 1) * 512]

    def _fwht_butterflies(self, slots, spares, nbits):
        nc, s = self.nc, self
        spc = list(spares)
        with nc.Block(s.name("bfly")) as blk:
            @blk.vector
            def _(e):
                n = len(slots)
                for lvl in range(nbits):
                    st = 1 << lvl
                    pairs = [(a, a | st) for a in range(n) if not (a & st)]
                    for gi in range(0, len(pairs), 4):
                        if lvl > 0 or gi > 0:
                            s.vsync(e)
                        freed = []
                        for (a, b) in pairs[gi:gi + 4]:
                            sa, sb_ = slots[a], slots[b]
                            sp1 = spc.pop(0)
                            sp2 = spc.pop(0)
                            e.tensor_add(s._ffn_sl(sp1), s._ffn_sl(sa),
                                         s._ffn_sl(sb_))
                            e.tensor_sub(s._ffn_sl(sp2), s._ffn_sl(sa),
                                         s._ffn_sl(sb_))
                            slots[a], slots[b] = sp1, sp2
                            freed += [sa, sb_]
                        spc.extend(freed)
        return slots, spc

    def _ffn_h128(self, slots, ps, src_of=None):
        nc, s = self.nc, self
        n = len(slots)
        with nc.Block(s.name("h128")) as blk:
            pe0 = s.cnt["pe"]
            ve0 = s.cnt["ve"]

            @blk.tensor
            def _(e):
                for i in range(n):
                    if i >= 2:
                        e.wait_ge(s.sem["ve"], ve0 + i - 1)
                    src = src_of(i) if src_of else s._ffn_sl(slots[i])
                    e.matmul(ps[:, (i % 2) * 512:(i % 2 + 1) * 512],
                             s.h128[:, :], src, start=True, stop=True,
                             skip_group_check=True).then_inc(s.sem["pe"], 1)
                    s.cnt["pe"] += 1

            @blk.vector
            def _(e):
                for i in range(n):
                    e.wait_ge(s.sem["pe"], pe0 + i + 1)
                    e.tensor_copy(s._ffn_sl(slots[i]),
                                  ps[:, (i % 2) * 512:(i % 2 + 1) * 512]) \
                        .then_inc(s.sem["ve"], 1)
                    s.cnt["ve"] += 1

    def _ffn_quant(self, slots, pt, tm_sb, tmpf):
        nc, s = self.nc, self
        n = len(slots)
        nw = (n + 15) // 16
        ng = (n * 128) // 512
        nh = min(ng, 4)
        nhalves = (ng + nh - 1) // nh
        with (
            nc.sbuf_tensor(s.name("qam"), [128, 2], F32) as am,
            nc.sbuf_tensor(s.name("qrb"), [128, 2048], I32) as rb,
            nc.sbuf_tensor(s.name("qcb"), [128, 2048], BF16) as cb,
            nc.sbuf_tensor(s.name("qc2"), [128, 2048], BF16) as c2,
            nc.Block(s.name("qnt")) as blk,
        ):
            pe0 = s.cnt["pe"]
            ve0 = s.cnt["ve"]
            se0 = s.cnt["se"]
            ve_sub = lambda sub: ve0 + sub * (2 * nw + 1 + nhalves)
            pe_sub = lambda sub: pe0 + sub * (2 * nw)
            se_sub = lambda sub: se0 + sub * ng

            @blk.scalar
            def _(e):
                for sub in range(4):
                    for hv in range(nhalves):
                        g0, g1 = hv * nh, min(ng, (hv + 1) * nh)
                        for g in range(g0, g1):
                            e.wait_ge(s.sem["ve"], ve_sub(sub) + nw + 1 + hv)
                            sl = tm_sb[:, g * 512:(g + 1) * 512]
                            e.activation(tmpf[:, (g - g0) * 512:(g - g0 + 1) * 512],
                                         sl, AF.Identity, bias=s.mg[:, :],
                                         scale=am[:, 0:1]).then_inc(s.sem["se"], 1)
                            s.cnt["se"] += 1

            @blk.tensor
            def _(e):
                for sub in range(4):
                    for w in range(nw):
                        if sub > 0 or w > 0:
                            e.wait_ge(s.sem["ve"], ve_sub(sub) + w)
                        lo = w * 16
                        hi = min(n, lo + 16)
                        for i2 in range(lo, hi):
                            mm = e.transpose(
                                pt[:, (i2 - lo) * 128:(i2 - lo + 1) * 128],
                                s._ffn_sl(slots[i2])[:, sub * 128:(sub + 1) * 128],
                                s.ident[:, :])
                        mm.then_inc(s.sem["pe"], 1)
                        s.cnt["pe"] += 1
                    for w in range(nw):
                        e.wait_ge(s.sem["ve"], ve_sub(sub) + nw + 1 + nhalves + w)
                        lo = w * 16
                        hi = min(n, lo + 16)
                        for i2 in range(lo, hi):
                            mm = e.transpose(
                                pt[:, (i2 - lo) * 128:(i2 - lo + 1) * 128],
                                tm_sb[:, i2 * 128:(i2 + 1) * 128],
                                s.ident[:, :])
                        mm.then_inc(s.sem["pe"], 1)
                        s.cnt["pe"] += 1

            @blk.vector
            def _(e):
                for sub in range(4):
                    for w in range(nw):
                        e.wait_ge(s.sem["pe"], pe_sub(sub) + w + 1)
                        lo = w * 16
                        hi = min(n, lo + 16)
                        e.tensor_copy(tm_sb[:, lo * 128:hi * 128],
                                      pt[:, 0:(hi - lo) * 128])
                        if w == 0:
                            e.reduce_max(am[:, 0:1], pt[:, 0:(hi - lo) * 128],
                                         axis=AX.X, apply_absolute_value=True)
                        else:
                            e.reduce_max(am[:, 1:2], pt[:, 0:(hi - lo) * 128],
                                         axis=AX.X, apply_absolute_value=True)
                            s.vsync(e)
                            e.tensor_max(am[:, 0:1], am[:, 0:1], am[:, 1:2])
                        e.engine_nop().then_inc(s.sem["ve"], 1)
                        s.cnt["ve"] += 1
                    s.vsync(e)
                    e.tensor_scalar(am[:, 0:1], am[:, 0:1], 1e-5, None,
                                    op0=ALU.max)
                    s.vsync(e)
                    e.tensor_scalar(am[:, 1:2], am[:, 0:1], 1.0 / 127.0, None,
                                    op0=ALU.mult)
                    s.vsync(e)
                    e.reciprocal(am[:, 0:1], am[:, 0:1])
                    s.vsync(e)
                    e.tensor_scalar(am[:, 0:1], am[:, 0:1], 127.0, None,
                                    op0=ALU.mult)
                    s.vsync(e)
                    e.engine_nop().then_inc(s.sem["ve"], 1)
                    s.cnt["ve"] += 1
                    for hv in range(nhalves):
                        g0, g1 = hv * nh, min(ng, (hv + 1) * nh)
                        w512 = (g1 - g0) * 512
                        e.wait_ge(s.sem["se"], se_sub(sub) + g1)
                        e.tensor_copy(rb[:, 0:w512], tmpf[:, 0:w512])
                        s.vsync(e)
                        e.tensor_tensor(tmpf[:, 0:w512], tmpf[:, 0:w512],
                                        rb[:, 0:w512], op=ALU.subtract)
                        s.vsync(e)
                        e.tensor_scalar(cb[:, 0:w512], tmpf[:, 0:w512], 0.5,
                                        None, op0=ALU.is_ge)
                        s.vsync(e)
                        e.tensor_scalar(c2[:, 0:w512], tmpf[:, 0:w512], 1.5,
                                        None, op0=ALU.is_ge)
                        s.vsync(e)
                        e.tensor_tensor(cb[:, 0:w512], cb[:, 0:w512],
                                        c2[:, 0:w512], op=ALU.add)
                        s.vsync(e)
                        e.tensor_scalar(c2[:, 0:w512], tmpf[:, 0:w512], -0.5,
                                        None, op0=ALU.is_lt)
                        s.vsync(e)
                        e.tensor_tensor(cb[:, 0:w512], cb[:, 0:w512],
                                        c2[:, 0:w512], op=ALU.subtract)
                        s.vsync(e)
                        e.tensor_scalar(c2[:, 0:w512], tmpf[:, 0:w512], -1.5,
                                        None, op0=ALU.is_lt)
                        s.vsync(e)
                        e.tensor_tensor(cb[:, 0:w512], cb[:, 0:w512],
                                        c2[:, 0:w512], op=ALU.subtract)
                        s.vsync(e)
                        e.tensor_scalar(c2[:, 0:w512], tmpf[:, 0:w512], 2.5,
                                        None, op0=ALU.is_ge)
                        s.vsync(e)
                        e.tensor_tensor(cb[:, 0:w512], cb[:, 0:w512],
                                        c2[:, 0:w512], op=ALU.add)
                        s.vsync(e)
                        e.tensor_scalar(c2[:, 0:w512], tmpf[:, 0:w512], -2.5,
                                        None, op0=ALU.is_lt)
                        s.vsync(e)
                        e.tensor_tensor(cb[:, 0:w512], cb[:, 0:w512],
                                        c2[:, 0:w512], op=ALU.subtract)
                        s.vsync(e)
                        e.tensor_tensor(tmpf[:, 0:w512], rb[:, 0:w512],
                                        cb[:, 0:w512], op=ALU.add)
                        s.vsync(e)
                        for g in range(g0, g1):
                            sl = tm_sb[:, g * 512:(g + 1) * 512]
                            e.tensor_scalar(sl,
                                            tmpf[:, (g - g0) * 512:(g - g0 + 1) * 512],
                                            512.0, am[:, 1:2],
                                            op0=ALU.subtract, op1=ALU.mult)
                        e.engine_nop().then_inc(s.sem["ve"], 1)
                        s.cnt["ve"] += 1
                    for w in range(nw):
                        e.wait_ge(s.sem["pe"], pe_sub(sub) + nw + w + 1)
                        lo = w * 16
                        hi = min(n, lo + 16)
                        for i2 in range(lo, hi):
                            e.tensor_copy(
                                s._ffn_sl(slots[i2])[:, sub * 128:(sub + 1) * 128],
                                pt[:, (i2 - lo) * 128:(i2 - lo + 1) * 128])
                        e.engine_nop().then_inc(s.sem["ve"], 1)
                        s.cnt["ve"] += 1

    def _dbg_tm(self, slots, pt, tm_sb):
        nc, s = self.nc, self
        n = len(slots)
        nw = (n + 15) // 16
        with (
            nc.sbuf_tensor(s.name("qam"), [128, 2], F32) as am,
            nc.Block(s.name("dbgtm")) as blk,
        ):
            pe0 = s.cnt["pe"]
            ve0 = s.cnt["ve"]

            @blk.tensor
            def _(e):
                for w in range(nw):
                    if w > 0:
                        e.wait_ge(s.sem["ve"], ve0 + w)
                    lo = w * 16
                    hi = min(n, lo + 16)
                    for i in range(lo, hi):
                        mm = e.transpose(
                            pt[:, (i - lo) * 128:(i - lo + 1) * 128],
                            s._ffn_sl(slots[i])[:, 0:128],
                            s.ident[:, :])
                    mm.then_inc(s.sem["pe"], 1)
                    s.cnt["pe"] += 1

            @blk.vector
            def _(e):
                for w in range(nw):
                    e.wait_ge(s.sem["pe"], pe0 + w + 1)
                    lo = w * 16
                    hi = min(n, lo + 16)
                    e.tensor_copy(tm_sb[:, lo * 128:hi * 128],
                                  pt[:, 0:(hi - lo) * 128])
                    if w == 0:
                        e.reduce_max(am[:, 0:1], pt[:, 0:(hi - lo) * 128],
                                     axis=AX.X, apply_absolute_value=True)
                    else:
                        e.reduce_max(am[:, 1:2], pt[:, 0:(hi - lo) * 128],
                                     axis=AX.X, apply_absolute_value=True)
                        e.tensor_max(am[:, 0:1], am[:, 0:1], am[:, 1:2])
                    e.engine_nop().then_inc(s.sem["ve"], 1)
                    s.cnt["ve"] += 1
                if self.upto == "ffn_tm2":
                    tmpf = self._dbg_tmpf
                    e.tensor_scalar(am[:, 0:1], am[:, 0:1], 1e-5, None,
                                    op0=ALU.max)
                    e.tensor_scalar(am[:, 1:2], am[:, 0:1], 1.0 / 127.0, None,
                                    op0=ALU.mult)
                    e.reciprocal(am[:, 0:1], am[:, 0:1])
                    e.tensor_scalar(am[:, 0:1], am[:, 0:1], 127.0, None,
                                    op0=ALU.mult)
                    for g in range((n * 128) // 512):
                        sl = tm_sb[:, g * 512:(g + 1) * 512]
                        e.tensor_scalar(tmpf[:, :], sl, am[:, 0:1], MAGIC,
                                        op0=ALU.mult, op1=ALU.add)
                        e.tensor_scalar(sl, tmpf[:, :], MAGIC, am[:, 1:2],
                                        op0=ALU.subtract, op1=ALU.mult)
                e.tensor_copy(s.kq[:, 27000:27000 + n * 128], tm_sb[:, 0:n * 128])
                e.tensor_copy(s.kq[:, 31500:31502], am[:, 0:2])

    def l0_ffn(self):
        nc, s = self.nc, self
        if s.kq is None:
            s.kq = s.ctx.enter_context(nc.sbuf_tensor("kq", [128, 16 * T], BF16))
        if self.upto.startswith("ffn_"):
            with nc.Block(s.name("dbg0")) as blk:
                @blk.gpsimd
                def _(e):
                    e.memset(s.kq[:, :], 0)
        self.l0_ffn_wquant()
        if self.upto == "ffn_w":
            with nc.Block("dbgw") as blk:
                @blk.vector
                def _(e):
                    e.tensor_copy(s.kq[:, 27000:27001], s.wu_m[:, :])
                    e.tensor_copy(s.kq[:, 27001:27002], s.wd_m[:, :])
                    e.tensor_copy(s.kq[:, 27002:27003], s.wu_r[:, :])
                    e.tensor_copy(s.kq[:, 27003:27004], s.wd_r[:, :])
                    e.tensor_copy(s.kq[:, 27004:27005], s.gup[:, :])

                @blk.sync
                def _(e):
                    dst = s.kq[:, 27856:27856 + 2048] \
                        .rearrange("p (a o) -> p a o", o=128)
                    src2 = s.wup_q[0].rearrange("(a p) o -> p a o", p=128)
                    s.dma(e, dst, src2, "qs")

            with nc.Block("dbgw2") as blk:
                @blk.vector
                def _(e):
                    e.wait_ge(s.sem["qs"], s.tgt("qs"))
            return
        FW = list(range(24))
        HS = list(range(24, 64))
        with (
            nc.sbuf_tensor("f_wb", [128, 2 * NDT * 128], BF16) as f_wb,
            nc.sbuf_tensor("f_db", [128, 2 * NOC1 * 128], BF16) as f_db,
            nc.sbuf_tensor("f_tmp", [128, 2048], F32) as f_tmp,
            nc.sbuf_tensor("f_rl", [128, 1024], BF16) as f_rl,
            nc.sbuf_tensor("f_tm", [128, NOC1 * 128], BF16) as f_tm,
            nc.psum_tensor("f_pa", [128, 1024], F32) as f_pa,
            nc.psum_tensor("f_pt", [128, 2048], BF16) as f_pt,
        ):
            for cc in range(4):
                c0 = cc * 512
                fw = FW[:16]
                sp = FW[16:24]
                self._ffn_h128(fw, f_pa, src_of=lambda i: s.xt(i, c0, 512))
                fw, sp = self._fwht_butterflies(fw, sp, 4)
                if self.upto in ("ffn_tm1", "ffn_tm2") and cc == 0:
                    self._dbg_tmpf = f_tmp
                    self._dbg_tm(fw, f_pt, f_tm)
                    self.dbg_slots = list(fw)
                    return
                if self.upto == "ffn_fwht1" and cc == 0:
                    self.dbg_slots = list(fw)
                    return
                self._ffn_quant(fw, f_pt, f_tm, f_tmp)
                if self.upto == "ffn_q1" and cc == 0:
                    self.dbg_slots = list(fw)
                    return

                hs = HS[:32]
                hsp = HS[32:40]
                with nc.Block(s.name("fup")) as blk:
                    pe0 = s.cnt["pe"]
                    ve0 = s.cnt["ve"]
                    ld_tgt = {}

                    @blk.scalar
                    def _(e):
                        for oc in range(NOC1):
                            if oc >= 2:
                                e.wait_ge(s.sem["pe"], pe0 + oc - 1)
                            q = ("dc", "dd")[oc % 2]
                            src = s.wup_q[oc].rearrange("(a p) o -> p a o", p=128)
                            dst = f_wb[:, (oc % 2) * 2048:(oc % 2 + 1) * 2048] \
                                .rearrange("p (a o) -> p a o", o=128)
                            s.dma(e, dst, src, q)
                            ld_tgt[oc] = (q, s.cnt[q])

                    @blk.tensor
                    def _(e):
                        for oc in range(NOC1):
                            e.wait_ge(s.sem[ld_tgt[oc][0]], ld_tgt[oc][1])
                            if oc >= 2:
                                e.wait_ge(s.sem["ve"], ve0 + oc - 1)
                            wb = (oc % 2) * 2048
                            for dt in range(NDT):
                                mm = e.matmul(
                                    f_pa[:, (oc % 2) * 512:(oc % 2 + 1) * 512],
                                    f_wb[:, wb + dt * 128: wb + (dt + 1) * 128],
                                    s._ffn_sl(fw[dt]),
                                    start=(dt == 0), stop=(dt == NDT - 1),
                                    skip_group_check=True)
                            mm.then_inc(s.sem["pe"], 1)
                            s.cnt["pe"] += 1

                    @blk.vector
                    def _(e):
                        for oc in range(NOC1):
                            e.wait_ge(s.sem["pe"], pe0 + oc + 1)
                            psl = f_pa[:, (oc % 2) * 512:(oc % 2 + 1) * 512]
                            rl = f_rl[:, (oc % 2) * 512:(oc % 2 + 1) * 512]
                            e.tensor_scalar(rl, psl, 0.0, None, op0=ALU.max)
                            s.vsync(e)
                            e.tensor_tensor(s._ffn_sl(hs[oc]), rl,
                                            rl, op=ALU.mult).then_inc(
                                s.sem["ve"], 1)
                            s.cnt["ve"] += 1

                if self.upto == "ffn_up" and cc == 0:
                    self.dbg_slots = list(hs)
                    return
                self._ffn_h128(hs, f_pa)
                hs, hsp = self._fwht_butterflies(hs, hsp, 5)
                self._ffn_quant(hs, f_pt, f_tm, f_tmp)

                with nc.Block(s.name("fdn")) as blk:
                    pe0 = s.cnt["pe"]
                    ve0 = s.cnt["ve"]
                    ld_tgt = {}

                    @blk.scalar
                    def _(e):
                        for ct in range(16):
                            if ct >= 2:
                                e.wait_ge(s.sem["pe"], pe0 + ct - 1)
                            q = ("dc", "dd")[ct % 2]
                            src = s.wdn_q[ct].rearrange("(a p) o -> p a o", p=128)
                            dst = f_db[:, (ct % 2) * 4096:(ct % 2 + 1) * 4096] \
                                .rearrange("p (a o) -> p a o", o=128)
                            s.dma(e, dst, src, q)
                            ld_tgt[ct] = (q, s.cnt[q])

                    @blk.tensor
                    def _(e):
                        for ct in range(16):
                            e.wait_ge(s.sem[ld_tgt[ct][0]], ld_tgt[ct][1])
                            if ct >= 2:
                                e.wait_ge(s.sem["ve"], ve0 + ct - 1)
                            wb = (ct % 2) * 4096
                            for ot in range(NOC1):
                                mm = e.matmul(
                                    f_pa[:, (ct % 2) * 512:(ct % 2 + 1) * 512],
                                    f_db[:, wb + ot * 128: wb + (ot + 1) * 128],
                                    s._ffn_sl(hs[ot]),
                                    start=(ot == 0), stop=(ot == NOC1 - 1),
                                    skip_group_check=True)
                            mm.then_inc(s.sem["pe"], 1)
                            s.cnt["pe"] += 1

                    @blk.vector
                    def _(e):
                        for ct in range(16):
                            e.wait_ge(s.sem["pe"], pe0 + ct + 1)
                            psl = f_pa[:, (ct % 2) * 512:(ct % 2 + 1) * 512]
                            xt = s.xt(ct, c0, 512)
                            e.scalar_tensor_tensor(
                                xt, psl, s.gup[:, :], xt,
                                op0=ALU.mult, op1=ALU.add).then_inc(s.sem["ve"], 1)
                            s.cnt["ve"] += 1

        for cc in range(4):
            self.layernorm_chunk(cc * 512)

    def _l1_rotary(self, tgt, c0, ncols, qsh):
        nc, s = self.nc, self
        for hg in range(2):
            with nc.Block(s.name("rot")) as blk:
                qs0 = s.cnt["qs"]

                @blk.sync
                def _(e):
                    for h8 in range(8):
                        h = hg * 8 + h8
                        sl = slice(h * T + c0, h * T + c0 + ncols)
                        d = slice(h8 * ncols, (h8 + 1) * ncols)
                        s.dma(e, qsh[0:16, d], tgt[16:32, sl], "qs")
                        s.dma(e, qsh[16:32, d], tgt[0:16, sl], "qs")

                @blk.vector
                def _(e):
                    e.wait_ge(s.sem["qs"], s.tgt("qs"))
                    sview = s.rotS[:, c0:c0 + ncols].unsqueeze(1) \
                        .broadcast_to([32, 8, ncols])
                    qview = qsh[:, 0:8 * ncols].rearrange("p (h t) -> p h t", h=8)
                    e.tensor_tensor(qview, qview, sview, op=ALU.mult)
                    for h8 in range(8):
                        h = hg * 8 + h8
                        sl = slice(h * T + c0, h * T + c0 + ncols)
                        rot = tgt[0:32, sl]
                        e.tensor_tensor(rot, rot, s.rotC[:, c0:c0 + ncols],
                                        op=ALU.mult)
                    s.vsync(e)
                    for h8 in range(8):
                        h = hg * 8 + h8
                        sl = slice(h * T + c0, h * T + c0 + ncols)
                        rot = tgt[0:32, sl]
                        e.tensor_tensor(rot, rot,
                                        qsh[:, h8 * ncols:(h8 + 1) * ncols],
                                        op=ALU.add)

    def _l1_proj_fm(self, wcol0, scale, store_scr):
        nc, s = self.nc, self
        with (
            nc.sbuf_tensor(s.name("pw"), [128, 2 * 16 * 128], BF16) as pw,
            nc.psum_tensor(s.name("pp"), [128, 4096], F32) as pp,
            nc.Block(s.name("proj")) as blk,
        ):
            WB = 16 * 128
            pe0 = s.cnt["pe"]
            ve0 = s.cnt["ve"]
            ld_tgt = {}

            @blk.gpsimd
            def _(e):
                for h in range(16):
                    if h >= 2:
                        e.wait_ge(s.sem["pe"], pe0 + h - 1)
                    q = ("da", "db")[h % 2]
                    src = s.wqkv[:, wcol0 + h * 128: wcol0 + (h + 1) * 128] \
                        .rearrange("(dt p) i -> p dt i", p=128)
                    dst = pw[:, (h % 2) * WB:(h % 2 + 1) * WB] \
                        .rearrange("p (dt i) -> p dt i", dt=NDT)
                    s.dma(e, dst, src, q)
                    ld_tgt[h] = (q, s.cnt[q])

            @blk.tensor
            def _(e):
                for h in range(16):
                    e.wait_ge(s.sem[ld_tgt[h][0]], ld_tgt[h][1])
                    if h >= 2:
                        e.wait_ge(s.sem["ve"], ve0 + h - 1)
                    wb = (h % 2) * WB
                    pb = (h % 2) * 2048
                    for dt in range(NDT):
                        for jc in range(4):
                            mm = e.matmul(
                                pp[:, pb + jc * 512: pb + (jc + 1) * 512],
                                pw[:, wb + dt * 128: wb + (dt + 1) * 128],
                                s.xt(dt, jc * 512, 512),
                                start=(dt == 0), stop=(dt == NDT - 1),
                                skip_group_check=True)
                    mm.then_inc(s.sem["pe"], 1)
                    s.cnt["pe"] += 1

            @blk.vector
            def _(e):
                for h in range(16):
                    e.wait_ge(s.sem["pe"], pe0 + h + 1)
                    pb = (h % 2) * 2048
                    if scale is None:
                        e.tensor_copy(s.kq[:, h * T:(h + 1) * T],
                                      pp[:, pb:pb + 2048])
                    else:
                        e.tensor_scalar(s.kq[:, h * T:(h + 1) * T],
                                        pp[:, pb:pb + 2048],
                                        s.qscl[:, 0:1], None, op0=ALU.mult)
                    e.engine_nop().then_inc(s.sem["ve"], 1)
                    s.cnt["ve"] += 1

        if not self.upto.endswith("_norot"):
            with nc.sbuf_tensor(s.name("qsh"), [32, 8 * T], BF16) as qsh:
                self._l1_rotary(s.kq, 0, T, qsh)

        if store_scr and not self.upto.startswith("l1x_kraw"):
            with nc.Block(s.name("kst")) as blk:
                @blk.sync
                def _(e):
                    for h in range(16):
                        s.dma(e, s.k_scr[h], s.kq[:, h * T:(h + 1) * T], "qs")
                    e.wait_ge(s.sem["qs"], s.tgt("qs"))

    def l1_attn(self):
        nc, s = self.nc, self
        QSC = 1.0 / np.sqrt(128.0)
        rot_ctx = ExitStack()
        s.rotC = rot_ctx.enter_context(nc.sbuf_tensor("rotC", [32, T], F32))
        s.rotS = rot_ctx.enter_context(nc.sbuf_tensor("rotS", [32, T], F32))
        s.mask4 = rot_ctx.enter_context(nc.sbuf_tensor("mask4", [128, 4 * 512], BF16))
        lo_ctx = ExitStack()
        rotCl = lo_ctx.enter_context(nc.sbuf_tensor("rotCl", [32, T], F32))
        rotSl = lo_ctx.enter_context(nc.sbuf_tensor("rotSl", [32, T], F32))
        with nc.Block("l1ld") as blk:
            @blk.sync
            def _(e):
                s.dma(e, s.rotC[:, :], s.rotC_h[:, :], "qs")
                s.dma(e, s.rotS[:, :], s.rotS_h[:, :], "qs")
                s.dma(e, rotCl[:, :], s.rotCl_h[:, :], "qs")
                s.dma(e, rotSl[:, :], s.rotSl_h[:, :], "qs")
                s.dma(e, s.mask4[:, :], s.mask_h[:, :], "qs")

            @blk.vector
            def _(e):
                e.wait_ge(s.sem["qs"], s.tgt("qs"))
                e.tensor_tensor(s.rotC[:, :], s.rotC[:, :], rotCl[:, :],
                                op=ALU.add)
                e.tensor_tensor(s.rotS[:, :], s.rotS[:, :], rotSl[:, :],
                                op=ALU.add)
        lo_ctx.close()

        self._l1_proj_fm(D, None, True)
        if self.upto.startswith("l1x_k"):
            return
        self._l1_proj_fm(0, 'qscl', False)
        if self.upto == "l1x_q":
            return

        with (
            nc.sbuf_tensor("vw", [128, 2 * 16 * 512], BF16) as vw,
            nc.sbuf_tensor("vtile", [128, 2 * 512], BF16) as vtile,
            nc.psum_tensor("vps", [128, 1024], F32) as vps,
            nc.Block("l1v") as blk,
        ):
            VB = 16 * 512
            pe0 = s.cnt["pe"]
            ve0 = s.cnt["ve"]
            sa0 = s.cnt["sa"]
            sb0 = s.cnt["sb"]
            ld_tgt = {}

            @blk.gpsimd
            def _(e):
                for ic in range(4):
                    if ic >= 2:
                        e.wait_ge(s.sem["pe"], pe0 + (ic - 1) * 16)
                    q = ("da", "db")[ic % 2]
                    src = s.wqkv[:, 4096 + ic * 512: 4096 + (ic + 1) * 512] \
                        .rearrange("(dt p) i -> p dt i", p=128)
                    dst = vw[:, (ic % 2) * VB:(ic % 2 + 1) * VB] \
                        .rearrange("p (dt i) -> p dt i", dt=NDT)
                    s.dma(e, dst, src, q)
                    ld_tgt[ic] = (q, s.cnt[q])
                for dt in range(NDT):
                    s.dma(e, s.x_scr[dt], s.xs[:, dt * T:(dt + 1) * T], "qg")

            @blk.tensor
            def _(e):
                for ic in range(4):
                    e.wait_ge(s.sem[ld_tgt[ic][0]], ld_tgt[ic][1])
                    wb = (ic % 2) * VB
                    for tt in range(16):
                        it = ic * 16 + tt
                        if it >= 2:
                            e.wait_ge(s.sem["ve"], ve0 + it - 1)
                        for dt in range(NDT):
                            mm = e.matmul(
                                vps[:, (it % 2) * 512:(it % 2 + 1) * 512],
                                s.xt(dt, tt * 128, 128),
                                vw[:, wb + dt * 512: wb + (dt + 1) * 512],
                                start=(dt == 0), stop=(dt == NDT - 1),
                                skip_group_check=True)
                        mm.then_inc(s.sem["pe"], 1)
                        s.cnt["pe"] += 1

            @blk.vector
            def _(e):
                for ic in range(4):
                    for tt in range(16):
                        it = ic * 16 + tt
                        e.wait_ge(s.sem["pe"], pe0 + it + 1)
                        if it >= 2:
                            par = it % 2
                            base = sa0 if par == 0 else sb0
                            e.wait_ge(s.sem[("sa", "sb")[par]],
                                      base + 16 * ((it - 2) // 2 + 1))
                        e.tensor_copy(vtile[:, (it % 2) * 512:(it % 2 + 1) * 512],
                                      vps[:, (it % 2) * 512:(it % 2 + 1) * 512]) \
                            .then_inc(s.sem["ve"], 1)
                        s.cnt["ve"] += 1

            @blk.sync
            def _(e):
                for ic in range(4):
                    for tt in range(16):
                        it = ic * 16 + tt
                        e.wait_ge(s.sem["ve"], ve0 + it + 1)
                        s.dma(e, s.v_scr[tt][:, ic * 512:(ic + 1) * 512],
                              vtile[:, (it % 2) * 512:(it % 2 + 1) * 512],
                              ("sa", "sb")[it % 2])
                e.wait_ge(s.sem["sa"], s.tgt("sa"))
                e.wait_ge(s.sem["sb"], s.tgt("sb"))
                e.wait_ge(s.sem["qg"], s.tgt("qg"))

        if self.upto == "l1x_v":
            return
        kbuf = lambda h8, j: s.xs[:, h8 * T + j * 128: h8 * T + (j + 1) * 128]
        vbuf = lambda j, h8: s.xs[:, 16 * 1024 + j * 1024 + h8 * 128:
                                  16 * 1024 + j * 1024 + (h8 + 1) * 128]
        with (
            nc.sbuf_tensor("pbuf", [128, 2 * 16 * 512], BF16) as pbuf,
            nc.sbuf_tensor("rec_c", [128, 2 * 512], F32) as rec_c,
            nc.psum_tensor("sc_ps", [128, 2048], F32) as sc_ps,
            nc.psum_tensor("sum_ps", [128, 512], F32) as sum_ps,
            nc.psum_tensor("av_ps", [128, 512], F32) as av_ps,
            nc.Block("l1att") as blk,
        ):
            PB = 16 * 512
            pe0 = s.cnt["pe"]
            ve0 = s.cnt["ve"]
            se0 = s.cnt["se"]
            qs0 = s.cnt["qs"]
            qc0 = s.cnt["qc"]
            kload = []
            vload = []

            @blk.sync
            def _(e):
                for c in range(4):
                    lim = 4 * (c + 1)
                    for hg in range(2):
                        idx = c * 2 + hg
                        if idx >= 1:
                            e.wait_ge(s.sem["pe"], pe0 + sum(
                                (4 * (cc + 1) + 3) // 4 + 2
                                for cc in range(4) for gg in range(2)
                                for _h in range(8)
                                if cc * 2 + gg < idx))
                        for h8 in range(8):
                            h = hg * 8 + h8
                            s.dma(e, s.xs[:, h8 * T: h8 * T + lim * 128],
                                  s.k_scr[h][:, 0:lim * 128], "qs")
                        kload.append(s.tgt("qs"))

            @blk.scalar
            def _(e):
                for c in range(4):
                    lim = 4 * (c + 1)
                    nwave = (lim + 3) // 4
                    for hg in range(2):
                        idx = c * 2 + hg
                        if idx >= 1:
                            e.wait_ge(s.sem["pe"], pe0 + sum(
                                (4 * (cc + 1) + 3) // 4 + 2
                                for cc in range(4) for gg in range(2)
                                for _h in range(8)
                                if cc * 2 + gg < idx))
                        for j in range(lim):
                            src = s.v_scr[j][:, hg * 1024:(hg + 1) * 1024]
                            dst = s.xs[:, 16 * 1024 + j * 1024:
                                       16 * 1024 + (j + 1) * 1024]
                            s.dma(e, dst, src, "qc")
                        vload.append(s.tgt("qc"))
                        for h8 in range(8):
                            hi = idx * 8 + h8
                            if hi >= 2:
                                pc, pg, ph = self._l1_head_seq(hi - 2)
                                e.wait_ge(s.sem["pe"], pe0 +
                                          self._l1_pe_before(pc, pg, ph) +
                                          (pc + 1) + 2)
                            for w in range(nwave):
                                e.wait_ge(s.sem["pe"],
                                          pe0 + self._l1_pe_before(c, hg, h8) + w + 1)
                                j0 = w * 4
                                j1 = min(lim, j0 + 4)
                                e.activation(
                                    pbuf[:, (h8 % 2) * PB + j0 * 512:
                                         (h8 % 2) * PB + j1 * 512],
                                    sc_ps[:, 0:(j1 - j0) * 512],
                                    AF.Exp).then_inc(s.sem["se"], 1)
                                s.cnt["se"] += 1

            @blk.tensor
            def _(e):
                for c in range(4):
                    lim = 4 * (c + 1)
                    nwave = (lim + 3) // 4
                    for hg in range(2):
                        idx = c * 2 + hg
                        e.wait_ge(s.sem["qs"], kload[idx])
                        e.wait_ge(s.sem["qc"], vload[idx])
                        for h8 in range(8):
                            h = hg * 8 + h8
                            for w in range(nwave):
                                j0 = w * 4
                                j1 = min(lim, j0 + 4)
                                if not (idx == 0 and h8 == 0 and w == 0):
                                    e.wait_ge(s.sem["se"], se0 +
                                              self._l1_se_before(c, hg, h8) + w)
                                for j in range(j0, j1):
                                    mm = e.matmul(
                                        sc_ps[:, (j - j0) * 512:(j - j0 + 1) * 512],
                                        kbuf(h8, j),
                                        s.kq[:, h * T + c * 512: h * T + c * 512 + 512],
                                        start=True, stop=True,
                                        skip_group_check=True)
                                mm.then_inc(s.sem["pe"], 1)
                                s.cnt["pe"] += 1
                            e.wait_ge(s.sem["ve"], ve0 + idx * 16 + h8 * 2 + 1)
                            for j in range(lim):
                                mm = e.matmul(
                                    sum_ps[:, :], s.ones[:, :],
                                    pbuf[:, (h8 % 2) * PB + j * 512:
                                         (h8 % 2) * PB + (j + 1) * 512],
                                    start=(j == 0), stop=(j == lim - 1),
                                    skip_group_check=True)
                            mm.then_inc(s.sem["pe"], 1)
                            s.cnt["pe"] += 1
                            for j in range(lim):
                                mm = e.matmul(
                                    av_ps[:, :], vbuf(j, h8),
                                    pbuf[:, (h8 % 2) * PB + j * 512:
                                         (h8 % 2) * PB + (j + 1) * 512],
                                    start=(j == 0), stop=(j == lim - 1),
                                    skip_group_check=True)
                            mm.then_inc(s.sem["pe"], 1)
                            s.cnt["pe"] += 1

            @blk.vector
            def _(e):
                for c in range(4):
                    lim = 4 * (c + 1)
                    nwave = (lim + 3) // 4
                    for hg in range(2):
                        idx = c * 2 + hg
                        for h8 in range(8):
                            h = hg * 8 + h8
                            e.wait_ge(s.sem["se"], se0 +
                                      self._l1_se_before(c, hg, h8) + nwave)
                            for r in range(4):
                                j = 4 * c + r
                                if j < lim:
                                    pslice = pbuf[:, (h8 % 2) * PB + j * 512:
                                                  (h8 % 2) * PB + (j + 1) * 512]
                                    e.tensor_tensor(
                                        pslice, pslice,
                                        s.mask4[:, r * 512:(r + 1) * 512],
                                        op=ALU.mult)
                            e.engine_nop().then_inc(s.sem["ve"], 1)
                            s.cnt["ve"] += 1
                            e.wait_ge(s.sem["pe"], pe0 +
                                      self._l1_pe_before(c, hg, h8) + nwave + 1)
                            e.reciprocal(rec_c[:, (h8 % 2) * 512:
                                               (h8 % 2 + 1) * 512], sum_ps[:, :])
                            s.vsync(e)
                            e.wait_ge(s.sem["pe"], pe0 +
                                      self._l1_pe_before(c, hg, h8) + nwave + 2)
                            e.tensor_tensor(
                                s.kq[:, h * T + c * 512: h * T + c * 512 + 512],
                                av_ps[:, :],
                                rec_c[:, (h8 % 2) * 512:(h8 % 2 + 1) * 512],
                                op=ALU.mult)
                            e.engine_nop().then_inc(s.sem["ve"], 1)
                            s.cnt["ve"] += 1

        if self.upto == "l1x_att":
            return
        with (
            nc.sbuf_tensor("ow_s", [128, 2 * 16 * 128], BF16) as ow_s,
            nc.psum_tensor("op2_ps", [128, 1024], F32) as op2_ps,
            nc.Block("l1o") as blk,
        ):
            WB = 16 * 128
            pe0 = s.cnt["pe"]
            ve0 = s.cnt["ve"]
            sa0 = s.cnt["sa"]
            sb0 = s.cnt["sb"]
            ld_tgt = {}

            @blk.gpsimd
            def _(e):
                for ot in range(16):
                    if ot >= 2:
                        e.wait_ge(s.sem["pe"], pe0 + (ot - 1) * 4)
                    q = ("da", "db")[ot % 2]
                    src = s.wout[:, ot * 128:(ot + 1) * 128] \
                        .rearrange("(it p) o -> p it o", p=128)
                    dst = ow_s[:, (ot % 2) * WB:(ot % 2 + 1) * WB] \
                        .rearrange("p (it o) -> p it o", it=NDT)
                    s.dma(e, dst, src, q)
                    ld_tgt[ot] = (q, s.cnt[q])

            @blk.sync
            def _(e):
                for ot in range(16):
                    if ot >= 2:
                        e.wait_ge(s.sem["ve"], ve0 + (ot - 1) * 4)
                    s.dma(e, s.xs[:, ot * T:(ot + 1) * T], s.x_scr[ot],
                          ("sa", "sb")[ot % 2])

            @blk.tensor
            def _(e):
                for ot in range(16):
                    e.wait_ge(s.sem[ld_tgt[ot][0]], ld_tgt[ot][1])
                    wb = (ot % 2) * WB
                    for ch in range(4):
                        it2 = ot * 4 + ch
                        pslice = op2_ps[:, (it2 % 2) * 512:(it2 % 2 + 1) * 512]
                        if it2 >= 2:
                            e.wait_ge(s.sem["ve"], ve0 + it2 - 1)
                        for itile in range(16):
                            mm = e.matmul(
                                pslice,
                                ow_s[:, wb + itile * 128: wb + (itile + 1) * 128],
                                s.kqt(itile, ch * 512, 512),
                                start=(itile == 0), stop=(itile == NDT - 1),
                                skip_group_check=True)
                        mm.then_inc(s.sem["pe"], 1)
                        s.cnt["pe"] += 1

            @blk.vector
            def _(e):
                for ot in range(16):
                    par = ot % 2
                    base = sa0 if par == 0 else sb0
                    e.wait_ge(s.sem[("sa", "sb")[par]],
                              base + 16 * (ot // 2 + 1))
                    for ch in range(4):
                        it2 = ot * 4 + ch
                        pslice = op2_ps[:, (it2 % 2) * 512:(it2 % 2 + 1) * 512]
                        e.wait_ge(s.sem["pe"], pe0 + it2 + 1)
                        xt = s.xt(ot, ch * 512, 512)
                        e.tensor_add(xt, pslice, xt).then_inc(s.sem["ve"], 1)
                        s.cnt["ve"] += 1

        for ch in range(4):
            self.layernorm_chunk(ch * 512)
        rot_ctx.close()

    def _l1_pe_before(self, c, hg, h8):
        tot = 0
        for cc in range(4):
            nw = cc + 1
            for gg in range(2):
                for hh in range(8):
                    if (cc, gg, hh) == (c, hg, h8):
                        return tot
                    tot += nw + 2
        return tot

    @staticmethod
    def _l1_head_seq(k):
        seq = [(cc, gg, hh) for cc in range(4) for gg in range(2)
               for hh in range(8)]
        return seq[k]

    def _l1_se_before(self, c, hg, h8):
        tot = 0
        for cc in range(4):
            nw = cc + 1
            for gg in range(2):
                for hh in range(8):
                    if (cc, gg, hh) == (c, hg, h8):
                        return tot
                    tot += nw
        return tot

    def l1_ffn(self):
        nc, s = self.nc, self

        def gsl(oc, lo, ncols):
            if oc < 32:
                return s.kq[:, oc * 1024 + lo: oc * 1024 + lo + ncols]
            return self._gx[:, (oc - 32) * 1024 + lo: (oc - 32) * 1024 + lo + ncols]

        with (
            nc.sbuf_tensor("gx", [128, 12 * 1024], BF16) as gx,
            nc.sbuf_tensor("gw", [128, 2 * 2048], BF16) as gw,
            nc.sbuf_tensor("uw", [128, 2 * 2048], BF16) as uw,
            nc.sbuf_tensor("dw", [128, 2 * NOC2 * 128], BF16) as dw,
            nc.sbuf_tensor("gt", [128, 2 * 1024], BF16) as gt,
            nc.sbuf_tensor("ut", [128, 2 * 1024], BF16) as ut,
            nc.psum_tensor("gps", [128, 1024], F32) as gps,
            nc.psum_tensor("ups", [128, 1024], F32) as ups,
            nc.psum_tensor("dps", [128, 1024], F32) as dps,
            nc.psum_tensor("xps", [128, 1024], F32) as xps,
        ):
            pairs = ((gps, ups), (dps, xps))
            self._gx = gx
            for tc in range(2):
                t0 = tc * 1024
                with nc.Block(s.name("f2gu")) as blk:
                    pe0 = s.cnt["pe"]
                    ve0 = s.cnt["ve"]
                    se0 = s.cnt["se"]
                    s0_ve2 = s.cnt["ve2"]
                    ld_tgt = {}

                    @blk.gpsimd
                    def _(e):
                        for oc in range(NOC2):
                            if oc >= 2:
                                e.wait_ge(s.sem["pe"], pe0 + 2 * (oc - 1))
                            q = ("da", "db")[oc % 2]
                            for (wh, wsb) in ((s.wg, gw), (s.wu, uw)):
                                src = wh[oc].rearrange("(dt p) o -> p dt o", p=128)
                                dst = wsb[:, (oc % 2) * 2048:(oc % 2 + 1) * 2048] \
                                    .rearrange("p (dt o) -> p dt o", dt=NDT)
                                s.dma(e, dst, src, q)
                            ld_tgt[oc] = (q, s.cnt[q])

                    @blk.vector
                    def _(e):
                        for oc in range(NOC2 + 1):
                            if oc < NOC2:
                                e.wait_ge(s.sem[ld_tgt[oc][0]], ld_tgt[oc][1])
                                if oc >= 2:
                                    e.wait_ge(s.sem["pe"], pe0 + 2 * (oc - 1))
                                for wsb in (gw, uw):
                                    sl = wsb[:, (oc % 2) * 2048:(oc % 2 + 1) * 2048]
                                    e.tensor_scalar(sl, sl, 128.0, None,
                                                    op0=ALU.subtract)
                                e.engine_nop().then_inc(s.sem["ve"], 1)
                                s.cnt["ve"] += 1
                            if oc >= 1:
                                po = oc - 1
                                e.wait_ge(s.sem["se"], se0 + 2 * (po + 1))
                                e.tensor_tensor(gsl(po, 0, 1024),
                                                gt[:, (po % 2) * 1024:(po % 2 + 1) * 1024],
                                                ut[:, (po % 2) * 1024:(po % 2 + 1) * 1024],
                                                op=ALU.mult)
                                e.engine_nop().then_inc(s.sem["ve2"], 1)
                                s.cnt["ve2"] += 1

                    @blk.tensor
                    def _(e):
                        for oc in range(NOC2):
                            e.wait_ge(s.sem["ve"], ve0 + oc + 1)
                            if oc >= 2:
                                e.wait_ge(s.sem["se"], se0 + 2 * (oc - 1))
                            gp, up = pairs[oc % 2]
                            wb = (oc % 2) * 2048
                            for dt in range(NDT):
                                for nn in range(2):
                                    e.matmul(
                                        gp[:, nn * 512:(nn + 1) * 512],
                                        gw[:, wb + dt * 128: wb + (dt + 1) * 128],
                                        s.xt(dt, t0 + nn * 512, 512),
                                        start=(dt == 0), stop=(dt == NDT - 1),
                                        skip_group_check=True)
                                    mm = e.matmul(
                                        up[:, nn * 512:(nn + 1) * 512],
                                        uw[:, wb + dt * 128: wb + (dt + 1) * 128],
                                        s.xt(dt, t0 + nn * 512, 512),
                                        start=(dt == 0), stop=(dt == NDT - 1),
                                        skip_group_check=True)
                            mm.then_inc(s.sem["pe"], 2)
                            s.cnt["pe"] += 2

                    @blk.scalar
                    def _(e):
                        for oc in range(NOC2):
                            e.wait_ge(s.sem["pe"], pe0 + 2 * (oc + 1))
                            if oc >= 2:
                                e.wait_ge(s.sem["ve2"], s0_ve2 + oc - 1)
                            gp, up = pairs[oc % 2]
                            e.activation(gt[:, (oc % 2) * 1024:(oc % 2 + 1) * 1024],
                                         gp[:, :], AF.Silu,
                                         scale=s.sg_sb[:, oc:oc + 1]).then_inc(
                                s.sem["se"], 1)
                            s.cnt["se"] += 1
                            e.activation(ut[:, (oc % 2) * 1024:(oc % 2 + 1) * 1024],
                                         up[:, :], AF.Identity,
                                         scale=s.su_sb[:, oc:oc + 1]).then_inc(
                                s.sem["se"], 1)
                            s.cnt["se"] += 1

                with nc.Block(s.name("f2dn")) as blk:
                    pe0 = s.cnt["pe"]
                    ve0 = s.cnt["ve"]
                    ld_tgt = {}

                    @blk.gpsimd
                    def _(e):
                        for ct in range(16):
                            if ct >= 2:
                                e.wait_ge(s.sem["pe"], pe0 + ct - 1)
                            q = ("da", "db")[ct % 2]
                            src = s.wd[ct].rearrange("(ot p) o -> p ot o", p=128)
                            dst = dw[:, (ct % 2) * NOC2 * 128:
                                     (ct % 2 + 1) * NOC2 * 128] \
                                .rearrange("p (ot o) -> p ot o", ot=NOC2)
                            s.dma(e, dst, src, q)
                            ld_tgt[ct] = (q, s.cnt[q])

                    @blk.vector
                    def _(e):
                        for ct in range(17):
                            if ct < 16:
                                e.wait_ge(s.sem[ld_tgt[ct][0]], ld_tgt[ct][1])
                                if ct >= 2:
                                    e.wait_ge(s.sem["pe"], pe0 + ct - 1)
                                sl = dw[:, (ct % 2) * NOC2 * 128:
                                        (ct % 2 + 1) * NOC2 * 128]
                                e.tensor_scalar(sl, sl, 128.0, None,
                                                op0=ALU.subtract)
                                e.engine_nop().then_inc(s.sem["ve"], 1)
                                s.cnt["ve"] += 1
                            if ct >= 1:
                                pc = ct - 1
                                e.wait_ge(s.sem["pe"], pe0 + pc + 1)
                                dp = (dps, xps)[pc % 2]
                                for nn in range(2):
                                    xt = s.xt(pc, t0 + nn * 512, 512)
                                    e.scalar_tensor_tensor(
                                        xt, dp[:, nn * 512:(nn + 1) * 512],
                                        s.sd_sb[:, pc:pc + 1], xt,
                                        op0=ALU.mult, op1=ALU.add)
                                e.engine_nop().then_inc(s.sem["ve2"], 1)
                                s.cnt["ve2"] += 1

                    @blk.tensor
                    def _(e):
                        for ct in range(16):
                            e.wait_ge(s.sem["ve"], ve0 + ct + 1)
                            if ct >= 2:
                                e.wait_ge(s.sem["ve2"],
                                          s.cnt["ve2"] - 16 + ct - 1)
                            dp = (dps, xps)[ct % 2]
                            wb = (ct % 2) * NOC2 * 128
                            for ot in range(NOC2):
                                for nn in range(2):
                                    mm = e.matmul(
                                        dp[:, nn * 512:(nn + 1) * 512],
                                        dw[:, wb + ot * 128: wb + (ot + 1) * 128],
                                        gsl(ot, nn * 512, 512),
                                        start=(ot == 0), stop=(ot == NOC2 - 1),
                                        skip_group_check=True)
                            mm.then_inc(s.sem["pe"], 1)
                            s.cnt["pe"] += 1

            self._gx = None

    def final_out(self):
        nc, s = self.nc, self
        with nc.sbuf_tensor("ostage", [128, 16 * 512], F32) as ostage:
            for ch in range(4):
                self.layernorm_chunk(ch * 512, final=True, out_stage=ostage)
                with nc.Block(s.name("fo")) as blk:
                    @blk.sync
                    def _(e):
                        for dt in range(NDT):
                            s.dma(e, s.outT[dt * 128:(dt + 1) * 128,
                                            ch * 512:(ch + 1) * 512],
                                  ostage[:, dt * 512:(dt + 1) * 512],
                                  ("sa", "sb")[ch % 2])
                        e.wait_ge(s.sem[("sa", "sb")[ch % 2]],
                                  s.tgt(("sa", "sb")[ch % 2]))

    def write_out_from_xs(self, srct=None):
        nc, s = self.nc, self
        if srct is None:
            srct = s.xs
        with (
            nc.sbuf_tensor(s.name("ostage"), [128, 2 * T], F32) as ostage,
            nc.Block(s.name("wout")) as blk,
        ):
            ve0 = s.cnt["ve"]
            sa0 = s.cnt["sa"]
            sb0 = s.cnt["sb"]

            @blk.vector
            def _(e):
                for dt in range(NDT):
                    if dt >= 2:
                        par = dt % 2
                        base = sa0 if par == 0 else sb0
                        e.wait_ge(s.sem[("sa", "sb")[par]],
                                  base + 16 * ((dt - 2) // 2 + 1))
                    e.tensor_copy(ostage[:, (dt % 2) * T:(dt % 2 + 1) * T],
                                  srct[:, dt * T:(dt + 1) * T]).then_inc(
                        s.sem["ve"], 1)
                    s.cnt["ve"] += 1

            @blk.sync
            def _(e):
                for dt in range(NDT):
                    e.wait_ge(s.sem["ve"], ve0 + dt + 1)
                    s.dma(e, s.outT[dt * 128:(dt + 1) * 128, :],
                          ostage[:, (dt % 2) * T:(dt % 2 + 1) * T],
                          ("sa", "sb")[dt % 2])
                e.wait_ge(s.sem["sa"], s.tgt("sa"))
                e.wait_ge(s.sem["sb"], s.tgt("sb"))

    def build(self):
        with self.nc.named_scope("prologue"):
            self.prologue()
        if self.upto.startswith("ffn_"):
            self.l0_ffn()
            self.write_out_from_xs(self.kq)
            self.ctx.close()
            return self.nc
        if self.upto == "l1a_only" or self.upto.startswith("l1x_"):
            if self.kq is None:
                self.kq = self.ctx.enter_context(
                    self.nc.sbuf_tensor("kq", [128, 16 * T], BF16))
            self.l1_attn()
            if self.upto.startswith("l1x_"):
                self.write_out_from_xs(self.kq)
            else:
                self.write_out_from_xs()
            self.ctx.close()
            return self.nc
        with self.nc.named_scope("l0_attn"):
            self.l0_attn()
        if self.upto == "l0a":
            self.write_out_from_xs()
            self.ctx.close()
            return self.nc
        with self.nc.named_scope("l0_ffn"):
            self.l0_ffn()
        if self.upto == "l0f":
            self.write_out_from_xs()
            self.ctx.close()
            return self.nc
        with self.nc.named_scope("l1_attn"):
            self.l1_attn()
        if self.upto == "l1a":
            self.write_out_from_xs()
            self.ctx.close()
            return self.nc
        with self.nc.named_scope("l1_ffn"):
            self.l1_ffn()
        with self.nc.named_scope("final_out"):
            self.final_out()
        self.ctx.close()
        return self.nc


def _host_consts():
    ones = np.ones((128, 128), dtype=ml_dtypes.bfloat16)
    gsel = np.zeros((128, 128), dtype=np.float32)
    gsel[0:64, 0:64] = 1.0
    gsel[64:128, 64:128] = 1.0
    ident = np.eye(128).astype(ml_dtypes.bfloat16)
    h128 = _sylvester(128).astype(ml_dtypes.bfloat16)
    import jax
    import jax.numpy as jnp
    cpu = jax.devices("cpu")[0]
    with jax.default_device(cpu):
        inv_freq = 1.0 / 10000.0 ** (jnp.arange(0, 32, 2,
                                                dtype=jnp.float32) / 32)
        fr_j = jnp.arange(T, dtype=jnp.float32)[:, None] * inv_freq[None, :]
        pos = jnp.concatenate((fr_j, fr_j), axis=-1)
        cosv = np.asarray(jnp.cos(pos)).astype(np.float32)
        sinv = np.asarray(jnp.sin(pos)).astype(np.float32)
    rotC = np.ascontiguousarray(cosv.T)
    sT = sinv.T
    rotS = np.concatenate([-sT[0:16], sT[16:32]], 0).astype(np.float32)
    rotS = np.ascontiguousarray(np.concatenate(
        [-sinv.T[16:32] * 0 - sinv.T[0:16], sinv.T[16:32]], 0)).astype(np.float32)
    rotCl = np.zeros_like(rotC)
    rotSl = np.zeros_like(rotS)
    qs_hi = np.float32(1.0 / np.sqrt(np.float64(128.0)))
    qs_lo = np.float32(1.0 / np.sqrt(np.float64(128.0)) - np.float64(qs_hi))
    misc = np.zeros((128, 4), dtype=np.float32)
    misc[:, 0] = qs_hi
    misc[:, 1] = qs_lo
    m = np.zeros((128, 4, 4, 128), dtype=np.float32)
    for r in range(4):
        for itl in range(4):
            if r < itl:
                m[:, r, itl, :] = 1.0
        m[:, r, r, :] = (np.arange(128)[:, None] <= np.arange(128)[None, :])
    mask = m.reshape(128, 4 * 512).astype(ml_dtypes.bfloat16)
    return dict(ones_h=ones, gsel_h=gsel.astype(ml_dtypes.bfloat16),
                ident_h=ident, h128_h=h128,
                rotC_h=rotC, rotS_h=rotS, rotCl_h=rotCl, rotSl_h=rotSl,
                misc_h=misc, mask_h=mask)


def _prep_core_inputs(b, inp, consts):
    c = lambda a: np.ascontiguousarray(a)
    x = np.asarray(inp["L0_attn_hidden_states"])
    d = dict(
        xT=c(np.asarray(x[b]).T.astype(np.float32)),
        Wq=c(np.asarray(inp["L0_attn_Wq"])), Wk=c(np.asarray(inp["L0_attn_Wk"])),
        Wv=c(np.asarray(inp["L0_attn_Wv"])), Wo=c(np.asarray(inp["L0_attn_Wo"])),
        bo=c(np.asarray(inp["L0_attn_bo"]).reshape(16, 128).T),
        wup=c(np.asarray(inp["L0_ffn_w_up"]).T.reshape(D, NOC1, 128)
              .transpose(1, 0, 2)),
        wdn=c(np.asarray(inp["L0_ffn_w_down"]).T.reshape(HID1, 16, 128)
              .transpose(1, 0, 2)),
        wqkv=c(np.asarray(inp["L1_attn_w_qkv"]).T),
        wout=c(np.asarray(inp["L1_attn_w_out"]).T),
        wg=c(np.asarray(inp["L1_ffn_wq_gate"]).T.reshape(D, NOC2, 128)
             .transpose(1, 0, 2)),
        wu=c(np.asarray(inp["L1_ffn_wq_up"]).T.reshape(D, NOC2, 128)
             .transpose(1, 0, 2)),
        wd=c(np.asarray(inp["L1_ffn_wq_down"]).T.reshape(HID2, 16, 128)
             .transpose(1, 0, 2)),
        sg=c(np.asarray(inp["L1_ffn_s_gate"]).reshape(NOC2, 128).T),
        su=c(np.asarray(inp["L1_ffn_s_up"]).reshape(NOC2, 128).T),
        sd=c(np.asarray(inp["L1_ffn_s_down"]).reshape(16, 128).T),
    )
    d.update(consts)
    return d


def kernel(**inputs):
    nc = B().build()
    consts = _host_consts()
    in_maps = [_prep_core_inputs(b, inputs, consts) for b in range(8)]
    res = run_bass_kernel_spmd(nc, in_maps, core_ids=list(range(8)))
    out = np.stack([np.ascontiguousarray(r["outT"].T) for r in res.results])
    return out.astype(np.float32)


# version 07
# speedup vs baseline: 1.1438x; 1.0004x over previous
import numpy as np
import ml_dtypes
import concourse.bass as bass
import concourse.mybir as mybir
from concourse.bass_utils import run_bass_kernel_spmd
from contextlib import ExitStack

F32 = mybir.dt.float32
BF16 = mybir.dt.bfloat16
I32 = mybir.dt.int32
AF = mybir.ActivationFunctionType
ALU = mybir.AluOpType
AX = mybir.AxisListType

MAGIC = 12582912.0
T = 2048
D = 2048
NDT = 16
HID1 = 4096
HID2 = 5632
NOC1 = HID1 // 128
NOC2 = HID2 // 128


def _sylvester(n):
    h = np.array([[1.0]])
    while h.shape[0] < n:
        h = np.block([[h, h], [h, -h]])
    return h


class B:

    def __init__(self, upto="all"):
        self.upto = upto
        self.uid = 0
        nc = bass.Bass("TRN2", target_bir_lowering=False)
        self.nc = nc
        self.ctx = ExitStack()

        dram = nc.dram_tensor
        self.xT = dram("xT", [D, T], F32, kind="ExternalInput")
        self.Wq = dram("Wq", [D, D], F32, kind="ExternalInput")
        self.Wk = dram("Wk", [D, D], F32, kind="ExternalInput")
        self.Wv = dram("Wv", [D, D], F32, kind="ExternalInput")
        self.Wo = dram("Wo", [D, D], F32, kind="ExternalInput")
        self.bo = dram("bo", [128, 16], F32, kind="ExternalInput")
        self.wup = dram("wup", [NOC1, D, 128], F32, kind="ExternalInput")
        self.wdn = dram("wdn", [16, HID1, 128], F32, kind="ExternalInput")
        self.wqkv = dram("wqkv", [D, 3 * D], F32, kind="ExternalInput")
        self.wout = dram("wout", [D, D], F32, kind="ExternalInput")
        self.wg = dram("wg", [NOC2, D, 128], I32, kind="ExternalInput")
        self.wu = dram("wu", [NOC2, D, 128], I32, kind="ExternalInput")
        self.wd = dram("wd", [16, HID2, 128], I32, kind="ExternalInput")
        self.sg = dram("sg", [128, NOC2], F32, kind="ExternalInput")
        self.su = dram("su", [128, NOC2], F32, kind="ExternalInput")
        self.sd = dram("sd", [128, 16], F32, kind="ExternalInput")
        self.ones_h = dram("ones_h", [128, 128], BF16, kind="ExternalInput")
        self.gsel_h = dram("gsel_h", [128, 128], BF16, kind="ExternalInput")
        self.ident_h = dram("ident_h", [128, 128], BF16, kind="ExternalInput")
        self.h128_h = dram("h128_h", [128, 128], BF16, kind="ExternalInput")
        self.rotC_h = dram("rotC_h", [32, T], F32, kind="ExternalInput")
        self.rotS_h = dram("rotS_h", [32, T], F32, kind="ExternalInput")
        self.rotCl_h = dram("rotCl_h", [32, T], F32, kind="ExternalInput")
        self.rotSl_h = dram("rotSl_h", [32, T], F32, kind="ExternalInput")
        self.misc_h = dram("misc_h", [128, 4], F32, kind="ExternalInput")
        self.mask_h = dram("mask_h", [128, 4 * 512], BF16, kind="ExternalInput")

        self.outT = dram("outT", [D, T], F32, kind="ExternalOutput")

        self.wup_q = dram("wup_q", [NOC1, D, 128], BF16)
        self.wdn_q = dram("wdn_q", [16, HID1, 128], BF16)
        self.k_scr = dram("k_scr", [16, 128, T], BF16)
        self.v_scr = dram("v_scr", [16, 128, D], BF16)
        self.x_scr = dram("x_scr", [16, 128, T], BF16)

        sb = lambda *a: self.ctx.enter_context(nc.sbuf_tensor(*a))
        self.xs = sb("xs", [128, NDT * T], BF16)
        self.kq = None
        self.ones = sb("ones", [128, 128], BF16)
        self.gsel = sb("gsel", [128, 128], BF16)
        self.ident = sb("ident", [128, 128], BF16)
        self.h128 = sb("h128", [128, 128], BF16)
        self.rotC = None
        self.rotS = None
        self.mask4 = None
        self.bo_sb = sb("bo_sb", [128, 16], F32)
        self.sg_sb = sb("sg_sb", [128, NOC2], F32)
        self.su_sb = sb("su_sb", [128, NOC2], F32)
        self.sd_sb = sb("sd_sb", [128, 16], F32)
        self.kvbd = sb("kvbd", [128, 16 * 128], BF16)
        self.ks_sb = sb("ks_sb", [128, 16], F32)
        self.gup = sb("gup", [128, 1], F32)
        self.mg = sb("mg", [128, 1], F32)
        self.misc = sb("misc", [128, 4], F32)
        self.qscl = sb("qscl", [128, 1], F32)
        self.mB = sb("mB", [128, 512], F32)
        self.rB = sb("rB", [128, 512], F32)

        self.sem = {}
        self.cnt = {}
        for name in ["qs", "qg", "qc", "da", "db", "dc", "dd", "sa", "sb",
                     "pe", "ve", "se", "pe2", "ve2", "se2", "vs"]:
            self.sem[name] = self.ctx.enter_context(nc.semaphore(name))
            self.cnt[name] = 0

    def name(self, prefix):
        self.uid += 1
        return f"{prefix}_{self.uid}"

    def dma(self, eng, dst, src, q):
        eng.dma_start(dst, src).then_inc(self.sem[q], 16)
        self.cnt[q] += 16
        return self.cnt[q]

    def tgt(self, name):
        return self.cnt[name]

    def vsync(self, e):
        self.cnt["vs"] += 1
        e.engine_nop().then_inc(self.sem["vs"], 1)
        e.wait_ge(self.sem["vs"], self.cnt["vs"])

    def xt(self, dt, lo, n):
        return self.xs[:, dt * T + lo: dt * T + lo + n]

    def kqt(self, it, lo, n):
        return self.kq[:, it * T + lo: it * T + lo + n]

    def prologue(self):
        nc, s = self.nc, self
        with nc.Block("prologue") as blk:
            @blk.sync
            def _(e):
                for dst, src in [
                    (s.ones, s.ones_h), (s.gsel, s.gsel_h), (s.ident, s.ident_h),
                    (s.h128, s.h128_h), (s.bo_sb, s.bo), (s.sg_sb, s.sg),
                    (s.su_sb, s.su), (s.sd_sb, s.sd), (s.misc, s.misc_h),
                ]:
                    s.dma(e, dst[:, :], src[:, :], "qs")

            @blk.gpsimd
            def _(e):
                xv = s.xT.rearrange("(dt p) t -> p dt t", p=128)
                xd = s.xs[:, :].rearrange("p (dt t) -> p dt t", dt=NDT)
                s.dma(e, xd, xv, "qg")
                e.memset(s.kvbd[:, :], 0)
                e.memset(s.mg[:, :], 512.0)

            @blk.vector
            def _(e):
                e.wait_ge(s.sem["qs"], s.tgt("qs"))
                e.wait_ge(s.sem["qg"], s.tgt("qg"))
                e.tensor_tensor(s.qscl[:, :], s.misc[:, 0:1], s.misc[:, 1:2],
                                op=ALU.add)

    def layernorm_chunk(self, c0, final=False, out_stage=None):
        nc, s = self.nc, self
        with (
            nc.sbuf_tensor(s.name("ln_sq"), [128, 1024], BF16) as sq,
            nc.psum_tensor(s.name("ln_ps"), [128, 1024], F32) as ps,
        ):
            psum_s = ps[:, 0:512]
            psum_q = ps[:, 512:1024]
            with nc.Block(s.name("ln1")) as blk:
                ve0 = s.cnt["ve"]
                pe0 = s.cnt["pe"]

                @blk.vector
                def _(e):
                    for dt in range(NDT):
                        if dt > 1:
                            e.wait_ge(s.sem["pe"], pe0 + dt - 1)
                        xt = s.xt(dt, c0, 512)
                        e.tensor_mul(sq[:, (dt % 2) * 512:(dt % 2 + 1) * 512],
                                     xt, xt).then_inc(s.sem["ve"], 1)
                        s.cnt["ve"] += 1

                @blk.tensor
                def _(e):
                    for dt in range(NDT):
                        xt = s.xt(dt, c0, 512)
                        e.wait_ge(s.sem["ve"], ve0 + dt + 1)
                        e.matmul(psum_s, s.ones[:, :], xt,
                                 start=(dt == 0), stop=(dt == NDT - 1),
                                 skip_group_check=True)
                        e.matmul(psum_q, s.ones[:, :],
                                 sq[:, (dt % 2) * 512:(dt % 2 + 1) * 512],
                                 start=(dt == 0), stop=(dt == NDT - 1),
                                 skip_group_check=True).then_inc(s.sem["pe"], 1)
                        s.cnt["pe"] += 1

            with nc.Block(s.name("ln2")) as blk:
                @blk.vector
                def _(e):
                    e.tensor_scalar(s.mB[:, :], psum_s, 1.0 / D, None, op0=ALU.mult)
                    e.tensor_scalar(s.rB[:, :], psum_q, 1.0 / D, None, op0=ALU.mult)
                    s.vsync(e)
                    e.tensor_tensor(sq[:, 0:512], s.mB[:, :], s.mB[:, :],
                                    op=ALU.mult)
                    s.vsync(e)
                    e.tensor_tensor(s.rB[:, :], s.rB[:, :], sq[:, 0:512],
                                    op=ALU.subtract)
                    s.vsync(e)
                    e.tensor_scalar(s.rB[:, :], s.rB[:, :], 1e-5, None,
                                    op0=ALU.add).then_inc(s.sem["ve"], 1)
                    s.cnt["ve"] += 1

                @blk.scalar
                def _(e):
                    e.wait_ge(s.sem["ve"], s.tgt("ve"))
                    e.activation(s.rB[:, :], s.rB[:, :], AF.Sqrt).then_inc(
                        s.sem["se"], 1)
                    s.cnt["se"] += 1

            with nc.Block(s.name("ln3")) as blk:
                @blk.vector
                def _(e):
                    e.wait_ge(s.sem["se"], s.tgt("se"))
                    e.reciprocal(s.rB[:, :], s.rB[:, :])
                    s.vsync(e)
                    for dt in range(NDT):
                        xt = s.xt(dt, c0, 512)
                        sqp = sq[:, (dt % 2) * 512:(dt % 2 + 1) * 512]
                        e.tensor_sub(sqp, xt, s.mB[:, :])
                        s.vsync(e)
                        if final:
                            e.tensor_mul(out_stage[:, dt * 512:(dt + 1) * 512],
                                         sqp, s.rB[:, :])
                        else:
                            e.tensor_mul(xt, sqp, s.rB[:, :])

    def l0_attn(self):
        nc, s = self.nc, self
        scopeA = nc.named_scope("l0a_A"); scopeA.__enter__()
        with (
            nc.sbuf_tensor("wk_s", [128, 2 * 16 * 512], BF16) as wk_s,
            nc.sbuf_tensor("wv_s", [128, 2 * 16 * 512], BF16) as wv_s,
            nc.sbuf_tensor("k_sb", [128, 512], BF16) as k_sb,
            nc.sbuf_tensor("v_sb", [128, 512], BF16) as v_sb,
            nc.psum_tensor("kp", [128, 512], F32) as kp,
            nc.psum_tensor("vp", [128, 512], F32) as vp,
            nc.psum_tensor("kvp", [128, 512], F32) as kvp,
            nc.psum_tensor("ksp", [128, 16], F32) as ksp,
            nc.Block("l0a_A") as blk,
        ):
            HB = 16 * 512
            pe0 = s.cnt["pe"]
            pe20 = s.cnt["pe2"]
            ve0 = s.cnt["ve"]
            ve20 = s.cnt["ve2"]

            ld_tgt = {}

            @blk.gpsimd
            def _(e):
                for c4 in range(4):
                    if c4 >= 2:
                        e.wait_ge(s.sem["pe"], pe0 + (c4 - 1) * 16)
                    q = ("da", "db")[c4 % 2]
                    for (w_h, w_s) in ((s.Wk, wk_s), (s.Wv, wv_s)):
                        src = w_h[:, c4 * 512:(c4 + 1) * 512] \
                            .rearrange("(dt p) i -> p dt i", p=128)
                        dst = w_s[:, (c4 % 2) * HB:(c4 % 2 + 1) * HB] \
                            .rearrange("p (dt i) -> p dt i", dt=NDT)
                        s.dma(e, dst, src, q)
                    ld_tgt[c4] = (q, s.cnt[q])

            @blk.tensor
            def _(e):
                for c4 in range(4):
                    e.wait_ge(s.sem[ld_tgt[c4][0]], ld_tgt[c4][1])
                    if c4 > 0:
                        e.wait_ge(s.sem["ve2"], ve20 + c4)
                    wb = (c4 % 2) * HB
                    for tt in range(16):
                        it = c4 * 16 + tt
                        if it > 0:
                            e.wait_ge(s.sem["ve"], ve0 + it)
                        for dt in range(NDT):
                            xt = s.xt(dt, tt * 128, 128)
                            e.matmul(kp[:, :], xt,
                                     wk_s[:, wb + dt * 512: wb + (dt + 1) * 512],
                                     start=(dt == 0), stop=(dt == NDT - 1),
                                     skip_group_check=True)
                            mm = e.matmul(vp[:, :], xt,
                                          wv_s[:, wb + dt * 512: wb + (dt + 1) * 512],
                                          start=(dt == 0), stop=(dt == NDT - 1),
                                          skip_group_check=True)
                        mm.then_inc(s.sem["pe"], 1)
                        s.cnt["pe"] += 1
                        e.wait_ge(s.sem["ve"], ve0 + it + 1)
                        for ps_i in range(4):
                            pair = c4 * 4 + ps_i
                            sl = slice(ps_i * 128, (ps_i + 1) * 128)
                            e.matmul(kvp[:, sl], k_sb[:, sl], v_sb[:, sl],
                                     start=(tt == 0 and ps_i == 0),
                                     stop=(tt == 15),
                                     skip_group_check=True)
                            mm = e.matmul(ksp[:, pair:pair + 1], k_sb[:, sl],
                                          s.ones[:, 0:1],
                                          start=(tt == 0 and ps_i == 0),
                                          stop=(tt == 15),
                                          skip_group_check=True)
                        mm.then_inc(s.sem["pe2"], 1)
                        s.cnt["pe2"] += 1

            @blk.vector
            def _(e):
                for c4 in range(4):
                    for tt in range(16):
                        it = c4 * 16 + tt
                        e.wait_ge(s.sem["pe"], pe0 + it + 1)
                        e.tensor_scalar(k_sb[:, :], kp[:, :], 0.0, None,
                                        op0=ALU.max)
                        e.tensor_copy(v_sb[:, :], vp[:, :]).then_inc(
                            s.sem["ve"], 1)
                        s.cnt["ve"] += 1
                    e.wait_ge(s.sem["pe2"], pe20 + (c4 + 1) * 16)
                    for ps_i in range(4):
                        pair = c4 * 4 + ps_i
                        for half in range(2):
                            h0 = half * 64
                            e.tensor_copy(
                                s.kvbd[h0:h0 + 64,
                                       pair * 128 + h0: pair * 128 + h0 + 64],
                                kvp[h0:h0 + 64,
                                    ps_i * 128 + h0: ps_i * 128 + h0 + 64])
                    e.tensor_copy(s.ks_sb[:, c4 * 4:(c4 + 1) * 4],
                                  ksp[:, c4 * 4:(c4 + 1) * 4]).then_inc(
                        s.sem["ve2"], 1)
                    s.cnt["ve2"] += 1

        scopeA.__exit__(None, None, None)
        scopeB = nc.named_scope("l0a_B1"); scopeB.__enter__()
        if s.kq is None:
            s.kq = s.ctx.enter_context(nc.sbuf_tensor("kq", [128, 16 * T], BF16))
        with (
            nc.sbuf_tensor("wq_s", [128, 2 * 16 * 128], BF16) as wq_s,
            nc.sbuf_tensor("q_sb", [128, 2 * 512], BF16) as q_sb,
            nc.sbuf_tensor("tq_sb", [128, 512], BF16) as tq_sb,
            nc.sbuf_tensor("rec_sb", [128, 2 * 512], F32) as rec_sb,
            nc.psum_tensor("qp", [128, 1024], F32) as qp,
            nc.psum_tensor("npp", [128, 1024], F32) as npp,
            nc.psum_tensor("qkp", [128, 1024], F32) as qkp,
            nc.Block("l0a_B1") as blk,
        ):
            WB = 16 * 128
            pe0 = s.cnt["pe"]
            ve0 = s.cnt["ve"]
            ld_tgt = {}

            @blk.gpsimd
            def _(e):
                for pair in range(16):
                    if pair >= 2:
                        e.wait_ge(s.sem["pe"], pe0 + (pair - 1) * 12)
                    q = ("da", "db")[pair % 2]
                    src = s.Wq[:, pair * 128:(pair + 1) * 128] \
                        .rearrange("(dt p) i -> p dt i", p=128)
                    dst = wq_s[:, (pair % 2) * WB:(pair % 2 + 1) * WB] \
                        .rearrange("p (dt i) -> p dt i", dt=NDT)
                    s.dma(e, dst, src, q)
                    ld_tgt[pair] = (q, s.cnt[q])

            @blk.tensor
            def _(e):
                e.wait_ge(s.sem["ve2"], s.tgt("ve2"))
                for pair in range(16):
                    e.wait_ge(s.sem[ld_tgt[pair][0]], ld_tgt[pair][1])
                    wb = (pair % 2) * WB
                    for ch in range(4):
                        it = pair * 4 + ch
                        par = it % 2
                        if it >= 2:
                            e.wait_ge(s.sem["ve"], ve0 + (it - 2) * 3 + 1)
                        for dt in range(NDT):
                            mm = e.matmul(
                                qp[:, par * 512:(par + 1) * 512],
                                wq_s[:, wb + dt * 128: wb + (dt + 1) * 128],
                                s.xt(dt, ch * 512, 512),
                                start=(dt == 0), stop=(dt == NDT - 1),
                                skip_group_check=True)
                        mm.then_inc(s.sem["pe"], 1)
                        s.cnt["pe"] += 1
                        e.wait_ge(s.sem["ve"], ve0 + it * 3 + 1)
                        if it >= 2:
                            e.wait_ge(s.sem["ve"], ve0 + (it - 2) * 3 + 2)
                        e.matmul(npp[:, par * 512:(par + 1) * 512],
                                 s.gsel[:, :], tq_sb[:, :],
                                 start=True, stop=True,
                                 skip_group_check=True).then_inc(s.sem["pe"], 1)
                        s.cnt["pe"] += 1
                        if it >= 2:
                            e.wait_ge(s.sem["ve"], ve0 + (it - 2) * 3 + 3)
                        e.matmul(qkp[:, par * 512:(par + 1) * 512],
                                 s.kvbd[:, pair * 128:(pair + 1) * 128],
                                 q_sb[:, par * 512:(par + 1) * 512],
                                 start=True, stop=True,
                                 skip_group_check=True).then_inc(s.sem["pe"], 1)
                        s.cnt["pe"] += 1

            @blk.vector
            def _(e):
                for pair in range(16):
                    for ch in range(4):
                        it = pair * 4 + ch
                        par = it % 2
                        e.wait_ge(s.sem["pe"], pe0 + it * 3 + 1)
                        e.tensor_scalar(q_sb[:, par * 512:(par + 1) * 512],
                                        qp[:, par * 512:(par + 1) * 512],
                                        0.0, None, op0=ALU.max)
                        s.vsync(e)
                        e.tensor_scalar(tq_sb[:, :],
                                        q_sb[:, par * 512:(par + 1) * 512],
                                        s.ks_sb[:, pair:pair + 1], None,
                                        op0=ALU.mult).then_inc(s.sem["ve"], 1)
                        s.cnt["ve"] += 1
                        e.wait_ge(s.sem["pe"], pe0 + it * 3 + 2)
                        e.tensor_scalar(rec_sb[:, par * 512:(par + 1) * 512],
                                        npp[:, par * 512:(par + 1) * 512],
                                        1e-6, None, op0=ALU.max)
                        s.vsync(e)
                        e.reciprocal(rec_sb[:, par * 512:(par + 1) * 512],
                                     rec_sb[:, par * 512:(par + 1) * 512]) \
                            .then_inc(s.sem["ve"], 1)
                        s.cnt["ve"] += 1
                        s.vsync(e)
                        e.wait_ge(s.sem["pe"], pe0 + it * 3 + 3)
                        e.tensor_tensor(
                            s.kqt(pair, ch * 512, 512),
                            qkp[:, par * 512:(par + 1) * 512],
                            rec_sb[:, par * 512:(par + 1) * 512],
                            op=ALU.mult).then_inc(s.sem["ve"], 1)
                        s.cnt["ve"] += 1

        scopeB.__exit__(None, None, None)
        scopeB2 = nc.named_scope("l0a_B2"); scopeB2.__enter__()
        with (
            nc.sbuf_tensor("wo_s", [128, 2 * 16 * 128], BF16) as wo_s,
            nc.psum_tensor("op_ps", [128, 1024], F32) as op_ps,
            nc.Block("l0a_B2") as blk,
        ):
            WB = 16 * 128
            pe0 = s.cnt["pe"]
            ve0 = s.cnt["ve"]
            ld_tgt = {}

            @blk.gpsimd
            def _(e):
                for ot in range(16):
                    if ot >= 2:
                        e.wait_ge(s.sem["pe"], pe0 + (ot - 1) * 4)
                    q = ("da", "db")[ot % 2]
                    src = s.Wo[:, ot * 128:(ot + 1) * 128] \
                        .rearrange("(it p) o -> p it o", p=128)
                    dst = wo_s[:, (ot % 2) * WB:(ot % 2 + 1) * WB] \
                        .rearrange("p (it o) -> p it o", it=NDT)
                    s.dma(e, dst, src, q)
                    ld_tgt[ot] = (q, s.cnt[q])

            @blk.tensor
            def _(e):
                for ot in range(16):
                    e.wait_ge(s.sem[ld_tgt[ot][0]], ld_tgt[ot][1])
                    wb = (ot % 2) * WB
                    for ch in range(4):
                        it = ot * 4 + ch
                        pslice = op_ps[:, (it % 2) * 512:(it % 2 + 1) * 512]
                        if it >= 2:
                            e.wait_ge(s.sem["ve"], ve0 + it - 1)
                        for itile in range(16):
                            mm = e.matmul(
                                pslice, wo_s[:, wb + itile * 128: wb + (itile + 1) * 128],
                                s.kqt(itile, ch * 512, 512),
                                start=(itile == 0), stop=(itile == NDT - 1),
                                skip_group_check=True)
                        mm.then_inc(s.sem["pe"], 1)
                        s.cnt["pe"] += 1

            @blk.vector
            def _(e):
                for ot in range(16):
                    for ch in range(4):
                        it = ot * 4 + ch
                        pslice = op_ps[:, (it % 2) * 512:(it % 2 + 1) * 512]
                        e.wait_ge(s.sem["pe"], pe0 + it + 1)
                        xt = s.xt(ot, ch * 512, 512)
                        e.scalar_tensor_tensor(
                            xt, pslice, s.bo_sb[:, ot:ot + 1], xt,
                            op0=ALU.add, op1=ALU.add).then_inc(s.sem["ve"], 1)
                        s.cnt["ve"] += 1

        scopeB2.__exit__(None, None, None)
        with nc.named_scope("l0a_ln"):
            for ch in range(4):
                self.layernorm_chunk(ch * 512)

    def l0_ffn_wquant(self):
        nc, s = self.nc, self
        s.wu_m = s.ctx.enter_context(nc.sbuf_tensor("wu_m", [128, 1], F32))
        s.wd_m = s.ctx.enter_context(nc.sbuf_tensor("wd_m", [128, 1], F32))
        s.wu_r = s.ctx.enter_context(nc.sbuf_tensor("wu_r", [128, 1], F32))
        s.wd_r = s.ctx.enter_context(nc.sbuf_tensor("wd_r", [128, 1], F32))
        specs = [
            (s.wup, s.wup_q, NOC1, NDT * 128, s.wu_m, s.wu_r),
            (s.wdn, s.wdn_q, 16, NOC1 * 128, s.wd_m, s.wd_r),
        ]
        with (
            nc.sbuf_tensor("wqb", [128, 2 * 4096], F32) as wqb,
            nc.sbuf_tensor("wqt", [128, 2 * 4096], BF16) as wqt,
            nc.sbuf_tensor("wqi", [128, 4096], I32) as wqi,
            nc.sbuf_tensor("wqa", [128, 48], F32) as wqa,
            nc.sbuf_tensor("wqa_bf", [128, 1], BF16) as wqa_bf,
            nc.psum_tensor("wq_ps", [128, 1], F32) as wq_ps,
        ):
            with nc.Block("wq1") as blk:
                ve0 = s.cnt["ve"]
                ld_tgt = {}

                @blk.gpsimd
                def _(e):
                    i = 0
                    for (wh, _sc, n, epp, _m, _r) in specs:
                        for ch in range(n):
                            if i >= 2:
                                e.wait_ge(s.sem["ve"], ve0 + i - 1)
                            q = ("da", "db")[i % 2]
                            src = wh[ch].rearrange("(a p) o -> p a o", p=128)
                            dst = wqb[:, (i % 2) * 4096:(i % 2) * 4096 + epp] \
                                .rearrange("p (a o) -> p a o", o=128)
                            s.dma(e, dst, src, q)
                            ld_tgt[i] = (q, s.cnt[q])
                            i += 1

                @blk.vector
                def _(e):
                    i = 0
                    for (wh, _sc, n, epp, _m, _r) in specs:
                        for ch in range(n):
                            e.wait_ge(s.sem[ld_tgt[i][0]], ld_tgt[i][1])
                            e.reduce_sum(wqa[:, i:i + 1],
                                         wqb[:, (i % 2) * 4096:(i % 2) * 4096 + epp],
                                         axis=AX.X,
                                         apply_absolute_value=True).then_inc(
                                s.sem["ve"], 1)
                            s.cnt["ve"] += 1
                            i += 1

            for wi, (wh, _sc, n, epp, m_sb, r_sb) in enumerate(specs):
                lo = 0 if wi == 0 else NOC1
                cnt_el = float(n * epp * 128)
                with nc.Block(s.name("wqs")) as blk:
                    @blk.vector
                    def _(e):
                        e.reduce_sum(s.gup[:, :], wqa[:, lo:lo + n], axis=AX.X)
                        s.vsync(e)
                        e.tensor_copy(wqa_bf[:, :], s.gup[:, :]).then_inc(
                            s.sem["ve"], 1)
                        s.cnt["ve"] += 1

                    @blk.tensor
                    def _(e):
                        e.wait_ge(s.sem["ve"], s.tgt("ve"))
                        e.matmul(wq_ps[:, :], s.ones[:, :], wqa_bf[:, :],
                                 start=True, stop=True).then_inc(s.sem["pe"], 1)
                        s.cnt["pe"] += 1

                    @blk.vector
                    def _(e2):
                        e2.wait_ge(s.sem["pe"], s.tgt("pe"))
                        e2.tensor_scalar(m_sb[:, :], wq_ps[:, :], 1.0 / cnt_el,
                                         1e-5, op0=ALU.mult, op1=ALU.max)
                        s.vsync(e2)
                        e2.reciprocal(r_sb[:, :], m_sb[:, :])

            with nc.Block("wqg") as blk:
                @blk.vector
                def _(e):
                    e.tensor_mul(s.gup[:, :], s.wu_m[:, :], s.wu_m[:, :])
                    s.vsync(e)
                    e.tensor_mul(s.gup[:, :], s.gup[:, :], s.wd_m[:, :])
                    s.vsync(e)
                    e.tensor_scalar_mul(s.gup[:, :], s.gup[:, :], 1.0 / 131072.0)
                    s.vsync(e)

            with nc.Block("wq2") as blk:
                se0 = s.cnt["se"]
                ve0 = s.cnt["ve"]
                sa0 = s.cnt["sa"]
                sb0 = s.cnt["sb"]
                ld_tgt = {}

                @blk.gpsimd
                def _(e):
                    i = 0
                    for (wh, _sc, n, epp, _m, r_sb) in specs:
                        for ch in range(n):
                            if i >= 2:
                                e.wait_ge(s.sem["ve"], ve0 + i - 1)
                            q = ("da", "db")[i % 2]
                            src = wh[ch].rearrange("(a p) o -> p a o", p=128)
                            dst = wqb[:, (i % 2) * 4096:(i % 2) * 4096 + epp] \
                                .rearrange("p (a o) -> p a o", o=128)
                            s.dma(e, dst, src, q)
                            ld_tgt[i] = (q, s.cnt[q])
                            i += 1

                @blk.scalar
                def _(e):
                    i = 0
                    for (wh, _sc, n, epp, _m, r_sb) in specs:
                        for ch in range(n):
                            e.wait_ge(s.sem[ld_tgt[i][0]], ld_tgt[i][1])
                            if i >= 2:
                                e.wait_ge(s.sem["ve"], ve0 + i - 1)
                            sl = wqb[:, (i % 2) * 4096:(i % 2) * 4096 + epp]
                            e.activation(sl, sl, AF.Identity,
                                         scale=r_sb[:, :]).then_inc(s.sem["se"], 1)
                            s.cnt["se"] += 1
                            i += 1

                @blk.vector
                def _(e):
                    i = 0
                    for (wh, _sc, n, epp, _m, _r) in specs:
                        for ch in range(n):
                            e.wait_ge(s.sem["se"], se0 + i + 1)
                            if i >= 2:
                                par = i % 2
                                base = sa0 if par == 0 else sb0
                                e.wait_ge(s.sem[("sa", "sb")[par]],
                                          base + 16 * ((i - 2) // 2 + 1))
                            fsl = wqb[:, (i % 2) * 4096:(i % 2) * 4096 + epp]
                            tsl = wqt[:, (i % 2) * 4096:(i % 2) * 4096 + epp]
                            risl = wqi[:, 0:epp]
                            e.tensor_scalar(fsl, fsl, -1.0, 1.0,
                                            op0=ALU.max, op1=ALU.min)
                            s.vsync(e)
                            e.tensor_scalar(risl, fsl, 512.0, None, op0=ALU.add)
                            s.vsync(e)
                            e.scalar_tensor_tensor(fsl, risl, -1.0, fsl,
                                                   op0=ALU.mult, op1=ALU.add)
                            s.vsync(e)
                            e.tensor_scalar(fsl, fsl, -511.5, None, op0=ALU.is_ge)
                            s.vsync(e)
                            e.scalar_tensor_tensor(tsl, risl, -512.0, fsl,
                                                   op0=ALU.add, op1=ALU.add) \
                                .then_inc(s.sem["ve"], 1)
                            s.cnt["ve"] += 1
                            i += 1

                @blk.sync
                def _(e):
                    i = 0
                    for (wh, scr, n, epp, _m, _r) in specs:
                        for ch in range(n):
                            e.wait_ge(s.sem["ve"], ve0 + i + 1)
                            dst = scr[ch].rearrange("(a p) o -> p a o", p=128)
                            src = wqt[:, (i % 2) * 4096:(i % 2) * 4096 + epp] \
                                .rearrange("p (a o) -> p a o", o=128)
                            s.dma(e, dst, src, ("sa", "sb")[i % 2])
                            i += 1
                    e.wait_ge(s.sem["sa"], s.tgt("sa"))
                    e.wait_ge(s.sem["sb"], s.tgt("sb"))

    def _ffn_sl(self, i):
        return self.kq[:, i * 512:(i + 1) * 512]

    def _fwht_butterflies(self, slots, spares, nbits):
        nc, s = self.nc, self
        spc = list(spares)
        with nc.Block(s.name("bfly")) as blk:
            @blk.vector
            def _(e):
                n = len(slots)
                for lvl in range(nbits):
                    st = 1 << lvl
                    pairs = [(a, a | st) for a in range(n) if not (a & st)]
                    for gi in range(0, len(pairs), 4):
                        if lvl > 0 or gi > 0:
                            s.vsync(e)
                        freed = []
                        for (a, b) in pairs[gi:gi + 4]:
                            sa, sb_ = slots[a], slots[b]
                            sp1 = spc.pop(0)
                            sp2 = spc.pop(0)
                            e.tensor_add(s._ffn_sl(sp1), s._ffn_sl(sa),
                                         s._ffn_sl(sb_))
                            e.tensor_sub(s._ffn_sl(sp2), s._ffn_sl(sa),
                                         s._ffn_sl(sb_))
                            slots[a], slots[b] = sp1, sp2
                            freed += [sa, sb_]
                        spc.extend(freed)
        return slots, spc

    def _ffn_h128(self, slots, ps, src_of=None):
        nc, s = self.nc, self
        n = len(slots)
        with nc.Block(s.name("h128")) as blk:
            pe0 = s.cnt["pe"]
            ve0 = s.cnt["ve"]

            @blk.tensor
            def _(e):
                for i in range(n):
                    if i >= 2:
                        e.wait_ge(s.sem["ve"], ve0 + i - 1)
                    src = src_of(i) if src_of else s._ffn_sl(slots[i])
                    e.matmul(ps[:, (i % 2) * 512:(i % 2 + 1) * 512],
                             s.h128[:, :], src, start=True, stop=True,
                             skip_group_check=True).then_inc(s.sem["pe"], 1)
                    s.cnt["pe"] += 1

            @blk.vector
            def _(e):
                for i in range(n):
                    e.wait_ge(s.sem["pe"], pe0 + i + 1)
                    e.tensor_copy(s._ffn_sl(slots[i]),
                                  ps[:, (i % 2) * 512:(i % 2 + 1) * 512]) \
                        .then_inc(s.sem["ve"], 1)
                    s.cnt["ve"] += 1

    def _ffn_quant(self, slots, pt, tm_sb, tmpf):
        nc, s = self.nc, self
        n = len(slots)
        nw = (n + 15) // 16
        ng = (n * 128) // 512
        nh = min(ng, 4)
        nhalves = (ng + nh - 1) // nh
        with (
            nc.sbuf_tensor(s.name("qam"), [128, 2], F32) as am,
            nc.sbuf_tensor(s.name("qrb"), [128, 2048], I32) as rb,
            nc.sbuf_tensor(s.name("qcb"), [128, 2048], BF16) as cb,
            nc.sbuf_tensor(s.name("qc2"), [128, 2048], BF16) as c2,
            nc.Block(s.name("qnt")) as blk,
        ):
            pe0 = s.cnt["pe"]
            ve0 = s.cnt["ve"]
            se0 = s.cnt["se"]
            ve_sub = lambda sub: ve0 + sub * (2 * nw + 1 + nhalves)
            pe_sub = lambda sub: pe0 + sub * (2 * nw)
            se_sub = lambda sub: se0 + sub * ng

            @blk.scalar
            def _(e):
                for sub in range(4):
                    for hv in range(nhalves):
                        g0, g1 = hv * nh, min(ng, (hv + 1) * nh)
                        for g in range(g0, g1):
                            e.wait_ge(s.sem["ve"], ve_sub(sub) + nw + 1 + hv)
                            sl = tm_sb[:, g * 512:(g + 1) * 512]
                            e.activation(tmpf[:, (g - g0) * 512:(g - g0 + 1) * 512],
                                         sl, AF.Identity, bias=s.mg[:, :],
                                         scale=am[:, 0:1]).then_inc(s.sem["se"], 1)
                            s.cnt["se"] += 1

            @blk.tensor
            def _(e):
                for sub in range(4):
                    for w in range(nw):
                        if sub > 0 or w > 0:
                            e.wait_ge(s.sem["ve"], ve_sub(sub) + w)
                        lo = w * 16
                        hi = min(n, lo + 16)
                        for i2 in range(lo, hi):
                            mm = e.transpose(
                                pt[:, (i2 - lo) * 128:(i2 - lo + 1) * 128],
                                s._ffn_sl(slots[i2])[:, sub * 128:(sub + 1) * 128],
                                s.ident[:, :])
                        mm.then_inc(s.sem["pe"], 1)
                        s.cnt["pe"] += 1
                    for w in range(nw):
                        e.wait_ge(s.sem["ve"], ve_sub(sub) + nw + 1 + nhalves + w)
                        lo = w * 16
                        hi = min(n, lo + 16)
                        for i2 in range(lo, hi):
                            mm = e.transpose(
                                pt[:, (i2 - lo) * 128:(i2 - lo + 1) * 128],
                                tm_sb[:, i2 * 128:(i2 + 1) * 128],
                                s.ident[:, :])
                        mm.then_inc(s.sem["pe"], 1)
                        s.cnt["pe"] += 1

            @blk.vector
            def _(e):
                for sub in range(4):
                    for w in range(nw):
                        e.wait_ge(s.sem["pe"], pe_sub(sub) + w + 1)
                        lo = w * 16
                        hi = min(n, lo + 16)
                        e.tensor_copy(tm_sb[:, lo * 128:hi * 128],
                                      pt[:, 0:(hi - lo) * 128])
                        if w == 0:
                            e.reduce_max(am[:, 0:1], pt[:, 0:(hi - lo) * 128],
                                         axis=AX.X, apply_absolute_value=True)
                        else:
                            e.reduce_max(am[:, 1:2], pt[:, 0:(hi - lo) * 128],
                                         axis=AX.X, apply_absolute_value=True)
                            s.vsync(e)
                            e.tensor_max(am[:, 0:1], am[:, 0:1], am[:, 1:2])
                        e.engine_nop().then_inc(s.sem["ve"], 1)
                        s.cnt["ve"] += 1
                    s.vsync(e)
                    e.tensor_scalar(am[:, 1:2], am[:, 0:1], 1e-5, 1.0 / 127.0,
                                    op0=ALU.max, op1=ALU.mult)
                    s.vsync(e)
                    e.reciprocal(am[:, 0:1], am[:, 1:2])
                    s.vsync(e)
                    e.engine_nop().then_inc(s.sem["ve"], 1)
                    s.cnt["ve"] += 1
                    for hv in range(nhalves):
                        g0, g1 = hv * nh, min(ng, (hv + 1) * nh)
                        w512 = (g1 - g0) * 512
                        e.wait_ge(s.sem["se"], se_sub(sub) + g1)
                        e.tensor_copy(rb[:, 0:w512], tmpf[:, 0:w512])
                        s.vsync(e)
                        e.tensor_tensor(tmpf[:, 0:w512], tmpf[:, 0:w512],
                                        rb[:, 0:w512], op=ALU.subtract)
                        s.vsync(e)
                        e.tensor_scalar(cb[:, 0:w512], tmpf[:, 0:w512], 0.5,
                                        None, op0=ALU.is_ge)
                        s.vsync(e)
                        e.tensor_scalar(c2[:, 0:w512], tmpf[:, 0:w512], 1.5,
                                        None, op0=ALU.is_ge)
                        s.vsync(e)
                        e.tensor_tensor(cb[:, 0:w512], cb[:, 0:w512],
                                        c2[:, 0:w512], op=ALU.add)
                        s.vsync(e)
                        e.tensor_scalar(c2[:, 0:w512], tmpf[:, 0:w512], -0.5,
                                        None, op0=ALU.is_lt)
                        s.vsync(e)
                        e.tensor_tensor(cb[:, 0:w512], cb[:, 0:w512],
                                        c2[:, 0:w512], op=ALU.subtract)
                        s.vsync(e)
                        e.tensor_scalar(c2[:, 0:w512], tmpf[:, 0:w512], -1.5,
                                        None, op0=ALU.is_lt)
                        s.vsync(e)
                        e.tensor_tensor(cb[:, 0:w512], cb[:, 0:w512],
                                        c2[:, 0:w512], op=ALU.subtract)
                        s.vsync(e)
                        e.tensor_scalar(c2[:, 0:w512], tmpf[:, 0:w512], 2.5,
                                        None, op0=ALU.is_ge)
                        s.vsync(e)
                        e.tensor_tensor(cb[:, 0:w512], cb[:, 0:w512],
                                        c2[:, 0:w512], op=ALU.add)
                        s.vsync(e)
                        e.tensor_scalar(c2[:, 0:w512], tmpf[:, 0:w512], -2.5,
                                        None, op0=ALU.is_lt)
                        s.vsync(e)
                        e.tensor_tensor(cb[:, 0:w512], cb[:, 0:w512],
                                        c2[:, 0:w512], op=ALU.subtract)
                        s.vsync(e)
                        e.tensor_tensor(tmpf[:, 0:w512], rb[:, 0:w512],
                                        cb[:, 0:w512], op=ALU.add)
                        s.vsync(e)
                        e.tensor_scalar(tm_sb[:, g0 * 512:g1 * 512],
                                        tmpf[:, 0:w512],
                                        512.0, am[:, 1:2],
                                        op0=ALU.subtract, op1=ALU.mult)
                        e.engine_nop().then_inc(s.sem["ve"], 1)
                        s.cnt["ve"] += 1
                    for w in range(nw):
                        e.wait_ge(s.sem["pe"], pe_sub(sub) + nw + w + 1)
                        lo = w * 16
                        hi = min(n, lo + 16)
                        for i2 in range(lo, hi):
                            e.tensor_copy(
                                s._ffn_sl(slots[i2])[:, sub * 128:(sub + 1) * 128],
                                pt[:, (i2 - lo) * 128:(i2 - lo + 1) * 128])
                        e.engine_nop().then_inc(s.sem["ve"], 1)
                        s.cnt["ve"] += 1

    def _dbg_tm(self, slots, pt, tm_sb):
        nc, s = self.nc, self
        n = len(slots)
        nw = (n + 15) // 16
        with (
            nc.sbuf_tensor(s.name("qam"), [128, 2], F32) as am,
            nc.Block(s.name("dbgtm")) as blk,
        ):
            pe0 = s.cnt["pe"]
            ve0 = s.cnt["ve"]

            @blk.tensor
            def _(e):
                for w in range(nw):
                    if w > 0:
                        e.wait_ge(s.sem["ve"], ve0 + w)
                    lo = w * 16
                    hi = min(n, lo + 16)
                    for i in range(lo, hi):
                        mm = e.transpose(
                            pt[:, (i - lo) * 128:(i - lo + 1) * 128],
                            s._ffn_sl(slots[i])[:, 0:128],
                            s.ident[:, :])
                    mm.then_inc(s.sem["pe"], 1)
                    s.cnt["pe"] += 1

            @blk.vector
            def _(e):
                for w in range(nw):
                    e.wait_ge(s.sem["pe"], pe0 + w + 1)
                    lo = w * 16
                    hi = min(n, lo + 16)
                    e.tensor_copy(tm_sb[:, lo * 128:hi * 128],
                                  pt[:, 0:(hi - lo) * 128])
                    if w == 0:
                        e.reduce_max(am[:, 0:1], pt[:, 0:(hi - lo) * 128],
                                     axis=AX.X, apply_absolute_value=True)
                    else:
                        e.reduce_max(am[:, 1:2], pt[:, 0:(hi - lo) * 128],
                                     axis=AX.X, apply_absolute_value=True)
                        e.tensor_max(am[:, 0:1], am[:, 0:1], am[:, 1:2])
                    e.engine_nop().then_inc(s.sem["ve"], 1)
                    s.cnt["ve"] += 1
                if self.upto == "ffn_tm2":
                    tmpf = self._dbg_tmpf
                    e.tensor_scalar(am[:, 0:1], am[:, 0:1], 1e-5, None,
                                    op0=ALU.max)
                    e.tensor_scalar(am[:, 1:2], am[:, 0:1], 1.0 / 127.0, None,
                                    op0=ALU.mult)
                    e.reciprocal(am[:, 0:1], am[:, 0:1])
                    e.tensor_scalar(am[:, 0:1], am[:, 0:1], 127.0, None,
                                    op0=ALU.mult)
                    for g in range((n * 128) // 512):
                        sl = tm_sb[:, g * 512:(g + 1) * 512]
                        e.tensor_scalar(tmpf[:, :], sl, am[:, 0:1], MAGIC,
                                        op0=ALU.mult, op1=ALU.add)
                        e.tensor_scalar(sl, tmpf[:, :], MAGIC, am[:, 1:2],
                                        op0=ALU.subtract, op1=ALU.mult)
                e.tensor_copy(s.kq[:, 27000:27000 + n * 128], tm_sb[:, 0:n * 128])
                e.tensor_copy(s.kq[:, 31500:31502], am[:, 0:2])

    def l0_ffn(self):
        nc, s = self.nc, self
        if s.kq is None:
            s.kq = s.ctx.enter_context(nc.sbuf_tensor("kq", [128, 16 * T], BF16))
        if self.upto.startswith("ffn_"):
            with nc.Block(s.name("dbg0")) as blk:
                @blk.gpsimd
                def _(e):
                    e.memset(s.kq[:, :], 0)
        self.l0_ffn_wquant()
        if self.upto == "ffn_w":
            with nc.Block("dbgw") as blk:
                @blk.vector
                def _(e):
                    e.tensor_copy(s.kq[:, 27000:27001], s.wu_m[:, :])
                    e.tensor_copy(s.kq[:, 27001:27002], s.wd_m[:, :])
                    e.tensor_copy(s.kq[:, 27002:27003], s.wu_r[:, :])
                    e.tensor_copy(s.kq[:, 27003:27004], s.wd_r[:, :])
                    e.tensor_copy(s.kq[:, 27004:27005], s.gup[:, :])

                @blk.sync
                def _(e):
                    dst = s.kq[:, 27856:27856 + 2048] \
                        .rearrange("p (a o) -> p a o", o=128)
                    src2 = s.wup_q[0].rearrange("(a p) o -> p a o", p=128)
                    s.dma(e, dst, src2, "qs")

            with nc.Block("dbgw2") as blk:
                @blk.vector
                def _(e):
                    e.wait_ge(s.sem["qs"], s.tgt("qs"))
            return
        FW = list(range(24))
        HS = list(range(24, 64))
        with (
            nc.sbuf_tensor("f_wb", [128, 2 * NDT * 128], BF16) as f_wb,
            nc.sbuf_tensor("f_db", [128, 2 * NOC1 * 128], BF16) as f_db,
            nc.sbuf_tensor("f_tmp", [128, 2048], F32) as f_tmp,
            nc.sbuf_tensor("f_rl", [128, 1024], BF16) as f_rl,
            nc.sbuf_tensor("f_tm", [128, NOC1 * 128], BF16) as f_tm,
            nc.psum_tensor("f_pa", [128, 1024], F32) as f_pa,
            nc.psum_tensor("f_pt", [128, 2048], BF16) as f_pt,
        ):
            for cc in range(4):
                c0 = cc * 512
                fw = FW[:16]
                sp = FW[16:24]
                self._ffn_h128(fw, f_pa, src_of=lambda i: s.xt(i, c0, 512))
                fw, sp = self._fwht_butterflies(fw, sp, 4)
                if self.upto in ("ffn_tm1", "ffn_tm2") and cc == 0:
                    self._dbg_tmpf = f_tmp
                    self._dbg_tm(fw, f_pt, f_tm)
                    self.dbg_slots = list(fw)
                    return
                if self.upto == "ffn_fwht1" and cc == 0:
                    self.dbg_slots = list(fw)
                    return
                self._ffn_quant(fw, f_pt, f_tm, f_tmp)
                if self.upto == "ffn_q1" and cc == 0:
                    self.dbg_slots = list(fw)
                    return

                hs = HS[:32]
                hsp = HS[32:40]
                with nc.Block(s.name("fup")) as blk:
                    pe0 = s.cnt["pe"]
                    ve0 = s.cnt["ve"]
                    ld_tgt = {}

                    @blk.scalar
                    def _(e):
                        for oc in range(NOC1):
                            if oc >= 2:
                                e.wait_ge(s.sem["pe"], pe0 + oc - 1)
                            q = ("dc", "dd")[oc % 2]
                            src = s.wup_q[oc].rearrange("(a p) o -> p a o", p=128)
                            dst = f_wb[:, (oc % 2) * 2048:(oc % 2 + 1) * 2048] \
                                .rearrange("p (a o) -> p a o", o=128)
                            s.dma(e, dst, src, q)
                            ld_tgt[oc] = (q, s.cnt[q])

                    @blk.tensor
                    def _(e):
                        for oc in range(NOC1):
                            e.wait_ge(s.sem[ld_tgt[oc][0]], ld_tgt[oc][1])
                            if oc >= 2:
                                e.wait_ge(s.sem["ve"], ve0 + oc - 1)
                            wb = (oc % 2) * 2048
                            for dt in range(NDT):
                                mm = e.matmul(
                                    f_pa[:, (oc % 2) * 512:(oc % 2 + 1) * 512],
                                    f_wb[:, wb + dt * 128: wb + (dt + 1) * 128],
                                    s._ffn_sl(fw[dt]),
                                    start=(dt == 0), stop=(dt == NDT - 1),
                                    skip_group_check=True)
                            mm.then_inc(s.sem["pe"], 1)
                            s.cnt["pe"] += 1

                    @blk.vector
                    def _(e):
                        for oc in range(NOC1):
                            e.wait_ge(s.sem["pe"], pe0 + oc + 1)
                            psl = f_pa[:, (oc % 2) * 512:(oc % 2 + 1) * 512]
                            rl = f_rl[:, (oc % 2) * 512:(oc % 2 + 1) * 512]
                            e.tensor_scalar(rl, psl, 0.0, None, op0=ALU.max)
                            s.vsync(e)
                            e.tensor_tensor(s._ffn_sl(hs[oc]), rl,
                                            rl, op=ALU.mult).then_inc(
                                s.sem["ve"], 1)
                            s.cnt["ve"] += 1

                if self.upto == "ffn_up" and cc == 0:
                    self.dbg_slots = list(hs)
                    return
                self._ffn_h128(hs, f_pa)
                hs, hsp = self._fwht_butterflies(hs, hsp, 5)
                self._ffn_quant(hs, f_pt, f_tm, f_tmp)

                with nc.Block(s.name("fdn")) as blk:
                    pe0 = s.cnt["pe"]
                    ve0 = s.cnt["ve"]
                    ld_tgt = {}

                    @blk.scalar
                    def _(e):
                        for ct in range(16):
                            if ct >= 2:
                                e.wait_ge(s.sem["pe"], pe0 + ct - 1)
                            q = ("dc", "dd")[ct % 2]
                            src = s.wdn_q[ct].rearrange("(a p) o -> p a o", p=128)
                            dst = f_db[:, (ct % 2) * 4096:(ct % 2 + 1) * 4096] \
                                .rearrange("p (a o) -> p a o", o=128)
                            s.dma(e, dst, src, q)
                            ld_tgt[ct] = (q, s.cnt[q])

                    @blk.tensor
                    def _(e):
                        for ct in range(16):
                            e.wait_ge(s.sem[ld_tgt[ct][0]], ld_tgt[ct][1])
                            if ct >= 2:
                                e.wait_ge(s.sem["ve"], ve0 + ct - 1)
                            wb = (ct % 2) * 4096
                            for ot in range(NOC1):
                                mm = e.matmul(
                                    f_pa[:, (ct % 2) * 512:(ct % 2 + 1) * 512],
                                    f_db[:, wb + ot * 128: wb + (ot + 1) * 128],
                                    s._ffn_sl(hs[ot]),
                                    start=(ot == 0), stop=(ot == NOC1 - 1),
                                    skip_group_check=True)
                            mm.then_inc(s.sem["pe"], 1)
                            s.cnt["pe"] += 1

                    @blk.vector
                    def _(e):
                        for ct in range(16):
                            e.wait_ge(s.sem["pe"], pe0 + ct + 1)
                            psl = f_pa[:, (ct % 2) * 512:(ct % 2 + 1) * 512]
                            xt = s.xt(ct, c0, 512)
                            e.scalar_tensor_tensor(
                                xt, psl, s.gup[:, :], xt,
                                op0=ALU.mult, op1=ALU.add).then_inc(s.sem["ve"], 1)
                            s.cnt["ve"] += 1

        for cc in range(4):
            self.layernorm_chunk(cc * 512)

    def _l1_rotary(self, tgt, c0, ncols, qsh):
        nc, s = self.nc, self
        for hg in range(2):
            with nc.Block(s.name("rot")) as blk:
                qs0 = s.cnt["qs"]

                @blk.sync
                def _(e):
                    for h8 in range(8):
                        h = hg * 8 + h8
                        sl = slice(h * T + c0, h * T + c0 + ncols)
                        d = slice(h8 * ncols, (h8 + 1) * ncols)
                        s.dma(e, qsh[0:16, d], tgt[16:32, sl], "qs")
                        s.dma(e, qsh[16:32, d], tgt[0:16, sl], "qs")

                @blk.vector
                def _(e):
                    e.wait_ge(s.sem["qs"], s.tgt("qs"))
                    sview = s.rotS[:, c0:c0 + ncols].unsqueeze(1) \
                        .broadcast_to([32, 8, ncols])
                    qview = qsh[:, 0:8 * ncols].rearrange("p (h t) -> p h t", h=8)
                    e.tensor_tensor(qview, qview, sview, op=ALU.mult)
                    for h8 in range(8):
                        h = hg * 8 + h8
                        sl = slice(h * T + c0, h * T + c0 + ncols)
                        rot = tgt[0:32, sl]
                        e.tensor_tensor(rot, rot, s.rotC[:, c0:c0 + ncols],
                                        op=ALU.mult)
                    s.vsync(e)
                    for h8 in range(8):
                        h = hg * 8 + h8
                        sl = slice(h * T + c0, h * T + c0 + ncols)
                        rot = tgt[0:32, sl]
                        e.tensor_tensor(rot, rot,
                                        qsh[:, h8 * ncols:(h8 + 1) * ncols],
                                        op=ALU.add)

    def _l1_proj_fm(self, wcol0, scale, store_scr):
        nc, s = self.nc, self
        with (
            nc.sbuf_tensor(s.name("pw"), [128, 2 * 16 * 128], BF16) as pw,
            nc.psum_tensor(s.name("pp"), [128, 4096], F32) as pp,
            nc.Block(s.name("proj")) as blk,
        ):
            WB = 16 * 128
            pe0 = s.cnt["pe"]
            ve0 = s.cnt["ve"]
            ld_tgt = {}

            @blk.gpsimd
            def _(e):
                for h in range(16):
                    if h >= 2:
                        e.wait_ge(s.sem["pe"], pe0 + h - 1)
                    q = ("da", "db")[h % 2]
                    src = s.wqkv[:, wcol0 + h * 128: wcol0 + (h + 1) * 128] \
                        .rearrange("(dt p) i -> p dt i", p=128)
                    dst = pw[:, (h % 2) * WB:(h % 2 + 1) * WB] \
                        .rearrange("p (dt i) -> p dt i", dt=NDT)
                    s.dma(e, dst, src, q)
                    ld_tgt[h] = (q, s.cnt[q])

            @blk.tensor
            def _(e):
                for h in range(16):
                    e.wait_ge(s.sem[ld_tgt[h][0]], ld_tgt[h][1])
                    if h >= 2:
                        e.wait_ge(s.sem["ve"], ve0 + h - 1)
                    wb = (h % 2) * WB
                    pb = (h % 2) * 2048
                    for dt in range(NDT):
                        for jc in range(4):
                            mm = e.matmul(
                                pp[:, pb + jc * 512: pb + (jc + 1) * 512],
                                pw[:, wb + dt * 128: wb + (dt + 1) * 128],
                                s.xt(dt, jc * 512, 512),
                                start=(dt == 0), stop=(dt == NDT - 1),
                                skip_group_check=True)
                    mm.then_inc(s.sem["pe"], 1)
                    s.cnt["pe"] += 1

            @blk.vector
            def _(e):
                for h in range(16):
                    e.wait_ge(s.sem["pe"], pe0 + h + 1)
                    pb = (h % 2) * 2048
                    if scale is None:
                        e.tensor_copy(s.kq[:, h * T:(h + 1) * T],
                                      pp[:, pb:pb + 2048])
                    else:
                        e.tensor_scalar(s.kq[:, h * T:(h + 1) * T],
                                        pp[:, pb:pb + 2048],
                                        s.qscl[:, 0:1], None, op0=ALU.mult)
                    e.engine_nop().then_inc(s.sem["ve"], 1)
                    s.cnt["ve"] += 1

        if not self.upto.endswith("_norot"):
            with nc.sbuf_tensor(s.name("qsh"), [32, 8 * T], BF16) as qsh:
                self._l1_rotary(s.kq, 0, T, qsh)

        if store_scr and not self.upto.startswith("l1x_kraw"):
            with nc.Block(s.name("kst")) as blk:
                @blk.sync
                def _(e):
                    for h in range(16):
                        s.dma(e, s.k_scr[h], s.kq[:, h * T:(h + 1) * T], "qs")
                    e.wait_ge(s.sem["qs"], s.tgt("qs"))

    def l1_attn(self):
        nc, s = self.nc, self
        QSC = 1.0 / np.sqrt(128.0)
        rot_ctx = ExitStack()
        s.rotC = rot_ctx.enter_context(nc.sbuf_tensor("rotC", [32, T], F32))
        s.rotS = rot_ctx.enter_context(nc.sbuf_tensor("rotS", [32, T], F32))
        s.mask4 = rot_ctx.enter_context(nc.sbuf_tensor("mask4", [128, 4 * 512], BF16))
        lo_ctx = ExitStack()
        rotCl = lo_ctx.enter_context(nc.sbuf_tensor("rotCl", [32, T], F32))
        rotSl = lo_ctx.enter_context(nc.sbuf_tensor("rotSl", [32, T], F32))
        with nc.Block("l1ld") as blk:
            @blk.sync
            def _(e):
                s.dma(e, s.rotC[:, :], s.rotC_h[:, :], "qs")
                s.dma(e, s.rotS[:, :], s.rotS_h[:, :], "qs")
                s.dma(e, rotCl[:, :], s.rotCl_h[:, :], "qs")
                s.dma(e, rotSl[:, :], s.rotSl_h[:, :], "qs")
                s.dma(e, s.mask4[:, :], s.mask_h[:, :], "qs")

            @blk.vector
            def _(e):
                e.wait_ge(s.sem["qs"], s.tgt("qs"))
                e.tensor_tensor(s.rotC[:, :], s.rotC[:, :], rotCl[:, :],
                                op=ALU.add)
                e.tensor_tensor(s.rotS[:, :], s.rotS[:, :], rotSl[:, :],
                                op=ALU.add)
        lo_ctx.close()

        self._l1_proj_fm(D, None, True)
        if self.upto.startswith("l1x_k"):
            return
        self._l1_proj_fm(0, 'qscl', False)
        if self.upto == "l1x_q":
            return

        with (
            nc.sbuf_tensor("vw", [128, 2 * 16 * 512], BF16) as vw,
            nc.sbuf_tensor("vtile", [128, 2 * 512], BF16) as vtile,
            nc.psum_tensor("vps", [128, 1024], F32) as vps,
            nc.Block("l1v") as blk,
        ):
            VB = 16 * 512
            pe0 = s.cnt["pe"]
            ve0 = s.cnt["ve"]
            sa0 = s.cnt["sa"]
            sb0 = s.cnt["sb"]
            ld_tgt = {}

            @blk.gpsimd
            def _(e):
                for ic in range(4):
                    if ic >= 2:
                        e.wait_ge(s.sem["pe"], pe0 + (ic - 1) * 16)
                    q = ("da", "db")[ic % 2]
                    src = s.wqkv[:, 4096 + ic * 512: 4096 + (ic + 1) * 512] \
                        .rearrange("(dt p) i -> p dt i", p=128)
                    dst = vw[:, (ic % 2) * VB:(ic % 2 + 1) * VB] \
                        .rearrange("p (dt i) -> p dt i", dt=NDT)
                    s.dma(e, dst, src, q)
                    ld_tgt[ic] = (q, s.cnt[q])
                for dt in range(NDT):
                    s.dma(e, s.x_scr[dt], s.xs[:, dt * T:(dt + 1) * T], "qg")

            @blk.tensor
            def _(e):
                for ic in range(4):
                    e.wait_ge(s.sem[ld_tgt[ic][0]], ld_tgt[ic][1])
                    wb = (ic % 2) * VB
                    for tt in range(16):
                        it = ic * 16 + tt
                        if it >= 2:
                            e.wait_ge(s.sem["ve"], ve0 + it - 1)
                        for dt in range(NDT):
                            mm = e.matmul(
                                vps[:, (it % 2) * 512:(it % 2 + 1) * 512],
                                s.xt(dt, tt * 128, 128),
                                vw[:, wb + dt * 512: wb + (dt + 1) * 512],
                                start=(dt == 0), stop=(dt == NDT - 1),
                                skip_group_check=True)
                        mm.then_inc(s.sem["pe"], 1)
                        s.cnt["pe"] += 1

            @blk.vector
            def _(e):
                for ic in range(4):
                    for tt in range(16):
                        it = ic * 16 + tt
                        e.wait_ge(s.sem["pe"], pe0 + it + 1)
                        if it >= 2:
                            par = it % 2
                            base = sa0 if par == 0 else sb0
                            e.wait_ge(s.sem[("sa", "sb")[par]],
                                      base + 16 * ((it - 2) // 2 + 1))
                        e.tensor_copy(vtile[:, (it % 2) * 512:(it % 2 + 1) * 512],
                                      vps[:, (it % 2) * 512:(it % 2 + 1) * 512]) \
                            .then_inc(s.sem["ve"], 1)
                        s.cnt["ve"] += 1

            @blk.sync
            def _(e):
                for ic in range(4):
                    for tt in range(16):
                        it = ic * 16 + tt
                        e.wait_ge(s.sem["ve"], ve0 + it + 1)
                        s.dma(e, s.v_scr[tt][:, ic * 512:(ic + 1) * 512],
                              vtile[:, (it % 2) * 512:(it % 2 + 1) * 512],
                              ("sa", "sb")[it % 2])
                e.wait_ge(s.sem["sa"], s.tgt("sa"))
                e.wait_ge(s.sem["sb"], s.tgt("sb"))
                e.wait_ge(s.sem["qg"], s.tgt("qg"))

        if self.upto == "l1x_v":
            return
        kbuf = lambda h8, j: s.xs[:, h8 * T + j * 128: h8 * T + (j + 1) * 128]
        vbuf = lambda j, h8: s.xs[:, 16 * 1024 + j * 1024 + h8 * 128:
                                  16 * 1024 + j * 1024 + (h8 + 1) * 128]
        with (
            nc.sbuf_tensor("pbuf", [128, 2 * 16 * 512], BF16) as pbuf,
            nc.sbuf_tensor("rec_c", [128, 2 * 512], F32) as rec_c,
            nc.psum_tensor("sc_ps", [128, 2048], F32) as sc_ps,
            nc.psum_tensor("sum_ps", [128, 512], F32) as sum_ps,
            nc.psum_tensor("av_ps", [128, 512], F32) as av_ps,
            nc.Block("l1att") as blk,
        ):
            PB = 16 * 512
            pe0 = s.cnt["pe"]
            ve0 = s.cnt["ve"]
            se0 = s.cnt["se"]
            qs0 = s.cnt["qs"]
            qc0 = s.cnt["qc"]
            kload = []
            vload = []

            @blk.sync
            def _(e):
                for c in range(4):
                    lim = 4 * (c + 1)
                    for hg in range(2):
                        idx = c * 2 + hg
                        if idx >= 1:
                            e.wait_ge(s.sem["pe"], pe0 + sum(
                                (4 * (cc + 1) + 3) // 4 + 2
                                for cc in range(4) for gg in range(2)
                                for _h in range(8)
                                if cc * 2 + gg < idx))
                        for h8 in range(8):
                            h = hg * 8 + h8
                            s.dma(e, s.xs[:, h8 * T: h8 * T + lim * 128],
                                  s.k_scr[h][:, 0:lim * 128], "qs")
                        kload.append(s.tgt("qs"))

            @blk.scalar
            def _(e):
                for c in range(4):
                    lim = 4 * (c + 1)
                    nwave = (lim + 3) // 4
                    for hg in range(2):
                        idx = c * 2 + hg
                        if idx >= 1:
                            e.wait_ge(s.sem["pe"], pe0 + sum(
                                (4 * (cc + 1) + 3) // 4 + 2
                                for cc in range(4) for gg in range(2)
                                for _h in range(8)
                                if cc * 2 + gg < idx))
                        for j in range(lim):
                            src = s.v_scr[j][:, hg * 1024:(hg + 1) * 1024]
                            dst = s.xs[:, 16 * 1024 + j * 1024:
                                       16 * 1024 + (j + 1) * 1024]
                            s.dma(e, dst, src, "qc")
                        vload.append(s.tgt("qc"))
                        for h8 in range(8):
                            hi = idx * 8 + h8
                            if hi >= 2:
                                pc, pg, ph = self._l1_head_seq(hi - 2)
                                e.wait_ge(s.sem["pe"], pe0 +
                                          self._l1_pe_before(pc, pg, ph) +
                                          (pc + 1) + 2)
                            for w in range(nwave):
                                e.wait_ge(s.sem["pe"],
                                          pe0 + self._l1_pe_before(c, hg, h8) + w + 1)
                                j0 = w * 4
                                j1 = min(lim, j0 + 4)
                                e.activation(
                                    pbuf[:, (h8 % 2) * PB + j0 * 512:
                                         (h8 % 2) * PB + j1 * 512],
                                    sc_ps[:, 0:(j1 - j0) * 512],
                                    AF.Exp).then_inc(s.sem["se"], 1)
                                s.cnt["se"] += 1

            @blk.tensor
            def _(e):
                for c in range(4):
                    lim = 4 * (c + 1)
                    nwave = (lim + 3) // 4
                    for hg in range(2):
                        idx = c * 2 + hg
                        e.wait_ge(s.sem["qs"], kload[idx])
                        e.wait_ge(s.sem["qc"], vload[idx])
                        for h8 in range(8):
                            h = hg * 8 + h8
                            for w in range(nwave):
                                j0 = w * 4
                                j1 = min(lim, j0 + 4)
                                if not (idx == 0 and h8 == 0 and w == 0):
                                    e.wait_ge(s.sem["se"], se0 +
                                              self._l1_se_before(c, hg, h8) + w)
                                for j in range(j0, j1):
                                    mm = e.matmul(
                                        sc_ps[:, (j - j0) * 512:(j - j0 + 1) * 512],
                                        kbuf(h8, j),
                                        s.kq[:, h * T + c * 512: h * T + c * 512 + 512],
                                        start=True, stop=True,
                                        skip_group_check=True)
                                mm.then_inc(s.sem["pe"], 1)
                                s.cnt["pe"] += 1
                            e.wait_ge(s.sem["ve"], ve0 + idx * 16 + h8 * 2 + 1)
                            for j in range(lim):
                                mm = e.matmul(
                                    sum_ps[:, :], s.ones[:, :],
                                    pbuf[:, (h8 % 2) * PB + j * 512:
                                         (h8 % 2) * PB + (j + 1) * 512],
                                    start=(j == 0), stop=(j == lim - 1),
                                    skip_group_check=True)
                            mm.then_inc(s.sem["pe"], 1)
                            s.cnt["pe"] += 1
                            for j in range(lim):
                                mm = e.matmul(
                                    av_ps[:, :], vbuf(j, h8),
                                    pbuf[:, (h8 % 2) * PB + j * 512:
                                         (h8 % 2) * PB + (j + 1) * 512],
                                    start=(j == 0), stop=(j == lim - 1),
                                    skip_group_check=True)
                            mm.then_inc(s.sem["pe"], 1)
                            s.cnt["pe"] += 1

            @blk.vector
            def _(e):
                for c in range(4):
                    lim = 4 * (c + 1)
                    nwave = (lim + 3) // 4
                    for hg in range(2):
                        idx = c * 2 + hg
                        for h8 in range(8):
                            h = hg * 8 + h8
                            e.wait_ge(s.sem["se"], se0 +
                                      self._l1_se_before(c, hg, h8) + nwave)
                            for r in range(4):
                                j = 4 * c + r
                                if j < lim:
                                    pslice = pbuf[:, (h8 % 2) * PB + j * 512:
                                                  (h8 % 2) * PB + (j + 1) * 512]
                                    e.tensor_tensor(
                                        pslice, pslice,
                                        s.mask4[:, r * 512:(r + 1) * 512],
                                        op=ALU.mult)
                            e.engine_nop().then_inc(s.sem["ve"], 1)
                            s.cnt["ve"] += 1
                            e.wait_ge(s.sem["pe"], pe0 +
                                      self._l1_pe_before(c, hg, h8) + nwave + 1)
                            e.reciprocal(rec_c[:, (h8 % 2) * 512:
                                               (h8 % 2 + 1) * 512], sum_ps[:, :])
                            s.vsync(e)
                            e.wait_ge(s.sem["pe"], pe0 +
                                      self._l1_pe_before(c, hg, h8) + nwave + 2)
                            e.tensor_tensor(
                                s.kq[:, h * T + c * 512: h * T + c * 512 + 512],
                                av_ps[:, :],
                                rec_c[:, (h8 % 2) * 512:(h8 % 2 + 1) * 512],
                                op=ALU.mult)
                            e.engine_nop().then_inc(s.sem["ve"], 1)
                            s.cnt["ve"] += 1

        if self.upto == "l1x_att":
            return
        with (
            nc.sbuf_tensor("ow_s", [128, 2 * 16 * 128], BF16) as ow_s,
            nc.psum_tensor("op2_ps", [128, 1024], F32) as op2_ps,
            nc.Block("l1o") as blk,
        ):
            WB = 16 * 128
            pe0 = s.cnt["pe"]
            ve0 = s.cnt["ve"]
            sa0 = s.cnt["sa"]
            sb0 = s.cnt["sb"]
            ld_tgt = {}

            @blk.gpsimd
            def _(e):
                for ot in range(16):
                    if ot >= 2:
                        e.wait_ge(s.sem["pe"], pe0 + (ot - 1) * 4)
                    q = ("da", "db")[ot % 2]
                    src = s.wout[:, ot * 128:(ot + 1) * 128] \
                        .rearrange("(it p) o -> p it o", p=128)
                    dst = ow_s[:, (ot % 2) * WB:(ot % 2 + 1) * WB] \
                        .rearrange("p (it o) -> p it o", it=NDT)
                    s.dma(e, dst, src, q)
                    ld_tgt[ot] = (q, s.cnt[q])

            @blk.sync
            def _(e):
                for ot in range(16):
                    if ot >= 2:
                        e.wait_ge(s.sem["ve"], ve0 + (ot - 1) * 4)
                    s.dma(e, s.xs[:, ot * T:(ot + 1) * T], s.x_scr[ot],
                          ("sa", "sb")[ot % 2])

            @blk.tensor
            def _(e):
                for ot in range(16):
                    e.wait_ge(s.sem[ld_tgt[ot][0]], ld_tgt[ot][1])
                    wb = (ot % 2) * WB
                    for ch in range(4):
                        it2 = ot * 4 + ch
                        pslice = op2_ps[:, (it2 % 2) * 512:(it2 % 2 + 1) * 512]
                        if it2 >= 2:
                            e.wait_ge(s.sem["ve"], ve0 + it2 - 1)
                        for itile in range(16):
                            mm = e.matmul(
                                pslice,
                                ow_s[:, wb + itile * 128: wb + (itile + 1) * 128],
                                s.kqt(itile, ch * 512, 512),
                                start=(itile == 0), stop=(itile == NDT - 1),
                                skip_group_check=True)
                        mm.then_inc(s.sem["pe"], 1)
                        s.cnt["pe"] += 1

            @blk.vector
            def _(e):
                for ot in range(16):
                    par = ot % 2
                    base = sa0 if par == 0 else sb0
                    e.wait_ge(s.sem[("sa", "sb")[par]],
                              base + 16 * (ot // 2 + 1))
                    for ch in range(4):
                        it2 = ot * 4 + ch
                        pslice = op2_ps[:, (it2 % 2) * 512:(it2 % 2 + 1) * 512]
                        e.wait_ge(s.sem["pe"], pe0 + it2 + 1)
                        xt = s.xt(ot, ch * 512, 512)
                        e.tensor_add(xt, pslice, xt).then_inc(s.sem["ve"], 1)
                        s.cnt["ve"] += 1

        for ch in range(4):
            self.layernorm_chunk(ch * 512)
        rot_ctx.close()

    def _l1_pe_before(self, c, hg, h8):
        tot = 0
        for cc in range(4):
            nw = cc + 1
            for gg in range(2):
                for hh in range(8):
                    if (cc, gg, hh) == (c, hg, h8):
                        return tot
                    tot += nw + 2
        return tot

    @staticmethod
    def _l1_head_seq(k):
        seq = [(cc, gg, hh) for cc in range(4) for gg in range(2)
               for hh in range(8)]
        return seq[k]

    def _l1_se_before(self, c, hg, h8):
        tot = 0
        for cc in range(4):
            nw = cc + 1
            for gg in range(2):
                for hh in range(8):
                    if (cc, gg, hh) == (c, hg, h8):
                        return tot
                    tot += nw
        return tot

    def l1_ffn(self):
        nc, s = self.nc, self

        def gsl(oc, lo, ncols):
            if oc < 32:
                return s.kq[:, oc * 1024 + lo: oc * 1024 + lo + ncols]
            return self._gx[:, (oc - 32) * 1024 + lo: (oc - 32) * 1024 + lo + ncols]

        with (
            nc.sbuf_tensor("gx", [128, 12 * 1024], BF16) as gx,
            nc.sbuf_tensor("gw", [128, 2 * 2048], BF16) as gw,
            nc.sbuf_tensor("uw", [128, 2 * 2048], BF16) as uw,
            nc.sbuf_tensor("dw", [128, 2 * NOC2 * 128], BF16) as dw,
            nc.sbuf_tensor("gt", [128, 2 * 1024], BF16) as gt,
            nc.sbuf_tensor("ut", [128, 2 * 1024], BF16) as ut,
            nc.psum_tensor("gps", [128, 1024], F32) as gps,
            nc.psum_tensor("ups", [128, 1024], F32) as ups,
            nc.psum_tensor("dps", [128, 1024], F32) as dps,
            nc.psum_tensor("xps", [128, 1024], F32) as xps,
        ):
            pairs = ((gps, ups), (dps, xps))
            self._gx = gx
            for tc in range(2):
                t0 = tc * 1024
                with nc.Block(s.name("f2gu")) as blk:
                    pe0 = s.cnt["pe"]
                    ve0 = s.cnt["ve"]
                    se0 = s.cnt["se"]
                    s0_ve2 = s.cnt["ve2"]
                    ld_tgt = {}

                    @blk.gpsimd
                    def _(e):
                        for oc in range(NOC2):
                            if oc >= 2:
                                e.wait_ge(s.sem["pe"], pe0 + 2 * (oc - 1))
                            q = ("da", "db")[oc % 2]
                            for (wh, wsb) in ((s.wg, gw), (s.wu, uw)):
                                src = wh[oc].rearrange("(dt p) o -> p dt o", p=128)
                                dst = wsb[:, (oc % 2) * 2048:(oc % 2 + 1) * 2048] \
                                    .rearrange("p (dt o) -> p dt o", dt=NDT)
                                s.dma(e, dst, src, q)
                            ld_tgt[oc] = (q, s.cnt[q])

                    @blk.vector
                    def _(e):
                        for oc in range(NOC2 + 1):
                            if oc < NOC2:
                                e.wait_ge(s.sem[ld_tgt[oc][0]], ld_tgt[oc][1])
                                if oc >= 2:
                                    e.wait_ge(s.sem["pe"], pe0 + 2 * (oc - 1))
                                for wsb in (gw, uw):
                                    sl = wsb[:, (oc % 2) * 2048:(oc % 2 + 1) * 2048]
                                    e.tensor_scalar(sl, sl, 128.0, None,
                                                    op0=ALU.subtract)
                                e.engine_nop().then_inc(s.sem["ve"], 1)
                                s.cnt["ve"] += 1
                            if oc >= 1:
                                po = oc - 1
                                e.wait_ge(s.sem["se"], se0 + 2 * (po + 1))
                                e.tensor_tensor(gsl(po, 0, 1024),
                                                gt[:, (po % 2) * 1024:(po % 2 + 1) * 1024],
                                                ut[:, (po % 2) * 1024:(po % 2 + 1) * 1024],
                                                op=ALU.mult)
                                e.engine_nop().then_inc(s.sem["ve2"], 1)
                                s.cnt["ve2"] += 1

                    @blk.tensor
                    def _(e):
                        for oc in range(NOC2):
                            e.wait_ge(s.sem["ve"], ve0 + oc + 1)
                            if oc >= 2:
                                e.wait_ge(s.sem["se"], se0 + 2 * (oc - 1))
                            gp, up = pairs[oc % 2]
                            wb = (oc % 2) * 2048
                            for dt in range(NDT):
                                for nn in range(2):
                                    e.matmul(
                                        gp[:, nn * 512:(nn + 1) * 512],
                                        gw[:, wb + dt * 128: wb + (dt + 1) * 128],
                                        s.xt(dt, t0 + nn * 512, 512),
                                        start=(dt == 0), stop=(dt == NDT - 1),
                                        skip_group_check=True)
                                    mm = e.matmul(
                                        up[:, nn * 512:(nn + 1) * 512],
                                        uw[:, wb + dt * 128: wb + (dt + 1) * 128],
                                        s.xt(dt, t0 + nn * 512, 512),
                                        start=(dt == 0), stop=(dt == NDT - 1),
                                        skip_group_check=True)
                            mm.then_inc(s.sem["pe"], 2)
                            s.cnt["pe"] += 2

                    @blk.scalar
                    def _(e):
                        for oc in range(NOC2):
                            e.wait_ge(s.sem["pe"], pe0 + 2 * (oc + 1))
                            if oc >= 2:
                                e.wait_ge(s.sem["ve2"], s0_ve2 + oc - 1)
                            gp, up = pairs[oc % 2]
                            e.activation(gt[:, (oc % 2) * 1024:(oc % 2 + 1) * 1024],
                                         gp[:, :], AF.Silu,
                                         scale=s.sg_sb[:, oc:oc + 1]).then_inc(
                                s.sem["se"], 1)
                            s.cnt["se"] += 1
                            e.activation(ut[:, (oc % 2) * 1024:(oc % 2 + 1) * 1024],
                                         up[:, :], AF.Identity,
                                         scale=s.su_sb[:, oc:oc + 1]).then_inc(
                                s.sem["se"], 1)
                            s.cnt["se"] += 1

                with nc.Block(s.name("f2dn")) as blk:
                    pe0 = s.cnt["pe"]
                    ve0 = s.cnt["ve"]
                    ld_tgt = {}

                    @blk.gpsimd
                    def _(e):
                        for ct in range(16):
                            if ct >= 2:
                                e.wait_ge(s.sem["pe"], pe0 + ct - 1)
                            q = ("da", "db")[ct % 2]
                            src = s.wd[ct].rearrange("(ot p) o -> p ot o", p=128)
                            dst = dw[:, (ct % 2) * NOC2 * 128:
                                     (ct % 2 + 1) * NOC2 * 128] \
                                .rearrange("p (ot o) -> p ot o", ot=NOC2)
                            s.dma(e, dst, src, q)
                            ld_tgt[ct] = (q, s.cnt[q])

                    @blk.vector
                    def _(e):
                        for ct in range(17):
                            if ct < 16:
                                e.wait_ge(s.sem[ld_tgt[ct][0]], ld_tgt[ct][1])
                                if ct >= 2:
                                    e.wait_ge(s.sem["pe"], pe0 + ct - 1)
                                sl = dw[:, (ct % 2) * NOC2 * 128:
                                        (ct % 2 + 1) * NOC2 * 128]
                                e.tensor_scalar(sl, sl, 128.0, None,
                                                op0=ALU.subtract)
                                e.engine_nop().then_inc(s.sem["ve"], 1)
                                s.cnt["ve"] += 1
                            if ct >= 1:
                                pc = ct - 1
                                e.wait_ge(s.sem["pe"], pe0 + pc + 1)
                                dp = (dps, xps)[pc % 2]
                                for nn in range(2):
                                    xt = s.xt(pc, t0 + nn * 512, 512)
                                    e.scalar_tensor_tensor(
                                        xt, dp[:, nn * 512:(nn + 1) * 512],
                                        s.sd_sb[:, pc:pc + 1], xt,
                                        op0=ALU.mult, op1=ALU.add)
                                e.engine_nop().then_inc(s.sem["ve2"], 1)
                                s.cnt["ve2"] += 1

                    @blk.tensor
                    def _(e):
                        for ct in range(16):
                            e.wait_ge(s.sem["ve"], ve0 + ct + 1)
                            if ct >= 2:
                                e.wait_ge(s.sem["ve2"],
                                          s.cnt["ve2"] - 16 + ct - 1)
                            dp = (dps, xps)[ct % 2]
                            wb = (ct % 2) * NOC2 * 128
                            for ot in range(NOC2):
                                for nn in range(2):
                                    mm = e.matmul(
                                        dp[:, nn * 512:(nn + 1) * 512],
                                        dw[:, wb + ot * 128: wb + (ot + 1) * 128],
                                        gsl(ot, nn * 512, 512),
                                        start=(ot == 0), stop=(ot == NOC2 - 1),
                                        skip_group_check=True)
                            mm.then_inc(s.sem["pe"], 1)
                            s.cnt["pe"] += 1

            self._gx = None

    def final_out(self):
        nc, s = self.nc, self
        with nc.sbuf_tensor("ostage", [128, 16 * 512], F32) as ostage:
            for ch in range(4):
                self.layernorm_chunk(ch * 512, final=True, out_stage=ostage)
                with nc.Block(s.name("fo")) as blk:
                    @blk.sync
                    def _(e):
                        for dt in range(NDT):
                            s.dma(e, s.outT[dt * 128:(dt + 1) * 128,
                                            ch * 512:(ch + 1) * 512],
                                  ostage[:, dt * 512:(dt + 1) * 512],
                                  ("sa", "sb")[ch % 2])
                        e.wait_ge(s.sem[("sa", "sb")[ch % 2]],
                                  s.tgt(("sa", "sb")[ch % 2]))

    def write_out_from_xs(self, srct=None):
        nc, s = self.nc, self
        if srct is None:
            srct = s.xs
        with (
            nc.sbuf_tensor(s.name("ostage"), [128, 2 * T], F32) as ostage,
            nc.Block(s.name("wout")) as blk,
        ):
            ve0 = s.cnt["ve"]
            sa0 = s.cnt["sa"]
            sb0 = s.cnt["sb"]

            @blk.vector
            def _(e):
                for dt in range(NDT):
                    if dt >= 2:
                        par = dt % 2
                        base = sa0 if par == 0 else sb0
                        e.wait_ge(s.sem[("sa", "sb")[par]],
                                  base + 16 * ((dt - 2) // 2 + 1))
                    e.tensor_copy(ostage[:, (dt % 2) * T:(dt % 2 + 1) * T],
                                  srct[:, dt * T:(dt + 1) * T]).then_inc(
                        s.sem["ve"], 1)
                    s.cnt["ve"] += 1

            @blk.sync
            def _(e):
                for dt in range(NDT):
                    e.wait_ge(s.sem["ve"], ve0 + dt + 1)
                    s.dma(e, s.outT[dt * 128:(dt + 1) * 128, :],
                          ostage[:, (dt % 2) * T:(dt % 2 + 1) * T],
                          ("sa", "sb")[dt % 2])
                e.wait_ge(s.sem["sa"], s.tgt("sa"))
                e.wait_ge(s.sem["sb"], s.tgt("sb"))

    def build(self):
        with self.nc.named_scope("prologue"):
            self.prologue()
        if self.upto.startswith("ffn_"):
            self.l0_ffn()
            self.write_out_from_xs(self.kq)
            self.ctx.close()
            return self.nc
        if self.upto == "l1a_only" or self.upto.startswith("l1x_"):
            if self.kq is None:
                self.kq = self.ctx.enter_context(
                    self.nc.sbuf_tensor("kq", [128, 16 * T], BF16))
            self.l1_attn()
            if self.upto.startswith("l1x_"):
                self.write_out_from_xs(self.kq)
            else:
                self.write_out_from_xs()
            self.ctx.close()
            return self.nc
        with self.nc.named_scope("l0_attn"):
            self.l0_attn()
        if self.upto == "l0a":
            self.write_out_from_xs()
            self.ctx.close()
            return self.nc
        with self.nc.named_scope("l0_ffn"):
            self.l0_ffn()
        if self.upto == "l0f":
            self.write_out_from_xs()
            self.ctx.close()
            return self.nc
        with self.nc.named_scope("l1_attn"):
            self.l1_attn()
        if self.upto == "l1a":
            self.write_out_from_xs()
            self.ctx.close()
            return self.nc
        with self.nc.named_scope("l1_ffn"):
            self.l1_ffn()
        with self.nc.named_scope("final_out"):
            self.final_out()
        self.ctx.close()
        return self.nc


def _host_consts():
    ones = np.ones((128, 128), dtype=ml_dtypes.bfloat16)
    gsel = np.zeros((128, 128), dtype=np.float32)
    gsel[0:64, 0:64] = 1.0
    gsel[64:128, 64:128] = 1.0
    ident = np.eye(128).astype(ml_dtypes.bfloat16)
    h128 = _sylvester(128).astype(ml_dtypes.bfloat16)
    import jax
    import jax.numpy as jnp
    cpu = jax.devices("cpu")[0]
    with jax.default_device(cpu):
        inv_freq = 1.0 / 10000.0 ** (jnp.arange(0, 32, 2,
                                                dtype=jnp.float32) / 32)
        fr_j = jnp.arange(T, dtype=jnp.float32)[:, None] * inv_freq[None, :]
        pos = jnp.concatenate((fr_j, fr_j), axis=-1)
        cosv = np.asarray(jnp.cos(pos)).astype(np.float32)
        sinv = np.asarray(jnp.sin(pos)).astype(np.float32)
    rotC = np.ascontiguousarray(cosv.T)
    sT = sinv.T
    rotS = np.concatenate([-sT[0:16], sT[16:32]], 0).astype(np.float32)
    rotS = np.ascontiguousarray(np.concatenate(
        [-sinv.T[16:32] * 0 - sinv.T[0:16], sinv.T[16:32]], 0)).astype(np.float32)
    rotCl = np.zeros_like(rotC)
    rotSl = np.zeros_like(rotS)
    qs_hi = np.float32(1.0 / np.sqrt(np.float64(128.0)))
    qs_lo = np.float32(1.0 / np.sqrt(np.float64(128.0)) - np.float64(qs_hi))
    misc = np.zeros((128, 4), dtype=np.float32)
    misc[:, 0] = qs_hi
    misc[:, 1] = qs_lo
    m = np.zeros((128, 4, 4, 128), dtype=np.float32)
    for r in range(4):
        for itl in range(4):
            if r < itl:
                m[:, r, itl, :] = 1.0
        m[:, r, r, :] = (np.arange(128)[:, None] <= np.arange(128)[None, :])
    mask = m.reshape(128, 4 * 512).astype(ml_dtypes.bfloat16)
    return dict(ones_h=ones, gsel_h=gsel.astype(ml_dtypes.bfloat16),
                ident_h=ident, h128_h=h128,
                rotC_h=rotC, rotS_h=rotS, rotCl_h=rotCl, rotSl_h=rotSl,
                misc_h=misc, mask_h=mask)


def _prep_core_inputs(b, inp, consts):
    c = lambda a: np.ascontiguousarray(a)
    x = np.asarray(inp["L0_attn_hidden_states"])
    d = dict(
        xT=c(np.asarray(x[b]).T.astype(np.float32)),
        Wq=c(np.asarray(inp["L0_attn_Wq"])), Wk=c(np.asarray(inp["L0_attn_Wk"])),
        Wv=c(np.asarray(inp["L0_attn_Wv"])), Wo=c(np.asarray(inp["L0_attn_Wo"])),
        bo=c(np.asarray(inp["L0_attn_bo"]).reshape(16, 128).T),
        wup=c(np.asarray(inp["L0_ffn_w_up"]).T.reshape(D, NOC1, 128)
              .transpose(1, 0, 2)),
        wdn=c(np.asarray(inp["L0_ffn_w_down"]).T.reshape(HID1, 16, 128)
              .transpose(1, 0, 2)),
        wqkv=c(np.asarray(inp["L1_attn_w_qkv"]).T),
        wout=c(np.asarray(inp["L1_attn_w_out"]).T),
        wg=c(np.asarray(inp["L1_ffn_wq_gate"]).T.reshape(D, NOC2, 128)
             .transpose(1, 0, 2)),
        wu=c(np.asarray(inp["L1_ffn_wq_up"]).T.reshape(D, NOC2, 128)
             .transpose(1, 0, 2)),
        wd=c(np.asarray(inp["L1_ffn_wq_down"]).T.reshape(HID2, 16, 128)
             .transpose(1, 0, 2)),
        sg=c(np.asarray(inp["L1_ffn_s_gate"]).reshape(NOC2, 128).T),
        su=c(np.asarray(inp["L1_ffn_s_up"]).reshape(NOC2, 128).T),
        sd=c(np.asarray(inp["L1_ffn_s_down"]).reshape(16, 128).T),
    )
    d.update(consts)
    return d


def kernel(**inputs):
    nc = B().build()
    consts = _host_consts()
    in_maps = [_prep_core_inputs(b, inputs, consts) for b in range(8)]
    res = run_bass_kernel_spmd(nc, in_maps, core_ids=list(range(8)))
    out = np.stack([np.ascontiguousarray(r["outT"].T) for r in res.results])
    return out.astype(np.float32)
